# version 07
# baseline (speedup 1.0000x reference)
import contextlib
import math
import numpy as np
import ml_dtypes

import concourse.bass as bass
import concourse.mybir as mybir
from concourse.bass_utils import run_bass_kernel_spmd
from concourse.alu_op_type import AluOpType as OP

F32 = mybir.dt.float32
BF16 = mybir.dt.bfloat16
I32 = mybir.dt.int32
AF = mybir.ActivationFunctionType
AX = mybir.AxisListType

D = 1024
S = 2048
NH = 16
DH = 64
LAT = 256
DFF = 4096
NE = 8
IFF = 2816
NT = S // 128
NC = D // 128
NW = S // 512
SCALE = 1.0 / math.sqrt(DH)
MAGIC = 8388608.0
ENGS = ("pe", "act", "dve", "pool", "sp")
_BLOCKN = {"pe": "tensor", "act": "scalar", "dve": "vector", "pool": "gpsimd", "sp": "sync"}
ALL_STAGES = ("l0a", "l0f", "l1a", "l1f")
DBG = False
PHASES = 99
SUBP = 9
MOEPH = 9

GROUPS = [ALL_STAGES]


def blk_bcast(ap2d, nb, blk):
    return bass.AP(ap2d.tensor, ap2d.offset, [list(ap2d.ap[0]), list(ap2d.ap[1]), [0, blk]])


def aug_lhsT(vcol_ap, tail_off_elems):
    base = vcol_ap.offset
    step = tail_off_elems - base
    return bass.AP(vcol_ap.tensor, base, [list(vcol_ap.ap[0]), [step, 2], [1, 64]])


class Ctx:
    def __init__(self):
        self.cmds = {e: [] for e in ENGS}
        self.counts = {}
        self.steps = {}
        self.waited = {e: {} for e in ENGS}
        self.handles = None

    def sem(self, name, step=1):
        self.counts[name] = 0
        self.steps[name] = step

    def ev(self, name):
        return (name, self.counts[name])

    def _filter(self, eng, waits):
        ws = []
        for w in waits:
            if w is None:
                continue
            s, v = w
            if v <= 0:
                continue
            if self.waited[eng].get(s, 0) < v:
                ws.append((s, v))
                self.waited[eng][s] = v
        return ws

    def op(self, eng, build, waits=(), inc=None, fence=True):
        if fence and eng in ("dve", "act", "pool") and inc is not None                 and self.counts.get(inc, 0) > 0:
            waits = list(waits) + [(inc, self.counts[inc])]
        ws = self._filter(eng, waits)
        amt = self.steps[inc] if inc is not None else 0
        if inc is not None:
            self.counts[inc] += amt

        def run(e, ws=tuple(ws), build=build, inc=inc, amt=amt, self=self):
            h = self.handles
            for (s, v) in ws:
                e.wait_ge(h[s], v)
            inst = build(e)
            if inc is not None:
                inst.then_inc(h[inc], amt)
        self.cmds[eng].append(run)
        return (inc, self.counts[inc]) if inc is not None else None

    def wait(self, eng, evs):
        ws = self._filter(eng, evs)
        if not ws:
            return

        def run(e, ws=tuple(ws), self=self):
            h = self.handles
            for (s, v) in ws:
                e.wait_ge(h[s], v)
        self.cmds[eng].append(run)


class MK:
    def __init__(self, b_loc=4, stages=ALL_STAGES):
        self.b_loc = b_loc
        self.stages = stages
        self.nc = bass.Bass("TRN2", target_bir_lowering=False)
        self.C = Ctx()
        self.es = contextlib.ExitStack()
        self.t = {}
        self.dram = {}

    def declare(self):
        nc, d, t = self.nc, self.dram, self.t
        B = self.b_loc
        st = self.stages

        def ext(n, sh, dt=F32):
            d[n] = nc.dram_tensor(n, sh, dt, kind="ExternalInput").ap()

        ext("x", [B, S, D])
        if "l0a" in st:
            ext("Wq0", [D, D]); ext("Wl0", [D, LAT]); ext("Wk0", [LAT, D])
            ext("Wv0", [LAT, D]); ext("Wo0", [D, D])
            ext("bqT0", [128, NC]); ext("blT0", [128, LAT // 128]); ext("bkT0", [128, NC])
            ext("rows0", [1, 2048])
        if "l0f" in st:
            ext("rw0", [D, NE]); ext("w1", [NE, D, DFF]); ext("w2", [NE, DFF, D])
            ext("b1T", [NE, 128, DFF // 128]); ext("b2r", [NE, D])
            ext("gam", [NE, D]); ext("bet", [NE, D])
        if "l1a" in st:
            ext("WqT1", [D, D]); ext("WkT1", [D, D]); ext("WvT1", [D, D]); ext("WoT1", [D, D])
            ext("cs2", [128, 2 * S]); ext("mask", [128, 896])
        if "l1f" in st:
            ext("wg1", [IFF, D]); ext("wu1", [IFF, D]); ext("wd1", [D, IFF])
        ext("ident", [128, 128]); ext("ones", [128, 128]); ext("o64x2", [128, 2])
        ext("R2", [128, 128]); ext("tail", [128, 64]); ext("I2", [2, 2])
        d["y"] = nc.dram_tensor("y", [B, S, D], F32, kind="ExternalOutput").ap()
        if DBG:
            d["dbg"] = nc.dram_tensor("dbg", [16, 128, 2048], F32, kind="ExternalOutput").ap()
            self._dbg_slot = 0
        if "l0f" in st:
            d["hscr"] = nc.dram_tensor("hscr", [2, DFF // 128, 128, S], BF16).ap()
            d["yscr"] = nc.dram_tensor("yscr", [NT, 128, D], F32).ap()
        if "l1f" in st:
            d["wgT"] = nc.dram_tensor("wgT", [D, IFF], BF16).ap()
            d["wuT"] = nc.dram_tensor("wuT", [D, IFF], BF16).ap()
            d["wdT"] = nc.dram_tensor("wdT", [IFF, D], BF16).ap()

        es = self.es

        def sb(n, sh, dt=F32):
            t[n] = es.enter_context(nc.sbuf_tensor(n, sh, dt))

        sb("x_sb", [128, NT * D])
        sb("ar1", [128, 16512])
        sb("ar2", [128, 12288])
        sb("ar3", [128, 5120])
        sb("ident_bf", [128, 128], BF16)
        sb("ones_bf", [128, 128], BF16)
        sb("o64x2_bf", [128, 2], BF16)
        sb("R2_bf", [128, 128], BF16)
        sb("I2_bf", [2, 2], BF16)
        sb("rows_bf", [1, 1024], BF16)
        sb("bqT", [128, NC]); sb("bkT", [128, NC]); sb("blT", [128, LAT // 128])
        sb("stat_m", [128, NT]); sb("stat_r", [128, NT])
        sb("gates", [128, NT * NE])
        sb("gatesT_bf", [8, 128], BF16)
        sb("g8", [128, NE]); sb("g8b", [128, NE], BF16)
        sb("beta_sb", [NE, D], BF16)
        sb("b2sb", [NE, D], BF16)
        sb("rw_bf", [128, NC * NE], BF16)
        sb("b1t", [128, DFF // 128])
        sb("csA", [128, 1])
        sb("recL", [1, 512], BF16)
        sb("Lbc", [64, 512], BF16)
        sb("nrm", [128, 16])
        sb("nrm_bf", [128, 4], BF16)
        sb("tiny", [128, 8])
        sb("epsb", [128, 2])

        def ps(n, sh):
            t[n] = es.enter_context(nc.psum_tensor(n, sh, F32))

        ps("pS", [128, 1024]); ps("pT", [128, 1024])
        ps("pU2", [128, 1024]); ps("pV2", [128, 1024])
        t["pU"] = t["pU2"][:, 0:512]
        t["pV"] = t["pU2"][:, 512:1024]
        t["pW"] = t["pV2"][:, 0:512]
        t["pX"] = t["pV2"][:, 512:1024]

        C = self.C
        C.sem("sd", 16); C.sem("gd", 16)
        C.sem("pe"); C.sem("act"); C.sem("dve"); C.sem("pool")

    def v(self, name, dt=None):
        ap = self.t[name][:, :]
        return ap.bitcast(dt) if dt is not None else ap

    def ar(self, arena, dt, off_kib, cols):
        ap = self.t[arena][:, :].bitcast(dt)
        esz = 2 if dt == BF16 else 4
        off = int(off_kib * 1024) // esz
        return ap[:, off:off + cols]

    def dma(self, queue, dst, src, waits=(), **kw):
        sem = "sd" if queue == "sp" else "gd"
        return self.C.op(queue, lambda e: e.dma_start(dst, src, **kw), waits=waits, inc=sem)

    def mm(self, out, lhsT, rhs, start, stop, waits=(), inc=None, tp=None):
        return self.C.op("pe", lambda e: e.matmul(out, lhsT, rhs, start=start, stop=stop,
                                                  tile_position=tp), waits=waits, inc=inc)

    def trans(self, out, in_, waits=(), inc=None):
        idb = self.t["ident_bf"][:, :]
        return self.C.op("pe", lambda e: e.transpose(out, in_, idb), waits=waits, inc=inc)

    def act(self, out, in_, func, bias=None, scale=1.0, accum=None, waits=()):
        if bias is None:
            return self.C.op("act", lambda e: e.activation(out, in_, func, scale=scale,
                                                           accum_out=accum), waits=waits, inc="act")
        return self.C.op("act", lambda e: e.activation(out, in_, func, bias=bias, scale=scale,
                                                       accum_out=accum), waits=waits, inc="act")

    def dve(self, build, waits=()):
        return self.C.op("dve", build, waits=waits, inc="dve")

    def pool(self, build, waits=()):
        return self.C.op("pool", build, waits=waits, inc="pool")

    def dbg_dump(self, ap_bf, cols=2048):
        if not DBG:
            return
        t = self.t
        scr = self.ar("ar3", F32, 16, 1024)
        sl = self._dbg_slot
        self._dbg_slot += 1
        self.barrier()
        for h in range(0, cols, 1024):
            w = min(1024, cols - h)
            self.dve(lambda e, o=scr[:, 0:w], i=ap_bf[:, h:h + w]: e.tensor_copy(o, i))
            ev = self.C.ev("dve")
            self.dma("sp", self.dram["dbg"][sl][:, h:h + w], scr[:, 0:w], waits=[ev],
                     allow_slow_non_contiguous=True)
            self.barrier()

    def barrier(self):
        C = self.C
        evs = [C.ev(s) for s in ("pe", "act", "dve", "pool", "sd", "gd")]
        for e in ENGS:
            C.wait(e, evs)

    def build(self):
        with self.es:
            self.declare()
            self.load_consts()
            if "l1f" in self.stages:
                self.quant_weights_once()
                self.barrier()
            for b in range(self.b_loc):
                self.load_x(b)
                self.barrier()
                for stg in self.stages:
                    getattr(self, "emit_" + stg)(b)
                    self.barrier()
                self.store_x(b)
            self.C.wait("sp", [self.ev_store])
            C = self.C
            handles = {}
            for name in C.counts:
                handles[name] = self.es.enter_context(self.nc.semaphore(name))
            C.handles = handles
            with self.nc.Block() as block:
                for ename in ENGS:
                    cmds = C.cmds[ename]
                    if cmds:
                        getattr(block, _BLOCKN[ename])(
                            lambda e, cmds=tuple(cmds): [c(e) for c in cmds])
        return self.nc

    def load_consts(self):
        d = self.dram
        st = self.stages
        if "l0a" in st:
            self.dma("sp", self.v("bqT"), d["bqT0"])
            self.dma("sp", self.v("bkT"), d["bkT0"])
            self.dma("sp", self.v("blT"), d["blT0"])
        for dst, src in [(self.v("ident_bf"), d["ident"]), (self.v("ones_bf"), d["ones"]),
                         (self.v("o64x2_bf"), d["o64x2"]), (self.v("R2_bf"), d["R2"]),
                         (self.v("I2_bf"), d["I2"])]:
            self.dma("pool", dst, src)
        self.C.op("dve", lambda e: e.memset(self.t["epsb"][:, 0:1], 1e-5), inc="dve")
        self.C.op("dve", lambda e: e.memset(self.t["epsb"][:, 1:2], 1e-6), inc="dve")
        if "l0f" in st:
            self.dma("pool", self.t["beta_sb"][:, :], d["bet"])
            self.dma("pool", self.t["b2sb"][:, :], d["b2r"])
            self.dma("pool", self.v("rw_bf").rearrange("p (c e) -> p c e", e=NE), d["rw0"].rearrange("(c p) e -> p c e", p=128))
        self.barrier()

    def load_x(self, b):
        src = self.dram["x"][b].rearrange("(t p) d -> p t d", p=128)
        dst = self.t["x_sb"][:, :].rearrange("p (t d) -> p t d", d=D)
        self.dma("sp", dst, src)

    def store_x(self, b):
        self.barrier()
        dst = self.dram["y"][b].rearrange("(t p) d -> p t d", p=128)
        src = self.t["x_sb"][:, :].rearrange("p (t d) -> p t d", d=D)
        self.ev_store = self.dma("sp", dst, src)

    def ln_stats(self):
        C, t = self.C, self.t
        x, m, r = t["x_sb"], t["stat_m"], t["stat_r"]
        dump = self.ar("ar2", BF16, 44, 1024)
        for tt in range(NT):
            xt = x[:, tt * D:(tt + 1) * D]
            self.dve(lambda e, o=m[:, tt:tt + 1], i=xt: e.reduce_sum(o, i, axis=AX.X))
            self.act(dump, xt, AF.Square, accum=r[:, tt:tt + 1])
        evd, eva = C.ev("dve"), C.ev("act")
        self.dve(lambda e: e.tensor_scalar(m[:, :], m[:, :], 1.0 / D, None, OP.mult),
                 waits=[evd])
        self.dve(lambda e: e.tensor_tensor(t["tiny"][:, 0:1] if False else t["nrm"][:, 0:NT],
                                           m[:, :], m[:, :], OP.mult))
        self.dve(lambda e: e.scalar_tensor_tensor(r[:, :], r[:, :], 1.0 / D, t["nrm"][:, 0:NT],
                                                  OP.mult, OP.subtract), waits=[eva])
        ev = C.ev("dve")
        self.act(r[:, :], r[:, :], AF.Ln, bias=self.t["epsb"][:, 0:1], waits=[ev])
        self.act(r[:, :], r[:, :], AF.Exp, scale=-0.5)
        return C.ev("act")

    def xnT_build(self, ev_stat, tt_lo=0, tt_hi=NT, dst_kib=0):
        C, t = self.C, self.t
        x, m, r = t["x_sb"], t["stat_m"], t["stat_r"]
        W = (tt_hi - tt_lo) * 128
        xnT = self.ar("ar2", BF16, dst_kib, NC * W)
        scr = self.ar("ar2", BF16, 40, 4096)
        pE = self.t["pW"][:, :].bitcast(BF16)
        pF = self.t["pX"][:, :].bitcast(BF16)
        cons = {0: None, 1: None}
        evac = {0: None, 1: None}
        for i, tt in enumerate(range(tt_lo, tt_hi)):
            sl = i % 2
            xn_t = scr[:, sl * 2048:sl * 2048 + 1024]
            self.dve(lambda e, o=xn_t, x_=x[:, tt * D:(tt + 1) * D],
                     mm=m[:, tt:tt + 1], rr=r[:, tt:tt + 1]:
                     e.tensor_scalar(o, x_, mm, rr, OP.subtract, OP.mult),
                     waits=[ev_stat, cons[sl]])
            ev_xn = C.ev("dve")
            pb = pE if sl == 0 else pF
            for c in range(NC):
                self.trans(pb[:, c * 128:(c + 1) * 128], xn_t[:, c * 128:(c + 1) * 128],
                           waits=[ev_xn, evac[sl]], inc="pe" if c == NC - 1 else None)
            ev_tr = C.ev("pe")
            cons[sl] = ev_tr
            dst = xnT.rearrange("p (c q) -> p c q", q=W)[:, :, i * 128:i * 128 + 128]
            srcv = pb.rearrange("p (c q) -> p c q", q=128)
            self.act(dst, srcv, AF.Copy, waits=[ev_tr])
            evac[sl] = C.ev("act")
        return C.ev("act")

    def emit_l0a(self, b):
        self.attn(b, 0)

    def emit_l1a(self, b):
        self.attn(b, 1)

    def attn(self, b, layer):
        C, t, d = self.C, self.t, self.dram
        x = t["x_sb"]
        TW = 16 * 65
        ar1b = self.t["ar1"][:, :].bitcast(BF16)
        Vg = ar1b[:, 0:NT * TW]
        aoT = ar1b[:, NT * TW:NT * TW + NC * S]
        aoTc = lambda c: aoT[:, c * S:(c + 1) * S]
        xnT = self.ar("ar2", BF16, 0, NC * S)
        xnTc = lambda c: xnT[:, c * S:(c + 1) * S]
        qt = self.ar("ar2", BF16, 32, S)
        kt_ = self.ar("ar2", BF16, 36, S)
        scr_b = self.ar("ar2", BF16, 40, 4096)
        scr_f = self.ar("ar2", F32, 40, 2048)
        pt_slot = [scr_b[:, i * 512:(i + 1) * 512] for i in range(4)]
        pS, pT = t["pS"], t["pT"]
        pU, pV_, pW, pX = t["pU"], t["pV"], t["pW"], t["pX"]
        bank4 = lambda i, j: [pS[:, j * 512:(j + 1) * 512], pT[:, j * 512:(j + 1) * 512],
                              [pU, pV_][j][:, :], [pW, pX][j][:, :]][i]

        if layer == 0:
            latT = self.ar("ar3", BF16, 0, 2 * S)
            wl = self.ar("ar3", BF16, 8, 2048)
            wk = self.ar("ar3", BF16, 12, 2048)
            wv = self.ar("ar3", BF16, 16, 2048)
            self.dma("pool", wl.rearrange("p (c l) -> p c l", l=LAT),
                     d["Wl0"].rearrange("(c p) l -> p c l", p=128))
            self.dma("pool", wk.rearrange("p (c n) -> p c n", n=D),
                     d["Wk0"].rearrange("(c p) n -> p c n", p=128))
            self.dma("pool", wv.rearrange("p (c n) -> p c n", n=D),
                     d["Wv0"].rearrange("(c p) n -> p c n", p=128))
            self.dma("pool", t["rows_bf"][:, :], d["rows0"][:, 0:1024])
            Wq_src, Wo_src = d["Wq0"], d["Wo0"]
        else:
            cs2 = self.ar("ar3", BF16, 0, 2 * S)
            msk = self.ar("ar3", BF16, 8, 896)
            self.dma("pool", cs2, d["cs2"])
            self.dma("pool", msk, d["mask"])
            Wq_src, Wo_src = d["WqT1"], d["WoT1"]
        if layer == 0:
            wstr = self.ar("ar3", BF16, 8, 2048)
            wjits = [self.ar("ar3", BF16, 8, 1024), self.ar("ar3", BF16, 10, 1024)]
        else:
            wstr = self.ar("ar3", BF16, 10, 2048)
            wjits = [self.ar("ar3", BF16, 14, 2048), self.ar("ar3", BF16, 10, 2048)]
        ev_wq = {}
        ev_wk = {}

        def issue_jit(hp, waits=()):
            sl = hp % 2
            ev_wq[sl] = self.dma("pool", wjits[sl][:, 0:1024].rearrange("p (c m) -> p c m", m=128),
                                 Wq_src[:, hp * 128:(hp + 1) * 128].rearrange("(c p) m -> p c m", p=128),
                                 waits=list(waits))
            if layer == 1:
                ev_wk[sl] = self.dma("pool", wjits[sl][:, 1024:2048].rearrange("p (c m) -> p c m", m=128),
                                     d["WkT1"][:, hp * 128:(hp + 1) * 128].rearrange("(c p) m -> p c m", p=128),
                                     waits=list(waits))

        ev_stat = self.ln_stats()
        if DBG:
            self.dbg_dump(t["stat_m"], 16)
            self.dbg_dump(t["stat_r"], 16)
        self.xnT_build(ev_stat)
        self.barrier()
        if PHASES < 1:
            return

        if layer == 0:
            for lc in range(2):
                for w in range(NW):
                    pb = pU if (lc * NW + w) % 2 == 0 else pV_
                    for c in range(NC):
                        self.mm(pb[:, :], wl[:, c * LAT + lc * 128:c * LAT + lc * 128 + 128],
                                xnTc(c)[:, w * 512:(w + 1) * 512],
                                start=(c == 0), stop=(c == NC - 1),
                                inc="pe" if c == NC - 1 else None)
                    ev = C.ev("pe")
                    self.act(latT[:, lc * S + w * 512: lc * S + w * 512 + 512], pb[:, :],
                             AF.Identity, bias=t["blT"][:, lc:lc + 1], waits=[ev])
            self.barrier()
        if PHASES < 2:
            return

        ones_ap = bass.AP(ar1b.tensor, 64, [list(ar1b.ap[0]), [TW, NT], [65, 16], [1, 1]])
        self.dve(lambda e: e.memset(ones_ap, 1.0))
        vw_cons = {}
        for ttg in range(4):
            nk = 2 if layer == 0 else NC
            evl = {}
            for c in range(nk):
                if layer == 1:
                    sl = c % 2
                    evl[sl] = self.dma("pool", wstr[:, sl * 1024:sl * 1024 + 1024],
                                       d["WvT1"][c * 128:(c + 1) * 128, :],
                                       waits=[vw_cons.get(sl)])
                for i in range(4):
                    tt = ttg * 4 + i
                    for j in range(2):
                        if layer == 0:
                            lhsT = latT[:, c * S + tt * 128:c * S + tt * 128 + 128]
                            rhs = wv[:, c * D + j * 512:c * D + j * 512 + 512]
                            w8 = []
                        else:
                            lhsT = xnTc(c)[:, tt * 128:(tt + 1) * 128]
                            rhs = wstr[:, (c % 2) * 1024 + j * 512:(c % 2) * 1024 + j * 512 + 512]
                            w8 = [evl[c % 2]]
                        last = (c == nk - 1) and layer == 1
                        self.mm(bank4(i, j), lhsT, rhs, start=(c == 0), stop=last,
                                waits=w8, inc="pe" if (j == 1 and i == 3) else None)
                if layer == 1:
                    vw_cons[c % 2] = C.ev("pe")
            if layer == 0:
                for i in range(4):
                    tt = ttg * 4 + i
                    for j in range(2):
                        self.mm(bank4(i, j), t["ones_bf"][0:1, 0:128],
                                t["rows_bf"][0:1, j * 512:(j + 1) * 512],
                                start=False, stop=True,
                                inc="pe" if (i == 3 and j == 1) else None)
            ev = C.ev("pe")
            for i in range(4):
                tt = ttg * 4 + i
                for j in range(2):
                    base = tt * TW + j * 8 * 65
                    dst = bass.AP(ar1b.tensor, ar1b.offset + base,
                                  [list(ar1b.ap[0]), [65, 8], [1, 64]])
                    self.dve(lambda e, o=dst, p=bank4(i, j): e.tensor_copy(o, p), waits=[ev])
            self.barrier()

        if PHASES < 3:
            return
        for hp in range(min(NC, PHASES - 2)):
            sl_j = hp % 2
            if hp == 0:
                issue_jit(0)
            wjit = wjits[sl_j]
            ev_evq = {0: None, 1: None}
            ev_rot = None
            for dst, which in ((qt, "q"), (kt_, "k")):
                for w in range(NW):
                    pb = pT[:, (w % 2) * 512:(w % 2) * 512 + 512]
                    if layer == 0 and which == "k":
                        for lc in range(2):
                            self.mm(pb, wk[:, lc * D + hp * 128:lc * D + hp * 128 + 128],
                                    latT[:, lc * S + w * 512:lc * S + w * 512 + 512],
                                    start=(lc == 0), stop=(lc == 1),
                                    waits=[ev_evq[w % 2]],
                                    inc="pe" if lc == 1 else None)
                    else:
                        wj = wjit[:, 0:1024] if which == "q" else wjit[:, 1024:2048]
                        evw = ev_wq[sl_j] if which == "q" else ev_wk[sl_j]
                        for c in range(NC):
                            self.mm(pb, wj[:, c * 128:(c + 1) * 128],
                                    xnTc(c)[:, w * 512:(w + 1) * 512],
                                    start=(c == 0), stop=(c == NC - 1),
                                    waits=[evw, ev_evq[w % 2]],
                                    inc="pe" if c == NC - 1 else None)
                    ev = C.ev("pe")
                    dsl = dst[:, w * 512:(w + 1) * 512]
                    if layer == 0:
                        bias = t["bqT"][:, hp:hp + 1] if which == "q" else t["bkT"][:, hp:hp + 1]
                        self.act(dsl, pb, AF.Identity, bias=bias, waits=[ev])
                        ev_evq[w % 2] = C.ev("act")
                    else:
                        qp = scr_b[:, 0:512] if which == "q" else scr_b[:, 512:1024]
                        self.act(qp, pb, AF.Copy, waits=[ev])
                        eva = C.ev("act")
                        ev_evq[w % 2] = eva
                        rb = pV_[:, :]
                        self.mm(rb, t["R2_bf"][:, :], qp, start=True, stop=True,
                                waits=[eva, ev_rot], inc="pe")
                        evr = C.ev("pe")
                        fA = scr_f[:, 1024:1536]
                        fB = scr_f[:, 1536:2048]
                        self.dve(lambda e, o=fA, a=qp, c_=cs2[:, w * 512:(w + 1) * 512]:
                                 e.tensor_tensor(o, a, c_, OP.mult), waits=[eva])
                        self.dve(lambda e, o=fB, a=rb, c_=cs2[:, S + w * 512:S + w * 512 + 512]:
                                 e.tensor_tensor(o, a, c_, OP.mult), waits=[evr])
                        ev_rot = C.ev("dve")
                        self.dve(lambda e, o=dsl, a=fA, b_=fB:
                                 e.tensor_tensor(o, a, b_, OP.add))
            ev_qk = C.ev("act") if layer == 0 else C.ev("dve")
            if hp + 1 < NC:
                issue_jit(hp + 1, waits=[C.ev("pe")])
            if DBG and hp == 0:
                self.dbg_dump(qt)
                self.dbg_dump(kt_)
                self.dbg_dump(t["nrm"], 16)
                self.dbg_dump(t["csA"], 1)
                self.dbg_dump(xnTc(0))
                self.dbg_dump(wjit[:, 0:1024], 1024)
                self.dbg_dump(latT[:, 0:2048] if layer == 0 else qt)

            if SUBP < 1:
                continue
            nrm, nrm_bf, csA = t["nrm"], t["nrm_bf"], t["csA"]
            ev_nrd = None
            for src_i, (tens, col0) in enumerate(((qt, 0), (kt_, 4))):
                for w in range(NW):
                    sq = scr_b[:, 2048 + (w % 2) * 512: 2048 + (w % 2) * 512 + 512]
                    self.act(sq, tens[:, w * 512:(w + 1) * 512], AF.Square, waits=[ev_qk])
                    eva = C.ev("act")
                    self.mm(pU[0:2, :], t["o64x2_bf"][:, :], sq, start=True, stop=True,
                            waits=[eva, ev_nrd], inc="pe")
                    evp = C.ev("pe")
                    self.dve(lambda e, o=nrm[0:2, col0 + w:col0 + w + 1], i=pU[0:2, :]:
                             e.reduce_max(o, i, axis=AX.X), waits=[evp])
                    ev_nrd = C.ev("dve")
            self.dve(lambda e: e.reduce_max(nrm[0:2, 8:9], nrm[0:2, 0:4], axis=AX.X))
            self.dve(lambda e: e.reduce_max(nrm[0:2, 9:10], nrm[0:2, 4:8], axis=AX.X))
            self.dve(lambda e: e.tensor_tensor(nrm[0:2, 10:11], nrm[0:2, 8:9],
                                               nrm[0:2, 9:10], OP.mult))
            self.dve(lambda e: e.tensor_copy(nrm_bf[0:2, 0:1], nrm[0:2, 10:11]))
            ev = C.ev("dve")
            self.mm(pU[0:1, 0:2], nrm_bf[0:2, 0:1], t["I2_bf"][:, :], start=True, stop=True,
                    waits=[ev], inc="pe")
            ev = C.ev("pe")
            self.act(nrm[0:1, 11:13], pU[0:1, 0:2], AF.Ln, waits=[ev])
            ev = C.ev("act")
            self.dve(lambda e: e.reduce_max(nrm[0:1, 13:14], nrm[0:1, 11:13], axis=AX.X),
                     waits=[ev])
            ev = C.ev("dve")
            self.act(nrm[0:1, 14:15], nrm[0:1, 13:14], AF.Exp, scale=0.5, waits=[ev])
            ev = C.ev("act")
            self.dve(lambda e: e.tensor_copy(nrm_bf[0:1, 1:2], nrm[0:1, 14:15]), waits=[ev])
            ev = C.ev("dve")
            self.mm(pU[:, 0:1], t["ones_bf"][0:1, 0:128], nrm_bf[0:1, 1:2], start=True,
                    stop=True, waits=[ev], inc="pe")
            ev = C.ev("pe")
            self.act(csA[:, :], pU[:, 0:1], AF.Copy, scale=-SCALE, waits=[ev])
            ev_c = C.ev("act")
            self.barrier()
            if SUBP < 2:
                continue

            for qw in range(NW):
                kt_hi = NT if layer == 0 else 4 * qw + 4
                np_pairs = kt_hi // 2
                sc_bank = {0: t["pT"], 1: t["pU2"]}
                ev_exp = {}
                ev_pv = {}
                for pi in range(np_pairs):
                    for j in range(2):
                        ev_s = None
                        for ti in range(2):
                            ktile = 2 * pi + ti
                            sbank = sc_bank[j][:, ti * 512:(ti + 1) * 512]
                            self.mm(sbank, kt_[64 * j:64 * j + 64, ktile * 128:(ktile + 1) * 128],
                                    qt[64 * j:64 * j + 64, qw * 512:(qw + 1) * 512],
                                    start=True, stop=True,
                                    waits=[ev_exp.get((pi - 1, j))],
                                    inc="pe")
                            ev_s = C.ev("pe")
                            if layer == 1 and ktile >= 4 * qw:
                                jd = ktile - 4 * qw
                                o0 = 384 - 128 * jd
                                self.dve(lambda e, p=sbank, m_=msk[:, o0:o0 + 512]:
                                         e.tensor_tensor(p, p, m_, OP.add), waits=[ev_s])
                                ev_s = C.ev("dve")
                        slot = scr_b[:, ((pi % 2) * 2 + j) * 1024:((pi % 2) * 2 + j) * 1024 + 1024]
                        self.act(slot, sc_bank[j][:, :], AF.Exp, bias=csA[:, 0:1], scale=SCALE,
                                 waits=[ev_s, ev_pv.get((pi - 2, j))])
                        ev_exp[(pi, j)] = C.ev("act")
                        if SUBP < 3:
                            ev_pv[(pi, j)] = None
                            continue
                        h_glob = 2 * hp + j
                        for ti in range(2):
                            ktile = 2 * pi + ti
                            lhs = Vg[:, ktile * TW + 65 * h_glob:ktile * TW + 65 * h_glob + 65]
                            self.mm(pS[0:65, j * 512:(j + 1) * 512], lhs,
                                    slot[:, ti * 512:(ti + 1) * 512],
                                    start=(ktile == 0), stop=(ktile == kt_hi - 1),
                                    waits=[ev_exp[(pi, j)]], inc="pe" if ti == 1 else None)
                        ev_pv[(pi, j)] = C.ev("pe")
                for j in range(2 if SUBP >= 4 else 0):
                    Lrow = pS[64:65, j * 512:(j + 1) * 512]
                    lnl = scr_f[:, 1024:1536]
                    self.act(lnl[0:1, 0:512], Lrow, AF.Ln, waits=[ev_pv[(kt_hi // 2 - 1, j)]])
                    self.act(t["recL"][:, :], lnl[0:1, 0:512], AF.Exp, scale=-1.0)
                    ev = C.ev("act")
                    self.mm(t["pV2"][0:64, 0:512], t["ones_bf"][0:1, 0:64], t["recL"][:, :],
                            start=True, stop=True,
                            waits=[ev, getattr(self, "_ev_lbc", None)], inc="pe")
                    ev = C.ev("pe")
                    self.dve(lambda e: e.tensor_copy(t["Lbc"][:, :], t["pV2"][0:64, 0:512]),
                             waits=[ev])
                    self._ev_lbc = C.ev("dve")
                    self.dve(lambda e, o=aoTc(hp)[64 * j:64 * j + 64, qw * 512:(qw + 1) * 512],
                             p=pS[0:64, j * 512:(j + 1) * 512]:
                             e.tensor_tensor(o, p, t["Lbc"][:, :], OP.mult),
                             waits=[self._ev_lbc])
                self.barrier()
            if DBG and hp == 0:
                self.dbg_dump(aoTc(0))
        if PHASES < 20:
            return
        if layer == 0:
            self.dma("pool", t["rows_bf"][:, :], d["rows0"][:, 1024:2048])
            self.barrier()
        bo_row = t["rows_bf"] if layer == 0 else None
        wo_cons = {}
        for ttg in range(4):
            evl = {}
            for c in range(NC):
                sl = c % 2
                evl[sl] = self.dma("pool", wstr[:, sl * 1024:sl * 1024 + 1024],
                                   Wo_src[c * 128:(c + 1) * 128, :],
                                   waits=[wo_cons.get(sl)])
                for i in range(4):
                    tt = ttg * 4 + i
                    for j in range(2):
                        last = (c == NC - 1) and bo_row is None
                        self.mm(bank4(i, j), aoTc(c)[:, tt * 128:(tt + 1) * 128],
                                wstr[:, sl * 1024 + j * 512:sl * 1024 + j * 512 + 512],
                                start=(c == 0), stop=last, waits=[evl[sl]],
                                inc="pe" if (i == 3 and j == 1) else None)
                wo_cons[sl] = C.ev("pe")
            if bo_row is not None:
                for i in range(4):
                    for j in range(2):
                        self.mm(bank4(i, j), t["ones_bf"][0:1, 0:128],
                                bo_row[0:1, j * 512:(j + 1) * 512], start=False, stop=True,
                                inc="pe" if (i == 3 and j == 1) else None)
            ev = C.ev("pe")
            for i in range(4):
                tt = ttg * 4 + i
                for j in range(2):
                    xsl = x[:, tt * D + j * 512:tt * D + j * 512 + 512]
                    self.dve(lambda e, o=xsl, p=bank4(i, j):
                             e.tensor_tensor(o, o, p, OP.add), waits=[ev])
            self.barrier()

    def emit_l0f(self, b):
        moe(self, b)

    def emit_l1f(self, b):
        ffn1(self, b)

    def quant_weights_once(self):
        quant_weights_once(self)


def moe(K, b):
    C, t, d = K.C, K.t, K.dram
    x = t["x_sb"]
    xnT = K.ar("ar2", BF16, 0, NC * S)
    xnTc = lambda c: xnT[:, c * S:(c + 1) * S]
    gbc = K.ar("ar2", F32, 32, D)
    xj = K.ar("ar2", F32, 36, D)
    ctr = K.ar("ar2", F32, 40, D)
    dump = K.ar("ar2", BF16, 44, D)
    w2sb = K.ar("ar1", BF16, 0, 32 * D)
    w1s = K.ar("ar3", BF16, 0, 2048)
    htile = K.ar("ar3", BF16, 4, 8192)
    stage = K.ar("ar3", BF16, 4, 8192)
    pS, pT = t["pS"], t["pT"]
    pU, pV_, pW, pX = t["pU"], t["pV"], t["pW"], t["pX"]
    m, r = t["stat_m"], t["stat_r"]

    ev_stat = K.ln_stats()
    K.xnT_build(ev_stat)
    K.barrier()

    g8, g8b, gates = t["g8"], t["g8b"], t["gates"]
    nrm = t["nrm"]
    ev_lgrd = None
    ev_gtrd = None
    ev_btrd = None
    for tt in range(NT):
        for c in range(NC):
            K.mm(pU[:, 0:8], xnTc(c)[:, tt * 128:(tt + 1) * 128],
                 t["rw_bf"][:, c * 8:(c + 1) * 8], start=(c == 0), stop=(c == NC - 1),
                 waits=[ev_lgrd],
                 inc="pe" if c == NC - 1 else None)
        ev = C.ev("pe")
        lg = pU[:, 0:8]
        K.dve(lambda e: e.reduce_max(nrm[:, 0:1], lg, axis=AX.X), waits=[ev])
        K.dve(lambda e: e.tensor_scalar(g8[:, :], lg, nrm[:, 0:1], None, OP.is_equal))
        msked = gates[:, tt * 8:(tt + 1) * 8]
        K.dve(lambda e, o=msked: e.scalar_tensor_tensor(o, g8[:, :],
                                                        -1e30, lg, OP.mult, OP.add))
        K.dve(lambda e, i=msked: e.reduce_max(nrm[:, 1:2], i, axis=AX.X))
        K.dve(lambda e, o=msked: e.tensor_scalar(o, o, nrm[:, 1:2], None, OP.is_equal))
        K.dve(lambda e: e.tensor_tensor(nrm[:, 2:3], nrm[:, 1:2], nrm[:, 0:1], OP.subtract))
        ev = C.ev("dve")
        K.act(nrm[:, 3:4], nrm[:, 2:3], AF.Exp, waits=[ev])
        ev = C.ev("act")
        K.dve(lambda e: e.tensor_scalar(nrm[:, 4:5], nrm[:, 3:4], 1.0, None, OP.add),
              waits=[ev])
        K.dve(lambda e: e.reciprocal(nrm[:, 5:6], nrm[:, 4:5]))
        K.dve(lambda e: e.tensor_scalar(nrm[:, 6:7], nrm[:, 5:6], -1.0, 1.0, OP.mult, OP.add))
        K.dve(lambda e: e.tensor_scalar(g8[:, :], g8[:, :], nrm[:, 5:6], None, OP.mult))
        K.dve(lambda e, o=msked: e.scalar_tensor_tensor(o, o, nrm[:, 6:7], g8[:, :],
                                                         OP.mult, OP.add))
        ev_lgrd = C.ev("dve")
        if DBG:
            evx = C.ev("dve")
            K.dma("sp", d["dbg"][tt][:, 0:8], gates[:, tt * 8:(tt + 1) * 8], waits=[evx],
                  allow_slow_non_contiguous=True)
        K.dve(lambda e, i=msked: e.tensor_copy(g8b[:, :], i))
        ev = C.ev("dve")
        K.trans(t["pX"][:, :].bitcast(BF16)[0:8, 0:128], g8b[:, :], waits=[ev, ev_gtrd],
                inc="pe")
        ev = C.ev("pe")
        K.dve(lambda e: e.tensor_copy(t["gatesT_bf"][:, :],
                                      t["pX"][:, :].bitcast(BF16)[0:8, 0:128]), waits=[ev])
        ev_gtrd = C.ev("dve")
        ev = ev_gtrd
        for j in range(2):
            K.mm(pW[:, :], t["gatesT_bf"][:, :], t["beta_sb"][:, j * 512:(j + 1) * 512],
                 start=True, stop=True, waits=[ev, ev_btrd], inc="pe")
            evp = C.ev("pe")
            K.dve(lambda e, o=ctr[:, j * 512:(j + 1) * 512]: e.tensor_copy(o, pW[:, :]),
                  waits=[evp])
            ev_btrd = C.ev("dve")
        evd = C.ev("dve")
        K.dma("sp", d["yscr"][tt], ctr, waits=[evd])
    K.barrier()
    if MOEPH < 1:
        return

    ev_haccum = {}
    for e_i in range(NE):
        hbuf = d["hscr"][e_i % 2]
        ev_w2 = K.dma("pool", w2sb.rearrange("p (c dd) -> p c dd", dd=D),
                      d["w2"][e_i].rearrange("(c p) dd -> p c dd", p=128))
        K.dma("sp", t["b1t"][:, :], d["b1T"][e_i])
        K.dma("pool", t["rows_bf"][:, :], t["b2sb"][e_i:e_i + 1, :])
        K.dma("pool", gbc, d["gam"][e_i:e_i + 1, :].broadcast_to((128, D)))
        ev_ld = [C.ev("gd"), C.ev("sd")]
        ev_slab = {}
        ev_stg = {}
        ev_p1rd = {}
        for fc in range(32):
            sl = fc % 2
            ev_slab[sl] = K.dma("pool", w1s[:, sl * 1024:sl * 1024 + 1024]
                                .rearrange("p (c f) -> p c f", f=128),
                                d["w1"][e_i][:, fc * 128:(fc + 1) * 128]
                                .rearrange("(c p) f -> p c f", p=128),
                                waits=[ev_slab.get((sl, "cons"))])
            for w in range(NW):
                pb = [pU, pV_, pW, pX][w]
                for c in range(NC):
                    K.mm(pb[:, :], w1s[:, sl * 1024 + c * 128:sl * 1024 + c * 128 + 128],
                         xnTc(c)[:, w * 512:(w + 1) * 512], start=(c == 0),
                         stop=(c == NC - 1), waits=[ev_slab[sl], ev_p1rd.get(w)],
                         inc="pe" if c == NC - 1 else None)
                evp = C.ev("pe")
                if w == NW - 1:
                    ev_slab[(sl, "cons")] = evp
                ssl = stage[:, (w % 2) * 4096 + (fc % 4) * 1024:(w % 2) * 4096 + (fc % 4) * 1024 + 512]
                K.act(ssl, pb[:, :], AF.Gelu, bias=t["b1t"][:, fc:fc + 1],
                      waits=[evp, ev_stg.get((w % 2, fc % 4))])
                eva = C.ev("act")
                ev_p1rd[w] = eva
                ev_stg[(w % 2, fc % 4)] = K.dma("sp", hbuf[fc][:, w * 512:(w + 1) * 512],
                                                ssl, waits=[eva])
        K.barrier()
        ev_ht = {}
        ev_grp = {}
        for tt in range(NT):
            sl = tt % 2
            ev_ht[sl] = K.dma("sp", htile[:, sl * 4096:sl * 4096 + 4096]
                              .rearrange("p (c q) -> p c q", q=128),
                              hbuf[:, :, tt * 128:(tt + 1) * 128]
                              .rearrange("c p q -> p c q"),
                              waits=[ev_ht.get((sl, "cons"))])
            grp = pS if sl == 0 else pT
            for j in range(2):
                for fc in range(32):
                    K.mm(grp[:, j * 512:(j + 1) * 512],
                         htile[:, sl * 4096 + fc * 128:sl * 4096 + fc * 128 + 128],
                         w2sb[:, fc * D + j * 512:fc * D + j * 512 + 512],
                         start=(fc == 0), stop=False,
                         waits=[ev_ht[sl], ev_w2, ev_grp.get((sl, j))])
                K.mm(grp[:, j * 512:(j + 1) * 512], t["ones_bf"][0:1, 0:128],
                     t["rows_bf"][0:1, j * 512:(j + 1) * 512], start=False, stop=True,
                     waits=ev_ld, inc="pe")
            ev_ht[(sl, "cons")] = C.ev("pe")
            evp = C.ev("pe")
            K.dve(lambda e, o=xj, xs=x[:, tt * D:(tt + 1) * D], mm=m[:, tt:tt + 1],
                  rr=r[:, tt:tt + 1]: e.tensor_scalar(o, xs, mm, rr, OP.subtract, OP.mult))
            K.dve(lambda e, g=grp[:, :]: e.tensor_tensor(g, g, xj, OP.add), waits=[evp])
            K.dve(lambda e, g=grp[:, :]: e.reduce_sum(nrm[:, 0:1], g, axis=AX.X))
            ev = C.ev("dve")
            K.act(dump, grp[:, :], AF.Square, accum=nrm[:, 1:2], waits=[ev])
            eva = C.ev("act")
            K.dve(lambda e: e.tensor_scalar(nrm[:, 0:1], nrm[:, 0:1], 1.0 / D, None, OP.mult))
            K.dve(lambda e: e.tensor_tensor(nrm[:, 2:3], nrm[:, 0:1], nrm[:, 0:1], OP.mult))
            K.dve(lambda e: e.scalar_tensor_tensor(nrm[:, 1:2], nrm[:, 1:2], 1.0 / D,
                                                   nrm[:, 2:3], OP.mult, OP.subtract),
                  waits=[eva])
            ev = C.ev("dve")
            K.act(nrm[:, 1:2], nrm[:, 1:2], AF.Ln, bias=t["epsb"][:, 1:2], waits=[ev])
            K.act(nrm[:, 1:2], nrm[:, 1:2], AF.Exp, scale=-0.5)
            eva = C.ev("act")
            K.dve(lambda e, g=grp[:, :]: e.tensor_scalar(ctr, g, nrm[:, 0:1], nrm[:, 1:2],
                                                         OP.subtract, OP.mult), waits=[eva])
            ev_grp[(sl, 0)] = C.ev("dve")
            ev_grp[(sl, 1)] = ev_grp[(sl, 0)]
            gate_col = gates[:, tt * 8 + e_i:tt * 8 + e_i + 1]
            K.dve(lambda e, gc=gate_col: e.scalar_tensor_tensor(ctr, ctr, gc, gbc, OP.mult, OP.mult))
            ev = C.ev("dve")
            ev_haccum[tt] = K.dma("pool", d["yscr"][tt], ctr,
                                  waits=[ev, ev_haccum.get(tt)], accum_op=OP.add)
        K.barrier()
    slots = [ctr, xj]
    ev_ld = {}
    ev_add = {}
    for tt in range(NT):
        sl = tt % 2
        ev_ld[sl] = K.dma("sp" if sl == 0 else "pool", slots[sl], d["yscr"][tt],
                          waits=[ev_add.get(sl)])
        K.dve(lambda e, o=x[:, tt * D:(tt + 1) * D], c_=slots[sl]:
              e.tensor_tensor(o, o, c_, OP.add), waits=[ev_ld[sl]])
        ev_add[sl] = C.ev("dve")


def mk_qctx(K, arena, off_kib, W):
    slot = W / 256.0
    c = {}
    for i, n in enumerate(("A", "Bu", "Cu", "Du", "Eu")):
        c[n] = K.ar(arena, F32, off_kib + i * slot, W)
    sa = K.ar(arena, F32, off_kib + 5 * slot, W // 8)
    c["sc"] = sa[:, 0:W // 16]
    c["rec"] = sa[:, W // 16:W // 8]
    c["size_kib"] = 5 * slot + slot / 8.0
    return c


def t3v(ap, nb):
    return bass.AP(ap.tensor, ap.offset, [list(ap.ap[0]), [16, nb], [1, 16]])


def qt_prefix(K, ctx, tile_f32, W, waits=()):
    C = K.C
    Bu, Cu = ctx["Bu"][:, 0:W], ctx["Cu"][:, 0:W]
    nb = W // 16
    sc, rec = ctx["sc"][:, 0:nb], ctx["rec"][:, 0:nb]
    K.dve(lambda e: e.reduce_max(sc, t3v(tile_f32, nb), axis=AX.X,
                                 apply_absolute_value=True), waits=list(waits))
    K.dve(lambda e: e.tensor_scalar(sc, sc, 1.0 / 6.0, 1e-12, OP.mult, OP.max))
    K.dve(lambda e: e.reciprocal(rec, sc))
    K.dve(lambda e: e.tensor_tensor(t3v(Bu, nb), t3v(tile_f32, nb),
                                    blk_bcast(rec, nb, 16), OP.mult))
    K.dve(lambda e: e.tensor_single_scalar(Cu.bitcast(I32), Bu.bitcast(I32), 0x7FFFFFFF,
                                           OP.bitwise_and))
    K.dve(lambda e: e.tensor_single_scalar(Bu.bitcast(I32), Bu.bitcast(I32), -2147483648,
                                           OP.bitwise_and))
    return C.ev("dve")


def qt_stair(K, ctx, W, ev_cu):
    C = K.C
    A, Cu, Du = ctx["A"][:, 0:W], ctx["Cu"][:, 0:W], ctx["Du"][:, 0:W]
    K.pool(lambda e: e.tensor_scalar(Du, Cu, 1.9999990, MAGIC, OP.mult, OP.add),
           waits=[ev_cu])
    K.pool(lambda e: e.tensor_scalar(Du, Du, -MAGIC, 0.5, OP.add, OP.mult))
    K.pool(lambda e: e.tensor_scalar(A, Cu, 0.9999995, MAGIC, OP.mult, OP.add))
    K.pool(lambda e: e.tensor_scalar(A, A, -MAGIC, 1.0, OP.add, OP.mult))
    return C.ev("pool")


def qt_suffix(K, ctx, out_bf, W, ev_pool, out_waits=()):
    C = K.C
    A, Bu, Cu, Du, Eu = [ctx[n][:, 0:W] for n in ("A", "Bu", "Cu", "Du", "Eu")]
    nb = W // 16
    sc = ctx["sc"][:, 0:nb]
    K.dve(lambda e: e.tensor_single_scalar(Eu, Cu, 2.25, OP.is_gt))
    K.dve(lambda e: e.select(Du, Eu.bitcast(I32), A, Du), waits=[ev_pool])
    K.dve(lambda e: e.tensor_single_scalar(Eu, Cu, 5.0, OP.is_gt))
    K.dve(lambda e: e.tensor_scalar(A, Eu, 2.0, 4.0, OP.mult, OP.add))
    K.dve(lambda e: e.tensor_single_scalar(Eu, Cu, 4.5, OP.is_gt))
    K.dve(lambda e: e.select(Du, Eu.bitcast(I32), A, Du))
    iE = Eu.bitcast(I32)
    K.dve(lambda e: e.tensor_tensor(t3v(iE, nb), blk_bcast(sc.bitcast(I32), nb, 16),
                                    t3v(Bu.bitcast(I32), nb), OP.bitwise_or))
    K.dve(lambda e: e.tensor_tensor(out_bf, Du, Eu, OP.mult), waits=list(out_waits))
    return C.ev("dve")


def quant_weights_once(K):
    C, t, d = K.C, K.t, K.dram
    bigs = [K.ar("ar2", F32, 11 * i, IFF) for i in range(2)]
    qc = [mk_qctx(K, "ar1", 0, 512), mk_qctx(K, "ar1", 10.5, 512)]
    qb = K.ar("ar1", BF16, 24, IFF)
    stg = K.ar("ar1", BF16, 32, 2816)
    pE = t["pW"][:, :].bitcast(BF16)
    pF = t["pX"][:, :].bitcast(BF16)
    units = []
    for (srcn, dstn, nrc, wid, sub) in (("wg1", "wgT", 22, D, 512), ("wu1", "wuT", 22, D, 512),
                                        ("wd1", "wdT", 8, IFF, 256)):
        for rc in range(nrc):
            units.append((srcn, dstn, rc, wid, sub))
    ev_big = {}
    ev_pfx = {}
    ev_st = {}
    ev_evac = {0: None, 1: None}
    ev_trans_prev = None
    ev_stg_dma = None
    for g, (srcn, dstn, rc, wid, sub) in enumerate(units):
        if g == 0:
            ev_big[0] = K.dma("sp", bigs[0][:, 0:wid], d[srcn][rc * 128:(rc + 1) * 128, :])
        if g + 1 < len(units):
            ns, nd, nrc_, nwid, _ = units[g + 1]
            ev_big[(g + 1) % 2] = K.dma("sp", bigs[(g + 1) % 2][:, 0:nwid],
                                        d[ns][nrc_ * 128:(nrc_ + 1) * 128, :],
                                        waits=[ev_pfx.get((g + 1) % 2)])
        big = bigs[g % 2]
        nst = wid // sub
        ev_cu = {}
        ev_sfx = None
        for sti in range(nst):
            cx = sti % 2
            ev_cu[sti] = qt_prefix(K, qc[cx], big[:, sti * sub:(sti + 1) * sub], sub,
                                   waits=[ev_big[g % 2], ev_st.get(cx)])
            ev_st[cx] = qt_stair(K, qc[cx], sub, ev_cu[sti])
            if sti > 0:
                ev_sfx = qt_suffix(K, qc[(sti - 1) % 2], qb[:, (sti - 1) * sub:sti * sub],
                                   sub, ev_st[(sti - 1) % 2], out_waits=[ev_trans_prev])
        ev_pfx[g % 2] = C.ev("dve")
        ev_sfx = qt_suffix(K, qc[(nst - 1) % 2], qb[:, (nst - 1) * sub:nst * sub],
                           sub, ev_st[(nst - 1) % 2], out_waits=[ev_trans_prev])
        nch = wid // 128
        for c in range(nch):
            pb = pE if c % 2 == 0 else pF
            K.trans(pb[:, 0:128], qb[:, c * 128:(c + 1) * 128],
                    waits=[ev_sfx, ev_evac[c % 2]], inc="pe")
            ev = C.ev("pe")
            K.act(stg[:, c * 128:(c + 1) * 128], pb[:, 0:128], AF.Copy,
                  waits=[ev, ev_stg_dma])
            ev_evac[c % 2] = C.ev("act")
        ev_trans_prev = C.ev("pe")
        ev = C.ev("act")
        ev_stg_dma = K.dma("sp", d[dstn].rearrange("(c p) n -> p c n", p=128)[:, :, rc * 128:(rc + 1) * 128],
                           stg[:, 0:nch * 128].rearrange("p (c n) -> p c n", n=128), waits=[ev])


def ffn1(K, b):
    C, t, d = K.C, K.t, K.dram
    x = t["x_sb"]
    hqT = K.ar("ar1", BF16, 0, 22 * 1024)
    hqTc = lambda ic: hqT[:, ic * 1024:(ic + 1) * 1024]
    WG = [K.ar("ar1", BF16, 44, 4096), K.ar("ar3", BF16, 0, 4096)]
    WU = [K.ar("ar1", BF16, 52, 4096), K.ar("ar3", BF16, 8, 4096)]
    xqT = K.ar("ar2", BF16, 0, NC * 1024)
    xqTc = lambda c: xqT[:, c * 1024:(c + 1) * 1024]
    qc = [mk_qctx(K, "ar2", 16, 512), mk_qctx(K, "ar2", 27.5, 512)]
    qtl = [K.ar("ar2", BF16, 26.25, 512), K.ar("ar2", BF16, 37.75, 512)]
    silo = [K.ar("ar2", F32, 39, 512), K.ar("ar2", F32, 41, 512)]
    wds = K.ar("ar2", BF16, 43, 2048)
    pS, pT = t["pS"], t["pT"]
    pU, pV_, pW, pX = t["pU"], t["pV"], t["pW"], t["pX"]
    bank4 = lambda i, j: [pS[:, j * 512:(j + 1) * 512], pT[:, j * 512:(j + 1) * 512],
                          [pU, pV_][j][:, :], [pW, pX][j][:, :]][i]
    m, r = t["stat_m"], t["stat_r"]
    pE = pW[:, :].bitcast(BF16)
    pF = pX[:, :].bitcast(BF16)

    ev_stat = K.ln_stats()
    NSL = [512, 512, 512, 512, 512, 256]
    csum = [0]
    for w_ in NSL:
        csum.append(csum[-1] + w_)
    ev_wg = {}
    ev_wu = {}
    ev_slab_cons = {}

    def issue_slab(ns):
        par = ns % 2
        nn = NSL[ns]
        nc0 = csum[ns]
        ev_wg[par] = K.dma("sp", WG[par][:, 0:8 * nn].rearrange("p (c n) -> p c n", n=nn),
                           d["wgT"][:, nc0:nc0 + nn].rearrange("(c p) n -> p c n", p=128),
                           waits=[ev_slab_cons.get(par)])
        ev_wu[par] = K.dma("pool", WU[par][:, 0:8 * nn].rearrange("p (c n) -> p c n", n=nn),
                           d["wuT"][:, nc0:nc0 + nn].rearrange("(c p) n -> p c n", p=128),
                           waits=[ev_slab_cons.get(par)])

    for half in range(2):
        tt0 = half * 8
        issue_slab(0)
        issue_slab(1)
        ev_tre = {0: None, 1: None}
        ev_st = {}
        ev_sfx = {}
        ev_tr = {}

        def xq_trans(n):
            cx = n % 2
            i, sub = n // 2, n % 2
            for cc in range(4):
                c = sub * 4 + cc
                pb = pE if cc % 2 == 0 else pF
                K.trans(pb[:, 0:128], qtl[cx][:, cc * 128:(cc + 1) * 128],
                        waits=[ev_sfx[n], ev_tre[cc % 2]], inc="pe")
                ev = C.ev("pe")
                K.act(xqTc(c)[:, i * 128:(i + 1) * 128], pb[:, 0:128], AF.Copy,
                      waits=[ev])
                ev_tre[cc % 2] = C.ev("act")
            ev_tr[cx] = C.ev("pe")

        for n in range(16):
            cx = n % 2
            tt, sub = tt0 + n // 2, n % 2
            IN = qc[cx]["A"][:, 0:512]
            K.dve(lambda e, o=IN, xs=x[:, tt * D + sub * 512:tt * D + sub * 512 + 512],
                  mm=m[:, tt:tt + 1], rr=r[:, tt:tt + 1]:
                  e.tensor_scalar(o, xs, mm, rr, OP.subtract, OP.mult),
                  waits=[ev_stat, ev_st.get(cx)])
            ev_cu = qt_prefix(K, qc[cx], IN, 512)
            ev_st[cx] = qt_stair(K, qc[cx], 512, ev_cu)
            if n > 0:
                ev_sfx[n - 1] = qt_suffix(K, qc[(n - 1) % 2], qtl[(n - 1) % 2][:, 0:512],
                                          512, ev_st[(n - 1) % 2],
                                          out_waits=[ev_tr.get((n - 1) % 2)])
                xq_trans(n - 1)
        ev_sfx[15] = qt_suffix(K, qc[1], qtl[1][:, 0:512], 512, ev_st[1],
                               out_waits=[ev_tr.get(1)])
        xq_trans(15)
        K.barrier()
        nsw_of = lambda k: NSL[k // 8]
        ic0_of = lambda k: csum[k // 8] // 128
        ev_silu = {}
        ev_mult = {}
        ev_st = {}
        ev_sfx = {}
        ev_tr = {}
        ev_tre = {0: None, 1: None}

        def h_trans(k):
            cx = k % 2
            nsw = nsw_of(k)
            i = k % 8
            for cc in range(nsw // 128):
                ic = ic0_of(k) + cc
                pb = pE if cc % 2 == 0 else pF
                K.trans(pb[:, 0:128], qtl[cx][:, cc * 128:(cc + 1) * 128],
                        waits=[ev_sfx[k], ev_tre[cc % 2]], inc="pe")
                ev = C.ev("pe")
                K.act(hqTc(ic)[:, i * 128:(i + 1) * 128], pb[:, 0:128], AF.Copy,
                      waits=[ev])
                ev_tre[cc % 2] = C.ev("act")
            ev_tr[cx] = C.ev("pe")

        def h_suffix(k):
            cx = k % 2
            nsw = nsw_of(k)
            ev_sfx[k] = qt_suffix(K, qc[cx], qtl[cx][:, 0:nsw], nsw, ev_st[cx],
                                  out_waits=[ev_tr.get(cx)])

        for ns, nsw in enumerate(NSL):
            par = ns % 2
            for i in range(8):
                k = ns * 8 + i
                cx = k % 2
                gbank = pU[:, 0:nsw] if cx == 0 else pS[:, 0:nsw]
                ubank = pV_[:, 0:nsw] if cx == 0 else pS[:, 512:512 + nsw]
                for c in range(NC):
                    K.mm(gbank, xqTc(c)[:, i * 128:(i + 1) * 128],
                         WG[par][:, c * nsw:(c + 1) * nsw], start=(c == 0),
                         stop=(c == NC - 1),
                         waits=[ev_wg[par], ev_silu.get(k - 2)],
                         inc="pe" if c == NC - 1 else None)
                for c in range(NC):
                    K.mm(ubank, xqTc(c)[:, i * 128:(i + 1) * 128],
                         WU[par][:, c * nsw:(c + 1) * nsw], start=(c == 0),
                         stop=(c == NC - 1),
                         waits=[ev_wu[par], ev_mult.get(k - 2)],
                         inc="pe" if c == NC - 1 else None)
                evp = C.ev("pe")
                if i == 7:
                    ev_slab_cons[par] = evp
                K.act(silo[cx][:, 0:nsw], gbank, AF.Silu,
                      waits=[evp, ev_mult.get(k - 2)])
                ev_silu[k] = C.ev("act")
                IN = qc[cx]["A"][:, 0:nsw]
                K.dve(lambda e, o=IN, g=silo[cx][:, 0:nsw], u=ubank:
                      e.tensor_tensor(o, g, u, OP.mult),
                      waits=[ev_silu[k], ev_st.get(cx)])
                ev_mult[k] = C.ev("dve")
                ev_cu = qt_prefix(K, qc[cx], IN, nsw)
                ev_st[cx] = qt_stair(K, qc[cx], nsw, ev_cu)
                if k >= 2:
                    h_trans(k - 2)
                if k >= 1:
                    h_suffix(k - 1)
            if ns + 2 < len(NSL):
                issue_slab(ns + 2)
        h_trans(46)
        h_suffix(47)
        h_trans(47)
        K.barrier()
        for ttg in range(2):
            evl = {}
            for ic in range(22):
                sl = ic % 2
                evl[sl] = K.dma("sp" if sl == 0 else "pool",
                                wds[:, sl * 1024:sl * 1024 + 1024],
                                d["wdT"][ic * 128:(ic + 1) * 128, :],
                                waits=[evl.get((sl, "cons"))])
                for i in range(4):
                    il = ttg * 4 + i
                    for j in range(2):
                        K.mm(bank4(i, j), hqTc(ic)[:, il * 128:(il + 1) * 128],
                             wds[:, sl * 1024 + j * 512:sl * 1024 + j * 512 + 512],
                             start=(ic == 0), stop=(ic == 21), waits=[evl[sl]],
                             inc="pe" if (i == 3 and j == 1) else None)
                evl[(sl, "cons")] = C.ev("pe")
            ev = C.ev("pe")
            for i in range(4):
                tt = tt0 + ttg * 4 + i
                for j in range(2):
                    xsl = x[:, tt * D + j * 512:tt * D + j * 512 + 512]
                    K.dve(lambda e, o=xsl, p=bank4(i, j): e.tensor_tensor(o, o, p, OP.add),
                          waits=[ev])
            K.barrier()


MK.emit_l0f = lambda self, b: moe(self, b)
MK.emit_l1f = lambda self, b: ffn1(self, b)
MK.quant_weights_once = lambda self: quant_weights_once(self)


def _host_consts():
    c = {}
    c["ident"] = np.eye(128, dtype=np.float32)
    c["ones"] = np.ones((128, 128), dtype=np.float32)
    o = np.zeros((128, 2), dtype=np.float32)
    o[:64, 0] = 1.0
    o[64:, 1] = 1.0
    c["o64x2"] = o
    R = np.zeros((128, 128), dtype=np.float32)
    for i in range(64):
        R[2 * i + 1, 2 * i] = -1.0
        R[2 * i, 2 * i + 1] = 1.0
    c["R2"] = R
    tl = np.zeros((128, 64), dtype=np.float32)
    tl[:, 0] = 1.0
    c["tail"] = tl
    c["I2"] = np.eye(2, dtype=np.float32)
    inv = 1.0 / (10000.0 ** (np.arange(0, DH, 2, dtype=np.float64) / DH))
    pos = np.arange(S, dtype=np.float64)
    fr = pos[:, None] * inv[None, :]
    emb = np.repeat(fr, 2, axis=1)
    cs = np.zeros((128, 2 * S), dtype=np.float32)
    cs[:64, 0:S] = np.cos(emb).T.astype(np.float32)
    cs[64:, 0:S] = cs[:64, 0:S]
    cs[:64, S:] = np.sin(emb).T.astype(np.float32)
    cs[64:, S:] = cs[:64, S:]
    c["cs2"] = cs
    kk = np.arange(128)[:, None]
    ccol = np.arange(896)[None, :]
    c["mask"] = np.where(kk > ccol - 384, np.float32(-1e30), np.float32(0.0)).astype(np.float32)
    return c


def _prep_shared(inputs, stages):
    f = lambda a: np.ascontiguousarray(np.asarray(a), dtype=np.float32)
    s = {}
    if "l0a" in stages:
        s["Wq0"] = f(inputs["L0_attn_Wq"])
        s["Wl0"] = f(inputs["L0_attn_Wl"])
        s["Wk0"] = f(inputs["L0_attn_Wk"])
        s["Wv0"] = f(inputs["L0_attn_Wv"])
        s["Wo0"] = f(inputs["L0_attn_Wo"])
        s["bqT0"] = f(inputs["L0_attn_bq"]).reshape(NC, 128).T.copy()
        s["bkT0"] = f(inputs["L0_attn_bk"]).reshape(NC, 128).T.copy()
        s["blT0"] = f(inputs["L0_attn_bl"]).reshape(2, 128).T.copy()
        s["rows0"] = np.concatenate([f(inputs["L0_attn_bv"]), f(inputs["L0_attn_bo"])])[None, :].copy()
    if "l0f" in stages:
        s["rw0"] = f(inputs["L0_ffn_router_w"])
        s["w1"] = f(inputs["L0_ffn_w1"])
        s["w2"] = f(inputs["L0_ffn_w2"])
        s["b1T"] = f(inputs["L0_ffn_b1"]).reshape(NE, 32, 128).transpose(0, 2, 1).copy()
        s["b2r"] = f(inputs["L0_ffn_b2"])
        s["gam"] = f(inputs["L0_ffn_gamma"])
        s["bet"] = f(inputs["L0_ffn_beta"])
    if "l1a" in stages:
        s["WqT1"] = f(inputs["L1_attn_Wq"]).T.copy()
        s["WkT1"] = f(inputs["L1_attn_Wk"]).T.copy()
        s["WvT1"] = f(inputs["L1_attn_Wv"]).T.copy()
        s["WoT1"] = f(inputs["L1_attn_Wo"]).T.copy()
    if "l1f" in stages:
        s["wg1"] = f(inputs["L1_ffn_w_gate"])
        s["wu1"] = f(inputs["L1_ffn_w_up"])
        s["wd1"] = f(inputs["L1_ffn_w_down"])
    c = _host_consts()
    s["ident"] = c["ident"]; s["ones"] = c["ones"]; s["o64x2"] = c["o64x2"]
    s["R2"] = c["R2"]; s["tail"] = c["tail"]; s["I2"] = c["I2"]
    if "l1a" in stages:
        s["cs2"] = c["cs2"]; s["mask"] = c["mask"]
    return s


_BUILT = {}


def _get_nc(b_loc, stages):
    key = (b_loc, tuple(stages))
    if key not in _BUILT:
        _BUILT[key] = MK(b_loc, tuple(stages)).build()
    return _BUILT[key]


def kernel(**inputs):
    n_cores = 8
    B = int(np.asarray(inputs["L0_attn_x"]).shape[0])
    bl = B // n_cores
    x = np.ascontiguousarray(np.asarray(inputs["L0_attn_x"], dtype=np.float32))
    for stages in GROUPS:
        nc = _get_nc(bl, stages)
        shared = _prep_shared(inputs, stages)
        in_maps = []
        for c in range(n_cores):
            mdl = dict(shared)
            mdl["x"] = np.ascontiguousarray(x[c * bl:(c + 1) * bl])
            in_maps.append(mdl)
        res = run_bass_kernel_spmd(nc, in_maps, core_ids=list(range(n_cores)))
        x = np.concatenate([r["y"] for r in res.results], axis=0)
    return x


# version 08
# speedup vs baseline: 1.0917x; 1.0917x over previous
import contextlib
import math
import numpy as np
import ml_dtypes

import concourse.bass as bass
import concourse.mybir as mybir
from concourse.bass_utils import run_bass_kernel_spmd
from concourse.alu_op_type import AluOpType as OP

F32 = mybir.dt.float32
BF16 = mybir.dt.bfloat16
I32 = mybir.dt.int32
AF = mybir.ActivationFunctionType
AX = mybir.AxisListType

D = 1024
S = 2048
NH = 16
DH = 64
LAT = 256
DFF = 4096
NE = 8
IFF = 2816
NT = S // 128
NC = D // 128
NW = S // 512
SCALE = 1.0 / math.sqrt(DH)
MAGIC = 8388608.0
ENGS = ("pe", "act", "dve", "pool", "sp")
_BLOCKN = {"pe": "tensor", "act": "scalar", "dve": "vector", "pool": "gpsimd", "sp": "sync"}
ALL_STAGES = ("l0a", "l0f", "l1a", "l1f")
DBG = False
PHASES = 99
SUBP = 9
MOEPH = 9

GROUPS = [ALL_STAGES]


def blk_bcast(ap2d, nb, blk):
    return bass.AP(ap2d.tensor, ap2d.offset, [list(ap2d.ap[0]), list(ap2d.ap[1]), [0, blk]])


def aug_lhsT(vcol_ap, tail_off_elems):
    base = vcol_ap.offset
    step = tail_off_elems - base
    return bass.AP(vcol_ap.tensor, base, [list(vcol_ap.ap[0]), [step, 2], [1, 64]])


class Ctx:
    def __init__(self):
        self.cmds = {e: [] for e in ENGS}
        self.counts = {}
        self.steps = {}
        self.waited = {e: {} for e in ENGS}
        self.handles = None

    def sem(self, name, step=1):
        self.counts[name] = 0
        self.steps[name] = step

    def ev(self, name):
        return (name, self.counts[name])

    def _filter(self, eng, waits):
        ws = []
        for w in waits:
            if w is None:
                continue
            s, v = w
            if v <= 0:
                continue
            if self.waited[eng].get(s, 0) < v:
                ws.append((s, v))
                self.waited[eng][s] = v
        return ws

    def op(self, eng, build, waits=(), inc=None, fence=True):
        if fence and eng in ("dve", "act", "pool") and inc is not None                 and self.counts.get(inc, 0) > 0:
            waits = list(waits) + [(inc, self.counts[inc])]
        ws = self._filter(eng, waits)
        amt = self.steps[inc] if inc is not None else 0
        if inc is not None:
            self.counts[inc] += amt

        def run(e, ws=tuple(ws), build=build, inc=inc, amt=amt, self=self):
            h = self.handles
            for (s, v) in ws:
                e.wait_ge(h[s], v)
            inst = build(e)
            if inc is not None:
                inst.then_inc(h[inc], amt)
        self.cmds[eng].append(run)
        return (inc, self.counts[inc]) if inc is not None else None

    def wait(self, eng, evs):
        ws = self._filter(eng, evs)
        if not ws:
            return

        def run(e, ws=tuple(ws), self=self):
            h = self.handles
            for (s, v) in ws:
                e.wait_ge(h[s], v)
        self.cmds[eng].append(run)


class MK:
    def __init__(self, b_loc=4, stages=ALL_STAGES):
        self.b_loc = b_loc
        self.stages = stages
        self.nc = bass.Bass("TRN2", target_bir_lowering=False)
        self.C = Ctx()
        self.es = contextlib.ExitStack()
        self.t = {}
        self.dram = {}

    def declare(self):
        nc, d, t = self.nc, self.dram, self.t
        B = self.b_loc
        st = self.stages

        def ext(n, sh, dt=F32):
            d[n] = nc.dram_tensor(n, sh, dt, kind="ExternalInput").ap()

        ext("x", [B, S, D])
        if "l0a" in st:
            ext("Wq0", [D, D]); ext("Wl0", [D, LAT]); ext("Wk0", [LAT, D])
            ext("Wv0", [LAT, D]); ext("Wo0", [D, D])
            ext("bqT0", [128, NC]); ext("blT0", [128, LAT // 128]); ext("bkT0", [128, NC])
            ext("rows0", [1, 2048])
        if "l0f" in st:
            ext("rw0", [D, NE]); ext("w1", [NE, D, DFF]); ext("w2", [NE, DFF, D])
            ext("b1T", [NE, 128, DFF // 128]); ext("b2r", [NE, D])
            ext("gam", [NE, D]); ext("bet", [NE, D])
        if "l1a" in st:
            ext("WqT1", [D, D]); ext("WkT1", [D, D]); ext("WvT1", [D, D]); ext("WoT1", [D, D])
            ext("cs2", [128, 2 * S]); ext("mask", [128, 896])
        if "l1f" in st:
            ext("wg1", [IFF, D]); ext("wu1", [IFF, D]); ext("wd1", [D, IFF])
        ext("ident", [128, 128]); ext("ones", [128, 128]); ext("o64x2", [128, 2])
        ext("R2", [128, 128]); ext("tail", [128, 64]); ext("I2", [2, 2])
        d["y"] = nc.dram_tensor("y", [B, S, D], F32, kind="ExternalOutput").ap()
        if DBG:
            d["dbg"] = nc.dram_tensor("dbg", [16, 128, 2048], F32, kind="ExternalOutput").ap()
            self._dbg_slot = 0
        if "l0f" in st:
            d["hscr"] = nc.dram_tensor("hscr", [2, DFF // 128, 128, S], BF16).ap()
            d["yscr"] = nc.dram_tensor("yscr", [NT, 128, D], F32).ap()
        if "l1f" in st:
            d["wgT"] = nc.dram_tensor("wgT", [D, IFF], BF16).ap()
            d["wuT"] = nc.dram_tensor("wuT", [D, IFF], BF16).ap()
            d["wdT"] = nc.dram_tensor("wdT", [IFF, D], BF16).ap()

        es = self.es

        def sb(n, sh, dt=F32):
            t[n] = es.enter_context(nc.sbuf_tensor(n, sh, dt))

        sb("x_sb", [128, NT * D])
        sb("ar1", [128, 16512])
        sb("ar2", [128, 12288])
        sb("ar3", [128, 5120])
        sb("ident_bf", [128, 128], BF16)
        sb("ones_bf", [128, 128], BF16)
        sb("o64x2_bf", [128, 2], BF16)
        sb("R2_bf", [128, 128], BF16)
        sb("I2_bf", [2, 2], BF16)
        sb("rows_bf", [1, 1024], BF16)
        sb("bqT", [128, NC]); sb("bkT", [128, NC]); sb("blT", [128, LAT // 128])
        sb("stat_m", [128, NT]); sb("stat_r", [128, NT])
        sb("gates", [128, NT * NE])
        sb("gatesT_bf", [8, 128], BF16)
        sb("g8", [128, NE]); sb("g8b", [128, NE], BF16)
        sb("beta_sb", [NE, D], BF16)
        sb("b2sb", [NE, D], BF16)
        sb("rw_bf", [128, NC * NE], BF16)
        sb("b1t", [128, DFF // 128])
        sb("csA", [128, 1])
        sb("recL", [1, 512], BF16)
        sb("Lbc", [64, 512], BF16)
        sb("nrm", [128, 16])
        sb("nrm_bf", [128, 4], BF16)
        sb("tiny", [128, 8])
        sb("epsb", [128, 2])
        sb("qcb", [128, 3])

        def ps(n, sh):
            t[n] = es.enter_context(nc.psum_tensor(n, sh, F32))

        ps("pS", [128, 1024]); ps("pT", [128, 1024])
        ps("pU2", [128, 1024]); ps("pV2", [128, 1024])
        t["pU"] = t["pU2"][:, 0:512]
        t["pV"] = t["pU2"][:, 512:1024]
        t["pW"] = t["pV2"][:, 0:512]
        t["pX"] = t["pV2"][:, 512:1024]

        C = self.C
        C.sem("sd", 16); C.sem("gd", 16)
        C.sem("pe"); C.sem("act"); C.sem("dve"); C.sem("pool")

    def v(self, name, dt=None):
        ap = self.t[name][:, :]
        return ap.bitcast(dt) if dt is not None else ap

    def ar(self, arena, dt, off_kib, cols):
        ap = self.t[arena][:, :].bitcast(dt)
        esz = 2 if dt == BF16 else 4
        off = int(off_kib * 1024) // esz
        return ap[:, off:off + cols]

    def dma(self, queue, dst, src, waits=(), **kw):
        sem = "sd" if queue == "sp" else "gd"
        return self.C.op(queue, lambda e: e.dma_start(dst, src, **kw), waits=waits, inc=sem)

    def mm(self, out, lhsT, rhs, start, stop, waits=(), inc=None, tp=None):
        return self.C.op("pe", lambda e: e.matmul(out, lhsT, rhs, start=start, stop=stop,
                                                  tile_position=tp), waits=waits, inc=inc)

    def trans(self, out, in_, waits=(), inc=None):
        idb = self.t["ident_bf"][:, :]
        return self.C.op("pe", lambda e: e.transpose(out, in_, idb), waits=waits, inc=inc)

    def act(self, out, in_, func, bias=None, scale=1.0, accum=None, waits=()):
        if bias is None:
            return self.C.op("act", lambda e: e.activation(out, in_, func, scale=scale,
                                                           accum_out=accum), waits=waits, inc="act")
        return self.C.op("act", lambda e: e.activation(out, in_, func, bias=bias, scale=scale,
                                                       accum_out=accum), waits=waits, inc="act")

    def dve(self, build, waits=()):
        return self.C.op("dve", build, waits=waits, inc="dve")

    def pool(self, build, waits=()):
        return self.C.op("pool", build, waits=waits, inc="pool")

    def dbg_dump(self, ap_bf, cols=2048):
        if not DBG:
            return
        t = self.t
        scr = self.ar("ar3", F32, 16, 1024)
        sl = self._dbg_slot
        self._dbg_slot += 1
        self.barrier()
        for h in range(0, cols, 1024):
            w = min(1024, cols - h)
            self.dve(lambda e, o=scr[:, 0:w], i=ap_bf[:, h:h + w]: e.tensor_copy(o, i))
            ev = self.C.ev("dve")
            self.dma("sp", self.dram["dbg"][sl][:, h:h + w], scr[:, 0:w], waits=[ev],
                     allow_slow_non_contiguous=True)
            self.barrier()

    def barrier(self):
        C = self.C
        evs = [C.ev(s) for s in ("pe", "act", "dve", "pool", "sd", "gd")]
        for e in ENGS:
            C.wait(e, evs)

    def build(self):
        with self.es:
            self.declare()
            self.load_consts()
            if "l1f" in self.stages:
                self.quant_weights_once()
                self.barrier()
            for b in range(self.b_loc):
                self.load_x(b)
                self.barrier()
                for stg in self.stages:
                    getattr(self, "emit_" + stg)(b)
                    self.barrier()
                self.store_x(b)
            self.C.wait("sp", [self.ev_store])
            C = self.C
            handles = {}
            for name in C.counts:
                handles[name] = self.es.enter_context(self.nc.semaphore(name))
            C.handles = handles
            with self.nc.Block() as block:
                for ename in ENGS:
                    cmds = C.cmds[ename]
                    if cmds:
                        getattr(block, _BLOCKN[ename])(
                            lambda e, cmds=tuple(cmds): [c(e) for c in cmds])
        return self.nc

    def load_consts(self):
        d = self.dram
        st = self.stages
        if "l0a" in st:
            self.dma("sp", self.v("bqT"), d["bqT0"])
            self.dma("sp", self.v("bkT"), d["bkT0"])
            self.dma("sp", self.v("blT"), d["blT0"])
        for dst, src in [(self.v("ident_bf"), d["ident"]), (self.v("ones_bf"), d["ones"]),
                         (self.v("o64x2_bf"), d["o64x2"]), (self.v("R2_bf"), d["R2"]),
                         (self.v("I2_bf"), d["I2"])]:
            self.dma("pool", dst, src)
        self.C.op("dve", lambda e: e.memset(self.t["epsb"][:, 0:1], 1e-5), inc="dve")
        self.C.op("dve", lambda e: e.memset(self.t["epsb"][:, 1:2], 1e-6), inc="dve")
        self.C.op("dve", lambda e: e.memset(self.t["qcb"][:, 0:1], 2.0), inc="dve")
        self.C.op("dve", lambda e: e.memset(self.t["qcb"][:, 1:2], -2.0), inc="dve")
        self.C.op("dve", lambda e: e.memset(self.t["qcb"][:, 2:3], -4.0), inc="dve")
        if "l0f" in st:
            self.dma("pool", self.t["beta_sb"][:, :], d["bet"])
            self.dma("pool", self.t["b2sb"][:, :], d["b2r"])
            self.dma("pool", self.v("rw_bf").rearrange("p (c e) -> p c e", e=NE), d["rw0"].rearrange("(c p) e -> p c e", p=128))
        self.barrier()

    def load_x(self, b):
        src = self.dram["x"][b].rearrange("(t p) d -> p t d", p=128)
        dst = self.t["x_sb"][:, :].rearrange("p (t d) -> p t d", d=D)
        self.dma("sp", dst, src)

    def store_x(self, b):
        self.barrier()
        dst = self.dram["y"][b].rearrange("(t p) d -> p t d", p=128)
        src = self.t["x_sb"][:, :].rearrange("p (t d) -> p t d", d=D)
        self.ev_store = self.dma("sp", dst, src)

    def ln_stats(self):
        C, t = self.C, self.t
        x, m, r = t["x_sb"], t["stat_m"], t["stat_r"]
        dump = self.ar("ar2", BF16, 44, 1024)
        for tt in range(NT):
            xt = x[:, tt * D:(tt + 1) * D]
            self.dve(lambda e, o=m[:, tt:tt + 1], i=xt: e.reduce_sum(o, i, axis=AX.X))
            self.act(dump, xt, AF.Square, accum=r[:, tt:tt + 1])
        evd, eva = C.ev("dve"), C.ev("act")
        self.dve(lambda e: e.tensor_scalar(m[:, :], m[:, :], 1.0 / D, None, OP.mult),
                 waits=[evd])
        self.dve(lambda e: e.tensor_tensor(t["tiny"][:, 0:1] if False else t["nrm"][:, 0:NT],
                                           m[:, :], m[:, :], OP.mult))
        self.dve(lambda e: e.scalar_tensor_tensor(r[:, :], r[:, :], 1.0 / D, t["nrm"][:, 0:NT],
                                                  OP.mult, OP.subtract), waits=[eva])
        ev = C.ev("dve")
        self.act(r[:, :], r[:, :], AF.Ln, bias=self.t["epsb"][:, 0:1], waits=[ev])
        self.act(r[:, :], r[:, :], AF.Exp, scale=-0.5)
        return C.ev("act")

    def xnT_build(self, ev_stat, tt_lo=0, tt_hi=NT, dst_kib=0):
        C, t = self.C, self.t
        x, m, r = t["x_sb"], t["stat_m"], t["stat_r"]
        W = (tt_hi - tt_lo) * 128
        xnT = self.ar("ar2", BF16, dst_kib, NC * W)
        scr = self.ar("ar2", BF16, 40, 4096)
        pE = self.t["pW"][:, :].bitcast(BF16)
        pF = self.t["pX"][:, :].bitcast(BF16)
        cons = {0: None, 1: None}
        evac = {0: None, 1: None}
        for i, tt in enumerate(range(tt_lo, tt_hi)):
            sl = i % 2
            xn_t = scr[:, sl * 2048:sl * 2048 + 1024]
            self.dve(lambda e, o=xn_t, x_=x[:, tt * D:(tt + 1) * D],
                     mm=m[:, tt:tt + 1], rr=r[:, tt:tt + 1]:
                     e.tensor_scalar(o, x_, mm, rr, OP.subtract, OP.mult),
                     waits=[ev_stat, cons[sl]])
            ev_xn = C.ev("dve")
            pb = pE if sl == 0 else pF
            for c in range(NC):
                self.trans(pb[:, c * 128:(c + 1) * 128], xn_t[:, c * 128:(c + 1) * 128],
                           waits=[ev_xn, evac[sl]], inc="pe" if c == NC - 1 else None)
            ev_tr = C.ev("pe")
            cons[sl] = ev_tr
            dst = xnT.rearrange("p (c q) -> p c q", q=W)[:, :, i * 128:i * 128 + 128]
            srcv = pb.rearrange("p (c q) -> p c q", q=128)
            self.act(dst, srcv, AF.Copy, waits=[ev_tr])
            evac[sl] = C.ev("act")
        return C.ev("act")

    def emit_l0a(self, b):
        self.attn(b, 0)

    def emit_l1a(self, b):
        self.attn(b, 1)

    def attn(self, b, layer):
        C, t, d = self.C, self.t, self.dram
        x = t["x_sb"]
        TW = 16 * 65
        ar1b = self.t["ar1"][:, :].bitcast(BF16)
        Vg = ar1b[:, 0:NT * TW]
        aoT = ar1b[:, NT * TW:NT * TW + NC * S]
        aoTc = lambda c: aoT[:, c * S:(c + 1) * S]
        xnT = self.ar("ar2", BF16, 0, NC * S)
        xnTc = lambda c: xnT[:, c * S:(c + 1) * S]
        qt = self.ar("ar2", BF16, 32, S)
        kt_ = self.ar("ar2", BF16, 36, S)
        scr_b = self.ar("ar2", BF16, 40, 4096)
        scr_f = self.ar("ar2", F32, 40, 2048)
        pt_slot = [scr_b[:, i * 512:(i + 1) * 512] for i in range(4)]
        pS, pT = t["pS"], t["pT"]
        pU, pV_, pW, pX = t["pU"], t["pV"], t["pW"], t["pX"]
        bank4 = lambda i, j: [pS[:, j * 512:(j + 1) * 512], pT[:, j * 512:(j + 1) * 512],
                              [pU, pV_][j][:, :], [pW, pX][j][:, :]][i]

        if layer == 0:
            latT = self.ar("ar3", BF16, 0, 2 * S)
            wl = self.ar("ar3", BF16, 8, 2048)
            wk = self.ar("ar3", BF16, 12, 2048)
            wv = self.ar("ar3", BF16, 16, 2048)
            self.dma("pool", wl.rearrange("p (c l) -> p c l", l=LAT),
                     d["Wl0"].rearrange("(c p) l -> p c l", p=128))
            self.dma("pool", wk.rearrange("p (c n) -> p c n", n=D),
                     d["Wk0"].rearrange("(c p) n -> p c n", p=128))
            self.dma("pool", wv.rearrange("p (c n) -> p c n", n=D),
                     d["Wv0"].rearrange("(c p) n -> p c n", p=128))
            self.dma("pool", t["rows_bf"][:, :], d["rows0"][:, 0:1024])
            Wq_src, Wo_src = d["Wq0"], d["Wo0"]
        else:
            cs2 = self.ar("ar3", BF16, 0, 2 * S)
            msk = self.ar("ar3", BF16, 8, 896)
            self.dma("pool", cs2, d["cs2"])
            self.dma("pool", msk, d["mask"])
            Wq_src, Wo_src = d["WqT1"], d["WoT1"]
        if layer == 0:
            wstr = self.ar("ar3", BF16, 8, 2048)
            wjits = [self.ar("ar3", BF16, 8, 1024), self.ar("ar3", BF16, 10, 1024)]
        else:
            wstr = self.ar("ar3", BF16, 10, 2048)
            wjits = [self.ar("ar3", BF16, 14, 2048), self.ar("ar3", BF16, 10, 2048)]
        ev_wq = {}
        ev_wk = {}

        def issue_jit(hp, waits=()):
            sl = hp % 2
            ev_wq[sl] = self.dma("pool", wjits[sl][:, 0:1024].rearrange("p (c m) -> p c m", m=128),
                                 Wq_src[:, hp * 128:(hp + 1) * 128].rearrange("(c p) m -> p c m", p=128),
                                 waits=list(waits))
            if layer == 1:
                ev_wk[sl] = self.dma("pool", wjits[sl][:, 1024:2048].rearrange("p (c m) -> p c m", m=128),
                                     d["WkT1"][:, hp * 128:(hp + 1) * 128].rearrange("(c p) m -> p c m", p=128),
                                     waits=list(waits))

        ev_stat = self.ln_stats()
        if DBG:
            self.dbg_dump(t["stat_m"], 16)
            self.dbg_dump(t["stat_r"], 16)
        self.xnT_build(ev_stat)
        self.barrier()
        if PHASES < 1:
            return

        if layer == 0:
            for lc in range(2):
                for w in range(NW):
                    pb = pU if (lc * NW + w) % 2 == 0 else pV_
                    for c in range(NC):
                        self.mm(pb[:, :], wl[:, c * LAT + lc * 128:c * LAT + lc * 128 + 128],
                                xnTc(c)[:, w * 512:(w + 1) * 512],
                                start=(c == 0), stop=(c == NC - 1),
                                inc="pe" if c == NC - 1 else None)
                    ev = C.ev("pe")
                    self.act(latT[:, lc * S + w * 512: lc * S + w * 512 + 512], pb[:, :],
                             AF.Identity, bias=t["blT"][:, lc:lc + 1], waits=[ev])
            self.barrier()
        if PHASES < 2:
            return

        ones_ap = bass.AP(ar1b.tensor, 64, [list(ar1b.ap[0]), [TW, NT], [65, 16], [1, 1]])
        self.dve(lambda e: e.memset(ones_ap, 1.0))
        vw_cons = {}
        for ttg in range(4):
            nk = 2 if layer == 0 else NC
            evl = {}
            for c in range(nk):
                if layer == 1:
                    sl = c % 2
                    evl[sl] = self.dma("pool", wstr[:, sl * 1024:sl * 1024 + 1024],
                                       d["WvT1"][c * 128:(c + 1) * 128, :],
                                       waits=[vw_cons.get(sl)])
                for i in range(4):
                    tt = ttg * 4 + i
                    for j in range(2):
                        if layer == 0:
                            lhsT = latT[:, c * S + tt * 128:c * S + tt * 128 + 128]
                            rhs = wv[:, c * D + j * 512:c * D + j * 512 + 512]
                            w8 = []
                        else:
                            lhsT = xnTc(c)[:, tt * 128:(tt + 1) * 128]
                            rhs = wstr[:, (c % 2) * 1024 + j * 512:(c % 2) * 1024 + j * 512 + 512]
                            w8 = [evl[c % 2]]
                        last = (c == nk - 1) and layer == 1
                        self.mm(bank4(i, j), lhsT, rhs, start=(c == 0), stop=last,
                                waits=w8, inc="pe" if (j == 1 and i == 3) else None)
                if layer == 1:
                    vw_cons[c % 2] = C.ev("pe")
            if layer == 0:
                for i in range(4):
                    tt = ttg * 4 + i
                    for j in range(2):
                        self.mm(bank4(i, j), t["ones_bf"][0:1, 0:128],
                                t["rows_bf"][0:1, j * 512:(j + 1) * 512],
                                start=False, stop=True,
                                inc="pe" if (i == 3 and j == 1) else None)
            ev = C.ev("pe")
            for i in range(4):
                tt = ttg * 4 + i
                for j in range(2):
                    base = tt * TW + j * 8 * 65
                    dst = bass.AP(ar1b.tensor, ar1b.offset + base,
                                  [list(ar1b.ap[0]), [65, 8], [1, 64]])
                    self.dve(lambda e, o=dst, p=bank4(i, j): e.tensor_copy(o, p), waits=[ev])
            self.barrier()

        if PHASES < 3:
            return
        for hp in range(min(NC, PHASES - 2)):
            sl_j = hp % 2
            if hp == 0:
                issue_jit(0)
            wjit = wjits[sl_j]
            ev_evq = {0: None, 1: None}
            ev_rot = None
            for dst, which in ((qt, "q"), (kt_, "k")):
                for w in range(NW):
                    pb = pT[:, (w % 2) * 512:(w % 2) * 512 + 512]
                    if layer == 0 and which == "k":
                        for lc in range(2):
                            self.mm(pb, wk[:, lc * D + hp * 128:lc * D + hp * 128 + 128],
                                    latT[:, lc * S + w * 512:lc * S + w * 512 + 512],
                                    start=(lc == 0), stop=(lc == 1),
                                    waits=[ev_evq[w % 2]],
                                    inc="pe" if lc == 1 else None)
                    else:
                        wj = wjit[:, 0:1024] if which == "q" else wjit[:, 1024:2048]
                        evw = ev_wq[sl_j] if which == "q" else ev_wk[sl_j]
                        for c in range(NC):
                            self.mm(pb, wj[:, c * 128:(c + 1) * 128],
                                    xnTc(c)[:, w * 512:(w + 1) * 512],
                                    start=(c == 0), stop=(c == NC - 1),
                                    waits=[evw, ev_evq[w % 2]],
                                    inc="pe" if c == NC - 1 else None)
                    ev = C.ev("pe")
                    dsl = dst[:, w * 512:(w + 1) * 512]
                    if layer == 0:
                        bias = t["bqT"][:, hp:hp + 1] if which == "q" else t["bkT"][:, hp:hp + 1]
                        self.act(dsl, pb, AF.Identity, bias=bias, waits=[ev])
                        ev_evq[w % 2] = C.ev("act")
                    else:
                        qp = scr_b[:, 0:512] if which == "q" else scr_b[:, 512:1024]
                        self.act(qp, pb, AF.Copy, waits=[ev])
                        eva = C.ev("act")
                        ev_evq[w % 2] = eva
                        rb = pV_[:, :]
                        self.mm(rb, t["R2_bf"][:, :], qp, start=True, stop=True,
                                waits=[eva, ev_rot], inc="pe")
                        evr = C.ev("pe")
                        fA = scr_f[:, 1024:1536]
                        fB = scr_f[:, 1536:2048]
                        self.dve(lambda e, o=fA, a=qp, c_=cs2[:, w * 512:(w + 1) * 512]:
                                 e.tensor_tensor(o, a, c_, OP.mult), waits=[eva])
                        self.dve(lambda e, o=fB, a=rb, c_=cs2[:, S + w * 512:S + w * 512 + 512]:
                                 e.tensor_tensor(o, a, c_, OP.mult), waits=[evr])
                        ev_rot = C.ev("dve")
                        self.dve(lambda e, o=dsl, a=fA, b_=fB:
                                 e.tensor_tensor(o, a, b_, OP.add))
            ev_qk = C.ev("act") if layer == 0 else C.ev("dve")
            if hp + 1 < NC:
                issue_jit(hp + 1, waits=[C.ev("pe")])
            if DBG and hp == 0:
                self.dbg_dump(qt)
                self.dbg_dump(kt_)
                self.dbg_dump(t["nrm"], 16)
                self.dbg_dump(t["csA"], 1)
                self.dbg_dump(xnTc(0))
                self.dbg_dump(wjit[:, 0:1024], 1024)
                self.dbg_dump(latT[:, 0:2048] if layer == 0 else qt)

            if SUBP < 1:
                continue
            nrm, nrm_bf, csA = t["nrm"], t["nrm_bf"], t["csA"]
            ev_nrd = None
            for src_i, (tens, col0) in enumerate(((qt, 0), (kt_, 4))):
                for w in range(NW):
                    sq = scr_b[:, 2048 + (w % 2) * 512: 2048 + (w % 2) * 512 + 512]
                    self.act(sq, tens[:, w * 512:(w + 1) * 512], AF.Square, waits=[ev_qk])
                    eva = C.ev("act")
                    self.mm(pU[0:2, :], t["o64x2_bf"][:, :], sq, start=True, stop=True,
                            waits=[eva, ev_nrd], inc="pe")
                    evp = C.ev("pe")
                    self.dve(lambda e, o=nrm[0:2, col0 + w:col0 + w + 1], i=pU[0:2, :]:
                             e.reduce_max(o, i, axis=AX.X), waits=[evp])
                    ev_nrd = C.ev("dve")
            self.dve(lambda e: e.reduce_max(nrm[0:2, 8:9], nrm[0:2, 0:4], axis=AX.X))
            self.dve(lambda e: e.reduce_max(nrm[0:2, 9:10], nrm[0:2, 4:8], axis=AX.X))
            self.dve(lambda e: e.tensor_tensor(nrm[0:2, 10:11], nrm[0:2, 8:9],
                                               nrm[0:2, 9:10], OP.mult))
            self.dve(lambda e: e.tensor_copy(nrm_bf[0:2, 0:1], nrm[0:2, 10:11]))
            ev = C.ev("dve")
            self.mm(pU[0:1, 0:2], nrm_bf[0:2, 0:1], t["I2_bf"][:, :], start=True, stop=True,
                    waits=[ev], inc="pe")
            ev = C.ev("pe")
            self.act(nrm[0:1, 11:13], pU[0:1, 0:2], AF.Ln, waits=[ev])
            ev = C.ev("act")
            self.dve(lambda e: e.reduce_max(nrm[0:1, 13:14], nrm[0:1, 11:13], axis=AX.X),
                     waits=[ev])
            ev = C.ev("dve")
            self.act(nrm[0:1, 14:15], nrm[0:1, 13:14], AF.Exp, scale=0.5, waits=[ev])
            ev = C.ev("act")
            self.dve(lambda e: e.tensor_copy(nrm_bf[0:1, 1:2], nrm[0:1, 14:15]), waits=[ev])
            ev = C.ev("dve")
            self.mm(pU[:, 0:1], t["ones_bf"][0:1, 0:128], nrm_bf[0:1, 1:2], start=True,
                    stop=True, waits=[ev], inc="pe")
            ev = C.ev("pe")
            self.act(csA[:, :], pU[:, 0:1], AF.Copy, scale=-SCALE, waits=[ev])
            ev_c = C.ev("act")
            self.barrier()
            if SUBP < 2:
                continue

            for qw in range(NW):
                kt_hi = NT if layer == 0 else 4 * qw + 4
                np_pairs = kt_hi // 2
                sc_bank = {0: t["pT"], 1: t["pU2"]}
                ev_exp = {}
                ev_pv = {}
                for pi in range(np_pairs):
                    for j in range(2):
                        ev_s = None
                        for ti in range(2):
                            ktile = 2 * pi + ti
                            sbank = sc_bank[j][:, ti * 512:(ti + 1) * 512]
                            self.mm(sbank, kt_[64 * j:64 * j + 64, ktile * 128:(ktile + 1) * 128],
                                    qt[64 * j:64 * j + 64, qw * 512:(qw + 1) * 512],
                                    start=True, stop=True,
                                    waits=[ev_exp.get((pi - 1, j))],
                                    inc="pe")
                            ev_s = C.ev("pe")
                            if layer == 1 and ktile >= 4 * qw:
                                jd = ktile - 4 * qw
                                o0 = 384 - 128 * jd
                                self.dve(lambda e, p=sbank, m_=msk[:, o0:o0 + 512]:
                                         e.tensor_tensor(p, p, m_, OP.add), waits=[ev_s])
                                ev_s = C.ev("dve")
                        slot = scr_b[:, ((pi % 2) * 2 + j) * 1024:((pi % 2) * 2 + j) * 1024 + 1024]
                        self.act(slot, sc_bank[j][:, :], AF.Exp, bias=csA[:, 0:1], scale=SCALE,
                                 waits=[ev_s, ev_pv.get((pi - 2, j))])
                        ev_exp[(pi, j)] = C.ev("act")
                        if SUBP < 3:
                            ev_pv[(pi, j)] = None
                            continue
                        h_glob = 2 * hp + j
                        for ti in range(2):
                            ktile = 2 * pi + ti
                            lhs = Vg[:, ktile * TW + 65 * h_glob:ktile * TW + 65 * h_glob + 65]
                            self.mm(pS[0:65, j * 512:(j + 1) * 512], lhs,
                                    slot[:, ti * 512:(ti + 1) * 512],
                                    start=(ktile == 0), stop=(ktile == kt_hi - 1),
                                    waits=[ev_exp[(pi, j)]], inc="pe" if ti == 1 else None)
                        ev_pv[(pi, j)] = C.ev("pe")
                for j in range(2 if SUBP >= 4 else 0):
                    Lrow = pS[64:65, j * 512:(j + 1) * 512]
                    lnl = scr_f[:, 1024:1536]
                    self.act(lnl[0:1, 0:512], Lrow, AF.Ln, waits=[ev_pv[(kt_hi // 2 - 1, j)]])
                    self.act(t["recL"][:, :], lnl[0:1, 0:512], AF.Exp, scale=-1.0)
                    ev = C.ev("act")
                    self.mm(t["pV2"][0:64, 0:512], t["ones_bf"][0:1, 0:64], t["recL"][:, :],
                            start=True, stop=True,
                            waits=[ev, getattr(self, "_ev_lbc", None)], inc="pe")
                    ev = C.ev("pe")
                    self.dve(lambda e: e.tensor_copy(t["Lbc"][:, :], t["pV2"][0:64, 0:512]),
                             waits=[ev])
                    self._ev_lbc = C.ev("dve")
                    self.dve(lambda e, o=aoTc(hp)[64 * j:64 * j + 64, qw * 512:(qw + 1) * 512],
                             p=pS[0:64, j * 512:(j + 1) * 512]:
                             e.tensor_tensor(o, p, t["Lbc"][:, :], OP.mult),
                             waits=[self._ev_lbc])
                self.barrier()
            if DBG and hp == 0:
                self.dbg_dump(aoTc(0))
        if PHASES < 20:
            return
        if layer == 0:
            self.dma("pool", t["rows_bf"][:, :], d["rows0"][:, 1024:2048])
            self.barrier()
        bo_row = t["rows_bf"] if layer == 0 else None
        wo_cons = {}
        for ttg in range(4):
            evl = {}
            for c in range(NC):
                sl = c % 2
                evl[sl] = self.dma("pool", wstr[:, sl * 1024:sl * 1024 + 1024],
                                   Wo_src[c * 128:(c + 1) * 128, :],
                                   waits=[wo_cons.get(sl)])
                for i in range(4):
                    tt = ttg * 4 + i
                    for j in range(2):
                        last = (c == NC - 1) and bo_row is None
                        self.mm(bank4(i, j), aoTc(c)[:, tt * 128:(tt + 1) * 128],
                                wstr[:, sl * 1024 + j * 512:sl * 1024 + j * 512 + 512],
                                start=(c == 0), stop=last, waits=[evl[sl]],
                                inc="pe" if (i == 3 and j == 1) else None)
                wo_cons[sl] = C.ev("pe")
            if bo_row is not None:
                for i in range(4):
                    for j in range(2):
                        self.mm(bank4(i, j), t["ones_bf"][0:1, 0:128],
                                bo_row[0:1, j * 512:(j + 1) * 512], start=False, stop=True,
                                inc="pe" if (i == 3 and j == 1) else None)
            ev = C.ev("pe")
            for i in range(4):
                tt = ttg * 4 + i
                for j in range(2):
                    xsl = x[:, tt * D + j * 512:tt * D + j * 512 + 512]
                    self.dve(lambda e, o=xsl, p=bank4(i, j):
                             e.tensor_tensor(o, o, p, OP.add), waits=[ev])
            self.barrier()

    def emit_l0f(self, b):
        moe(self, b)

    def emit_l1f(self, b):
        ffn1(self, b)

    def quant_weights_once(self):
        quant_weights_once(self)


def moe(K, b):
    C, t, d = K.C, K.t, K.dram
    x = t["x_sb"]
    xnT = K.ar("ar2", BF16, 0, NC * S)
    xnTc = lambda c: xnT[:, c * S:(c + 1) * S]
    gbc = K.ar("ar2", F32, 32, D)
    xj = K.ar("ar2", F32, 36, D)
    ctr = K.ar("ar2", F32, 40, D)
    dump = K.ar("ar2", BF16, 44, D)
    w2sb = K.ar("ar1", BF16, 0, 32 * D)
    w1s = K.ar("ar3", BF16, 0, 2048)
    htile = K.ar("ar3", BF16, 4, 8192)
    stage = K.ar("ar3", BF16, 4, 8192)
    pS, pT = t["pS"], t["pT"]
    pU, pV_, pW, pX = t["pU"], t["pV"], t["pW"], t["pX"]
    m, r = t["stat_m"], t["stat_r"]

    ev_stat = K.ln_stats()
    K.xnT_build(ev_stat)
    K.barrier()

    g8, g8b, gates = t["g8"], t["g8b"], t["gates"]
    nrm = t["nrm"]
    ev_lgrd = None
    ev_gtrd = None
    ev_btrd = None
    for tt in range(NT):
        for c in range(NC):
            K.mm(pU[:, 0:8], xnTc(c)[:, tt * 128:(tt + 1) * 128],
                 t["rw_bf"][:, c * 8:(c + 1) * 8], start=(c == 0), stop=(c == NC - 1),
                 waits=[ev_lgrd],
                 inc="pe" if c == NC - 1 else None)
        ev = C.ev("pe")
        lg = pU[:, 0:8]
        K.dve(lambda e: e.reduce_max(nrm[:, 0:1], lg, axis=AX.X), waits=[ev])
        K.dve(lambda e: e.tensor_scalar(g8[:, :], lg, nrm[:, 0:1], None, OP.is_equal))
        msked = gates[:, tt * 8:(tt + 1) * 8]
        K.dve(lambda e, o=msked: e.scalar_tensor_tensor(o, g8[:, :],
                                                        -1e30, lg, OP.mult, OP.add))
        K.dve(lambda e, i=msked: e.reduce_max(nrm[:, 1:2], i, axis=AX.X))
        K.dve(lambda e, o=msked: e.tensor_scalar(o, o, nrm[:, 1:2], None, OP.is_equal))
        K.dve(lambda e: e.tensor_tensor(nrm[:, 2:3], nrm[:, 1:2], nrm[:, 0:1], OP.subtract))
        ev = C.ev("dve")
        K.act(nrm[:, 3:4], nrm[:, 2:3], AF.Exp, waits=[ev])
        ev = C.ev("act")
        K.dve(lambda e: e.tensor_scalar(nrm[:, 4:5], nrm[:, 3:4], 1.0, None, OP.add),
              waits=[ev])
        K.dve(lambda e: e.reciprocal(nrm[:, 5:6], nrm[:, 4:5]))
        K.dve(lambda e: e.tensor_scalar(nrm[:, 6:7], nrm[:, 5:6], -1.0, 1.0, OP.mult, OP.add))
        K.dve(lambda e: e.tensor_scalar(g8[:, :], g8[:, :], nrm[:, 5:6], None, OP.mult))
        K.dve(lambda e, o=msked: e.scalar_tensor_tensor(o, o, nrm[:, 6:7], g8[:, :],
                                                         OP.mult, OP.add))
        ev_lgrd = C.ev("dve")
        if DBG:
            evx = C.ev("dve")
            K.dma("sp", d["dbg"][tt][:, 0:8], gates[:, tt * 8:(tt + 1) * 8], waits=[evx],
                  allow_slow_non_contiguous=True)
        K.dve(lambda e, i=msked: e.tensor_copy(g8b[:, :], i))
        ev = C.ev("dve")
        K.trans(t["pX"][:, :].bitcast(BF16)[0:8, 0:128], g8b[:, :], waits=[ev, ev_gtrd],
                inc="pe")
        ev = C.ev("pe")
        K.dve(lambda e: e.tensor_copy(t["gatesT_bf"][:, :],
                                      t["pX"][:, :].bitcast(BF16)[0:8, 0:128]), waits=[ev])
        ev_gtrd = C.ev("dve")
        ev = ev_gtrd
        for j in range(2):
            K.mm(pW[:, :], t["gatesT_bf"][:, :], t["beta_sb"][:, j * 512:(j + 1) * 512],
                 start=True, stop=True, waits=[ev, ev_btrd], inc="pe")
            evp = C.ev("pe")
            K.dve(lambda e, o=ctr[:, j * 512:(j + 1) * 512]: e.tensor_copy(o, pW[:, :]),
                  waits=[evp])
            ev_btrd = C.ev("dve")
        evd = C.ev("dve")
        K.dma("sp", d["yscr"][tt], ctr, waits=[evd])
    K.barrier()
    if MOEPH < 1:
        return

    ev_haccum = {}
    for e_i in range(NE):
        hbuf = d["hscr"][e_i % 2]
        ev_w2 = K.dma("pool", w2sb.rearrange("p (c dd) -> p c dd", dd=D),
                      d["w2"][e_i].rearrange("(c p) dd -> p c dd", p=128))
        K.dma("sp", t["b1t"][:, :], d["b1T"][e_i])
        K.dma("pool", t["rows_bf"][:, :], t["b2sb"][e_i:e_i + 1, :])
        K.dma("pool", gbc, d["gam"][e_i:e_i + 1, :].broadcast_to((128, D)))
        ev_ld = [C.ev("gd"), C.ev("sd")]
        ev_slab = {}
        ev_stg = {}
        ev_p1rd = {}
        for fc in range(32):
            sl = fc % 2
            ev_slab[sl] = K.dma("pool", w1s[:, sl * 1024:sl * 1024 + 1024]
                                .rearrange("p (c f) -> p c f", f=128),
                                d["w1"][e_i][:, fc * 128:(fc + 1) * 128]
                                .rearrange("(c p) f -> p c f", p=128),
                                waits=[ev_slab.get((sl, "cons"))])
            for w in range(NW):
                pb = [pU, pV_, pW, pX][w]
                for c in range(NC):
                    K.mm(pb[:, :], w1s[:, sl * 1024 + c * 128:sl * 1024 + c * 128 + 128],
                         xnTc(c)[:, w * 512:(w + 1) * 512], start=(c == 0),
                         stop=(c == NC - 1), waits=[ev_slab[sl], ev_p1rd.get(w)],
                         inc="pe" if c == NC - 1 else None)
                evp = C.ev("pe")
                if w == NW - 1:
                    ev_slab[(sl, "cons")] = evp
                ssl = stage[:, (w % 2) * 4096 + (fc % 4) * 1024:(w % 2) * 4096 + (fc % 4) * 1024 + 512]
                K.act(ssl, pb[:, :], AF.Gelu, bias=t["b1t"][:, fc:fc + 1],
                      waits=[evp, ev_stg.get((w % 2, fc % 4))])
                eva = C.ev("act")
                ev_p1rd[w] = eva
                ev_stg[(w % 2, fc % 4)] = K.dma("sp", hbuf[fc][:, w * 512:(w + 1) * 512],
                                                ssl, waits=[eva])
        K.barrier()
        ev_ht = {}
        ev_grp = {}
        for tt in range(NT):
            sl = tt % 2
            ev_ht[sl] = K.dma("sp", htile[:, sl * 4096:sl * 4096 + 4096]
                              .rearrange("p (c q) -> p c q", q=128),
                              hbuf[:, :, tt * 128:(tt + 1) * 128]
                              .rearrange("c p q -> p c q"),
                              waits=[ev_ht.get((sl, "cons"))])
            grp = pS if sl == 0 else pT
            for j in range(2):
                for fc in range(32):
                    K.mm(grp[:, j * 512:(j + 1) * 512],
                         htile[:, sl * 4096 + fc * 128:sl * 4096 + fc * 128 + 128],
                         w2sb[:, fc * D + j * 512:fc * D + j * 512 + 512],
                         start=(fc == 0), stop=False,
                         waits=[ev_ht[sl], ev_w2, ev_grp.get((sl, j))])
                K.mm(grp[:, j * 512:(j + 1) * 512], t["ones_bf"][0:1, 0:128],
                     t["rows_bf"][0:1, j * 512:(j + 1) * 512], start=False, stop=True,
                     waits=ev_ld, inc="pe")
            ev_ht[(sl, "cons")] = C.ev("pe")
            evp = C.ev("pe")
            K.dve(lambda e, o=xj, xs=x[:, tt * D:(tt + 1) * D], mm=m[:, tt:tt + 1],
                  rr=r[:, tt:tt + 1]: e.tensor_scalar(o, xs, mm, rr, OP.subtract, OP.mult))
            K.dve(lambda e, g=grp[:, :]: e.tensor_tensor(g, g, xj, OP.add), waits=[evp])
            K.dve(lambda e, g=grp[:, :]: e.reduce_sum(nrm[:, 0:1], g, axis=AX.X))
            ev = C.ev("dve")
            K.act(dump, grp[:, :], AF.Square, accum=nrm[:, 1:2], waits=[ev])
            eva = C.ev("act")
            K.dve(lambda e: e.tensor_scalar(nrm[:, 0:1], nrm[:, 0:1], 1.0 / D, None, OP.mult))
            K.dve(lambda e: e.tensor_tensor(nrm[:, 2:3], nrm[:, 0:1], nrm[:, 0:1], OP.mult))
            K.dve(lambda e: e.scalar_tensor_tensor(nrm[:, 1:2], nrm[:, 1:2], 1.0 / D,
                                                   nrm[:, 2:3], OP.mult, OP.subtract),
                  waits=[eva])
            ev = C.ev("dve")
            K.act(nrm[:, 1:2], nrm[:, 1:2], AF.Ln, bias=t["epsb"][:, 1:2], waits=[ev])
            K.act(nrm[:, 1:2], nrm[:, 1:2], AF.Exp, scale=-0.5)
            eva = C.ev("act")
            K.dve(lambda e, g=grp[:, :]: e.tensor_scalar(ctr, g, nrm[:, 0:1], nrm[:, 1:2],
                                                         OP.subtract, OP.mult), waits=[eva])
            ev_grp[(sl, 0)] = C.ev("dve")
            ev_grp[(sl, 1)] = ev_grp[(sl, 0)]
            gate_col = gates[:, tt * 8 + e_i:tt * 8 + e_i + 1]
            K.dve(lambda e, gc=gate_col: e.scalar_tensor_tensor(ctr, ctr, gc, gbc, OP.mult, OP.mult))
            ev = C.ev("dve")
            ev_haccum[tt] = K.dma("pool", d["yscr"][tt], ctr,
                                  waits=[ev, ev_haccum.get(tt)], accum_op=OP.add)
        K.barrier()
    slots = [ctr, xj]
    ev_ld = {}
    ev_add = {}
    for tt in range(NT):
        sl = tt % 2
        ev_ld[sl] = K.dma("sp" if sl == 0 else "pool", slots[sl], d["yscr"][tt],
                          waits=[ev_add.get(sl)])
        K.dve(lambda e, o=x[:, tt * D:(tt + 1) * D], c_=slots[sl]:
              e.tensor_tensor(o, o, c_, OP.add), waits=[ev_ld[sl]])
        ev_add[sl] = C.ev("dve")


def mk_qctx(K, arena, off_kib, W):
    slot = W / 256.0
    c = {}
    for i, n in enumerate(("A", "Bu", "Cu", "Du", "Eu", "Fu", "Gu")):
        c[n] = K.ar(arena, F32, off_kib + i * slot, W)
    sa = K.ar(arena, F32, off_kib + 7 * slot, W // 8)
    c["sc"] = sa[:, 0:W // 16]
    c["rec"] = sa[:, W // 16:W // 8]
    c["size_kib"] = 7 * slot + slot / 8.0
    return c


def t3v(ap, nb):
    return bass.AP(ap.tensor, ap.offset, [list(ap.ap[0]), [16, nb], [1, 16]])


def qt_p1(K, ctx, tile_f32, W, waits=()):
    C = K.C
    Bu = ctx["Bu"][:, 0:W]
    nb = W // 16
    sc, rec = ctx["sc"][:, 0:nb], ctx["rec"][:, 0:nb]
    K.dve(lambda e: e.reduce_max(sc, t3v(tile_f32, nb), axis=AX.X,
                                 apply_absolute_value=True), waits=list(waits))
    K.dve(lambda e: e.tensor_scalar(sc, sc, 1.0 / 6.0, 1e-12, OP.mult, OP.max))
    K.dve(lambda e: e.reciprocal(rec, sc))
    K.dve(lambda e: e.tensor_tensor(t3v(Bu, nb), t3v(tile_f32, nb),
                                    blk_bcast(rec, nb, 16), OP.mult))
    return C.ev("dve")


def qt_p2a(K, ctx, W, ev_p1):
    Bu, Cu = ctx["Bu"][:, 0:W], ctx["Cu"][:, 0:W]
    K.act(Cu, Bu, AF.Abs, waits=[ev_p1])
    return K.C.ev("act")


def qt_stair(K, ctx, W, ev_abs, ev_p4_prev):
    C = K.C
    A, Cu, Du = ctx["A"][:, 0:W], ctx["Cu"][:, 0:W], ctx["Du"][:, 0:W]
    K.pool(lambda e: e.tensor_scalar(Du, Cu, 1.9999990, MAGIC, OP.mult, OP.add),
           waits=[ev_abs, ev_p4_prev])
    K.pool(lambda e: e.tensor_scalar(Du, Du, -MAGIC, 0.5, OP.add, OP.mult))
    K.pool(lambda e: e.tensor_scalar(A, Cu, 0.9999995, MAGIC, OP.mult, OP.add))
    K.pool(lambda e: e.tensor_scalar(A, A, -MAGIC, 1.0, OP.add, OP.mult))
    return C.ev("pool")


def qt_p2b(K, ctx, W, qcb, ev_stair):
    C = K.C
    A, Du, Eu, Fu, Gu = [ctx[n][:, 0:W] for n in ("A", "Du", "Eu", "Fu", "Gu")]
    K.act(Eu, Du, AF.Relu, bias=qcb[:, 0:1], scale=-1.0, waits=[ev_stair])
    K.act(Fu, Eu, AF.Identity, bias=qcb[:, 0:1], scale=-1.0)
    K.act(Gu, A, AF.Relu, bias=qcb[:, 1:2])
    K.act(Eu, A, AF.Relu, bias=qcb[:, 2:3])
    return C.ev("act")


def qt_p3b(K, ctx, W, ev_p2b):
    C = K.C
    A, Eu, Fu, Gu = [ctx[n][:, 0:W] for n in ("A", "Eu", "Fu", "Gu")]
    K.pool(lambda e: e.tensor_tensor(A, Gu, Eu, OP.subtract), waits=[ev_p2b])
    K.pool(lambda e: e.tensor_tensor(A, A, Fu, OP.add))
    return C.ev("pool")


def qt_p4(K, ctx, out_bf, W, ev_p2b, ev_p3b, out_waits=()):
    C = K.C
    A, Bu, Cu, Du = [ctx[n][:, 0:W] for n in ("A", "Bu", "Cu", "Du")]
    nb = W // 16
    sc = ctx["sc"][:, 0:nb]
    K.dve(lambda e: e.tensor_single_scalar(Bu.bitcast(I32), Bu.bitcast(I32), -2147483648,
                                           OP.bitwise_and), waits=[ev_p2b])
    K.dve(lambda e: e.tensor_single_scalar(Du, Cu, 5.0, OP.is_gt))
    K.dve(lambda e: e.scalar_tensor_tensor(Du, Du, 2.0, A, OP.mult, OP.add),
          waits=[ev_p3b])
    K.dve(lambda e: e.tensor_tensor(t3v(Bu.bitcast(I32), nb),
                                    blk_bcast(sc.bitcast(I32), nb, 16),
                                    t3v(Bu.bitcast(I32), nb), OP.bitwise_or))
    K.dve(lambda e: e.tensor_tensor(out_bf, Du, Bu, OP.mult), waits=list(out_waits))
    return C.ev("dve")


def quant_weights_once(K):
    C, t, d = K.C, K.t, K.dram
    bigs = [K.ar("ar2", F32, 11 * i, IFF) for i in range(2)]
    qc = [mk_qctx(K, "ar1", 0, 512), mk_qctx(K, "ar1", 14.5, 512)]
    qb = K.ar("ar1", BF16, 29, IFF)
    stg = K.ar("ar1", BF16, 35, 2816)
    qcb = K.t["qcb"][:, :]
    pE = t["pW"][:, :].bitcast(BF16)
    pF = t["pX"][:, :].bitcast(BF16)
    units = []
    for (srcn, dstn, nrc, wid, sub) in (("wg1", "wgT", 22, D, 512), ("wu1", "wuT", 22, D, 512),
                                        ("wd1", "wdT", 8, IFF, 256)):
        for rc in range(nrc):
            units.append((srcn, dstn, rc, wid, sub))
    ev_big = {}
    ev_pfx = {}
    ev_evac = {0: None, 1: None}
    ev_trans_prev = None
    ev_stg_dma = None
    ev_p2b = {}
    ev_st = {}
    ev_p3b = {}
    ev_p4 = {}
    tinfo = {}
    gt0 = [0]

    def u_p4(gt):
        cx = gt % 2
        qsl, sub_ = tinfo[gt]
        ev_p4[gt] = qt_p4(K, qc[cx], qsl, sub_, ev_p2b[gt], ev_p3b[gt],
                          out_waits=[ev_trans_prev])

    def u_p23(gt):
        _, sub_ = tinfo[gt]
        ev_p2b[gt] = qt_p2b(K, qc[gt % 2], sub_, qcb, ev_st[gt])
        ev_p3b[gt] = qt_p3b(K, qc[gt % 2], sub_, ev_p2b[gt])

    for g, (srcn, dstn, rc, wid, sub) in enumerate(units):
        if g == 0:
            ev_big[0] = K.dma("sp", bigs[0][:, 0:wid], d[srcn][rc * 128:(rc + 1) * 128, :])
        if g + 1 < len(units):
            ns, nd, nrc_, nwid, _ = units[g + 1]
            ev_big[(g + 1) % 2] = K.dma("sp", bigs[(g + 1) % 2][:, 0:nwid],
                                        d[ns][nrc_ * 128:(nrc_ + 1) * 128, :],
                                        waits=[ev_pfx.get((g + 1) % 2)])
        big = bigs[g % 2]
        nst = wid // sub
        base = gt0[0]
        for sti in range(nst):
            gt = base + sti
            cx = gt % 2
            tinfo[gt] = (qb[:, sti * sub:(sti + 1) * sub], sub)
            if gt >= base + 2:
                u_p4(gt - 2)
            ev_p1 = qt_p1(K, qc[cx], big[:, sti * sub:(sti + 1) * sub], sub,
                          waits=[ev_big[g % 2], ev_p2b.get(gt - 2), ev_p4.get(gt - 2)])
            ev_p2a = qt_p2a(K, qc[cx], sub, ev_p1)
            ev_st[gt] = qt_stair(K, qc[cx], sub, ev_p2a, ev_p4.get(gt - 2))
            if gt >= base + 1:
                u_p23(gt - 1)
        ev_pfx[g % 2] = C.ev("dve")
        last = base + nst - 1
        if nst >= 2:
            u_p4(last - 1)
        u_p23(last)
        u_p4(last)
        gt0[0] = last + 1
        ev_sfx = C.ev("dve")
        nch = wid // 128
        for c in range(nch):
            pb = pE if c % 2 == 0 else pF
            K.trans(pb[:, 0:128], qb[:, c * 128:(c + 1) * 128],
                    waits=[ev_sfx, ev_evac[c % 2]], inc="pe")
            ev = C.ev("pe")
            K.act(stg[:, c * 128:(c + 1) * 128], pb[:, 0:128], AF.Copy,
                  waits=[ev, ev_stg_dma])
            ev_evac[c % 2] = C.ev("act")
        ev_trans_prev = C.ev("pe")
        ev = C.ev("act")
        ev_stg_dma = K.dma("sp", d[dstn].rearrange("(c p) n -> p c n", p=128)[:, :, rc * 128:(rc + 1) * 128],
                           stg[:, 0:nch * 128].rearrange("p (c n) -> p c n", n=128), waits=[ev])


def ffn1(K, b):
    C, t, d = K.C, K.t, K.dram
    x = t["x_sb"]
    hqT = K.ar("ar1", BF16, 0, 22 * 1024)
    hqTc = lambda ic: hqT[:, ic * 1024:(ic + 1) * 1024]
    WG = [K.ar("ar1", BF16, 44, 4096), K.ar("ar3", BF16, 0, 4096)]
    WU = [K.ar("ar1", BF16, 52, 4096), K.ar("ar3", BF16, 8, 4096)]
    xqT = K.ar("ar2", BF16, 0, NC * 1024)
    xqTc = lambda c: xqT[:, c * 1024:(c + 1) * 1024]
    qc = [mk_qctx(K, "ar2", 16, 512), mk_qctx(K, "ar2", 31.5, 512)]
    qtl = [K.ar("ar2", BF16, 30.25, 512), K.ar("ar2", BF16, 45.75, 512)]
    wds = K.ar("ar3", BF16, 16, 2048)
    qcb = K.t["qcb"][:, :]
    pS, pT = t["pS"], t["pT"]
    pU, pV_, pW, pX = t["pU"], t["pV"], t["pW"], t["pX"]
    bank4 = lambda i, j: [pS[:, j * 512:(j + 1) * 512], pT[:, j * 512:(j + 1) * 512],
                          [pU, pV_][j][:, :], [pW, pX][j][:, :]][i]
    m, r = t["stat_m"], t["stat_r"]
    pE = pW[:, :].bitcast(BF16)
    pF = pX[:, :].bitcast(BF16)

    ev_stat = K.ln_stats()
    NSL = [512, 512, 512, 512, 512, 256]
    csum = [0]
    for w_ in NSL:
        csum.append(csum[-1] + w_)
    ev_wg = {}
    ev_wu = {}
    ev_slab_cons = {}

    def issue_slab(ns):
        par = ns % 2
        nn = NSL[ns]
        nc0 = csum[ns]
        ev_wg[par] = K.dma("sp", WG[par][:, 0:8 * nn].rearrange("p (c n) -> p c n", n=nn),
                           d["wgT"][:, nc0:nc0 + nn].rearrange("(c p) n -> p c n", p=128),
                           waits=[ev_slab_cons.get(par)])
        ev_wu[par] = K.dma("pool", WU[par][:, 0:8 * nn].rearrange("p (c n) -> p c n", n=nn),
                           d["wuT"][:, nc0:nc0 + nn].rearrange("(c p) n -> p c n", p=128),
                           waits=[ev_slab_cons.get(par)])

    for half in range(2):
        tt0 = half * 8
        issue_slab(0)
        issue_slab(1)
        ev_tre = {0: None, 1: None}
        ev_p1 = {}
        ev_p2a = {}
        ev_p2b = {}
        ev_st = {}
        ev_p3b = {}
        ev_p4 = {}
        ev_tr = {}

        def xq_trans(n):
            cx = n % 2
            i, sub = n // 2, n % 2
            for cc in range(4):
                c = sub * 4 + cc
                pb = pE if cc % 2 == 0 else pF
                K.trans(pb[:, 0:128], qtl[cx][:, cc * 128:(cc + 1) * 128],
                        waits=[ev_p4[n], ev_tre[cc % 2]], inc="pe")
                ev = C.ev("pe")
                K.act(xqTc(c)[:, i * 128:(i + 1) * 128], pb[:, 0:128], AF.Copy,
                      waits=[ev])
                ev_tre[cc % 2] = C.ev("act")
            ev_tr[cx] = C.ev("pe")

        def xq_p4(n):
            cx = n % 2
            ev_p4[n] = ev_p4[cx] = qt_p4(K, qc[cx], qtl[cx][:, 0:512], 512,
                                         ev_p2b[n], ev_p3b[n],
                                         out_waits=[ev_tr.get(cx)])

        for n in range(16):
            cx = n % 2
            tt, sub = tt0 + n // 2, n % 2
            if n >= 2:
                xq_p4(n - 2)
            IN = qc[cx]["A"][:, 0:512]
            K.dve(lambda e, o=IN, xs=x[:, tt * D + sub * 512:tt * D + sub * 512 + 512],
                  mm=m[:, tt:tt + 1], rr=r[:, tt:tt + 1]:
                  e.tensor_scalar(o, xs, mm, rr, OP.subtract, OP.mult),
                  waits=[ev_stat, ev_st.get(cx)])
            ev_p1[n] = qt_p1(K, qc[cx], IN, 512, waits=[ev_p2b.get(n - 2)])
            ev_p2a[n] = qt_p2a(K, qc[cx], 512, ev_p1[n])
            ev_st[cx] = ev_st[n] = qt_stair(K, qc[cx], 512, ev_p2a[n], ev_p4.get(cx))
            if n >= 1:
                ev_p2b[n - 1] = qt_p2b(K, qc[(n - 1) % 2], 512, qcb, ev_st[n - 1])
                ev_p3b[n - 1] = qt_p3b(K, qc[(n - 1) % 2], 512, ev_p2b[n - 1])
            if n >= 2:
                xq_trans(n - 2)
        xq_p4(14)
        xq_trans(14)
        ev_p2b[15] = qt_p2b(K, qc[1], 512, qcb, ev_st[15])
        ev_p3b[15] = qt_p3b(K, qc[1], 512, ev_p2b[15])
        xq_p4(15)
        xq_trans(15)
        K.barrier()
        nsw_of = lambda k: NSL[k // 8]
        ic0_of = lambda k: csum[k // 8] // 128
        ev_silu = {}
        ev_mult = {}
        ev_p2b = {}
        ev_st = {}
        ev_p3b = {}
        ev_p4 = {}
        ev_tr = {}
        ev_tre = {0: None, 1: None}

        def h_trans(k):
            cx = k % 2
            nsw = nsw_of(k)
            i = k % 8
            for cc in range(nsw // 128):
                ic = ic0_of(k) + cc
                pb = pE if cc % 2 == 0 else pF
                K.trans(pb[:, 0:128], qtl[cx][:, cc * 128:(cc + 1) * 128],
                        waits=[ev_p4[k], ev_tre[cc % 2]], inc="pe")
                ev = C.ev("pe")
                K.act(hqTc(ic)[:, i * 128:(i + 1) * 128], pb[:, 0:128], AF.Copy,
                      waits=[ev])
                ev_tre[cc % 2] = C.ev("act")
            ev_tr[cx] = C.ev("pe")

        def h_p4(k):
            cx = k % 2
            ev_p4[k] = ev_p4[(cx, "c")] = qt_p4(K, qc[cx], qtl[cx][:, 0:nsw_of(k)],
                                                nsw_of(k), ev_p2b[k], ev_p3b[k],
                                                out_waits=[ev_tr.get(cx)])

        def h_p23(k):
            ev_p2b[k] = qt_p2b(K, qc[k % 2], nsw_of(k), qcb, ev_st[k])
            ev_p3b[k] = qt_p3b(K, qc[k % 2], nsw_of(k), ev_p2b[k])

        for ns, nsw in enumerate(NSL):
            par = ns % 2
            for i in range(8):
                k = ns * 8 + i
                cx = k % 2
                gbank = pU[:, 0:nsw] if cx == 0 else pS[:, 0:nsw]
                ubank = pV_[:, 0:nsw] if cx == 0 else pS[:, 512:512 + nsw]
                for c in range(NC):
                    K.mm(gbank, xqTc(c)[:, i * 128:(i + 1) * 128],
                         WG[par][:, c * nsw:(c + 1) * nsw], start=(c == 0),
                         stop=(c == NC - 1),
                         waits=[ev_wg[par], ev_silu.get(k - 2)],
                         inc="pe" if c == NC - 1 else None)
                for c in range(NC):
                    K.mm(ubank, xqTc(c)[:, i * 128:(i + 1) * 128],
                         WU[par][:, c * nsw:(c + 1) * nsw], start=(c == 0),
                         stop=(c == NC - 1),
                         waits=[ev_wu[par], ev_mult.get(k - 2)],
                         inc="pe" if c == NC - 1 else None)
                evp = C.ev("pe")
                if i == 7:
                    ev_slab_cons[par] = evp
                if k >= 2:
                    h_p4(k - 2)
                IN = qc[cx]["A"][:, 0:nsw]
                K.act(IN, gbank, AF.Silu,
                      waits=[evp, ev_p4.get((cx, "c")), ev_mult.get(k - 2)])
                ev_silu[k] = C.ev("act")
                K.dve(lambda e, o=IN, u=ubank: e.tensor_tensor(o, o, u, OP.mult),
                      waits=[ev_silu[k]])
                ev_mult[k] = C.ev("dve")
                ev_p1 = qt_p1(K, qc[cx], IN, nsw, waits=[ev_p2b.get(k - 2)])
                ev_p2a = qt_p2a(K, qc[cx], nsw, ev_p1)
                ev_st[k] = qt_stair(K, qc[cx], nsw, ev_p2a, ev_p4.get((cx, "c")))
                if k >= 1:
                    h_p23(k - 1)
                if k >= 2:
                    h_trans(k - 2)
            if ns + 2 < len(NSL):
                issue_slab(ns + 2)
        h_p4(46)
        h_trans(46)
        h_p23(47)
        h_p4(47)
        h_trans(47)
        K.barrier()
        for ttg in range(2):
            evl = {}
            for ic in range(22):
                sl = ic % 2
                evl[sl] = K.dma("sp" if sl == 0 else "pool",
                                wds[:, sl * 1024:sl * 1024 + 1024],
                                d["wdT"][ic * 128:(ic + 1) * 128, :],
                                waits=[evl.get((sl, "cons"))])
                for i in range(4):
                    il = ttg * 4 + i
                    for j in range(2):
                        K.mm(bank4(i, j), hqTc(ic)[:, il * 128:(il + 1) * 128],
                             wds[:, sl * 1024 + j * 512:sl * 1024 + j * 512 + 512],
                             start=(ic == 0), stop=(ic == 21), waits=[evl[sl]],
                             inc="pe" if (i == 3 and j == 1) else None)
                evl[(sl, "cons")] = C.ev("pe")
            ev = C.ev("pe")
            for i in range(4):
                tt = tt0 + ttg * 4 + i
                for j in range(2):
                    xsl = x[:, tt * D + j * 512:tt * D + j * 512 + 512]
                    K.dve(lambda e, o=xsl, p=bank4(i, j): e.tensor_tensor(o, o, p, OP.add),
                          waits=[ev])
            K.barrier()


MK.emit_l0f = lambda self, b: moe(self, b)
MK.emit_l1f = lambda self, b: ffn1(self, b)
MK.quant_weights_once = lambda self: quant_weights_once(self)


def _host_consts():
    c = {}
    c["ident"] = np.eye(128, dtype=np.float32)
    c["ones"] = np.ones((128, 128), dtype=np.float32)
    o = np.zeros((128, 2), dtype=np.float32)
    o[:64, 0] = 1.0
    o[64:, 1] = 1.0
    c["o64x2"] = o
    R = np.zeros((128, 128), dtype=np.float32)
    for i in range(64):
        R[2 * i + 1, 2 * i] = -1.0
        R[2 * i, 2 * i + 1] = 1.0
    c["R2"] = R
    tl = np.zeros((128, 64), dtype=np.float32)
    tl[:, 0] = 1.0
    c["tail"] = tl
    c["I2"] = np.eye(2, dtype=np.float32)
    inv = 1.0 / (10000.0 ** (np.arange(0, DH, 2, dtype=np.float64) / DH))
    pos = np.arange(S, dtype=np.float64)
    fr = pos[:, None] * inv[None, :]
    emb = np.repeat(fr, 2, axis=1)
    cs = np.zeros((128, 2 * S), dtype=np.float32)
    cs[:64, 0:S] = np.cos(emb).T.astype(np.float32)
    cs[64:, 0:S] = cs[:64, 0:S]
    cs[:64, S:] = np.sin(emb).T.astype(np.float32)
    cs[64:, S:] = cs[:64, S:]
    c["cs2"] = cs
    kk = np.arange(128)[:, None]
    ccol = np.arange(896)[None, :]
    c["mask"] = np.where(kk > ccol - 384, np.float32(-1e30), np.float32(0.0)).astype(np.float32)
    return c


def _prep_shared(inputs, stages):
    f = lambda a: np.ascontiguousarray(np.asarray(a), dtype=np.float32)
    s = {}
    if "l0a" in stages:
        s["Wq0"] = f(inputs["L0_attn_Wq"])
        s["Wl0"] = f(inputs["L0_attn_Wl"])
        s["Wk0"] = f(inputs["L0_attn_Wk"])
        s["Wv0"] = f(inputs["L0_attn_Wv"])
        s["Wo0"] = f(inputs["L0_attn_Wo"])
        s["bqT0"] = f(inputs["L0_attn_bq"]).reshape(NC, 128).T.copy()
        s["bkT0"] = f(inputs["L0_attn_bk"]).reshape(NC, 128).T.copy()
        s["blT0"] = f(inputs["L0_attn_bl"]).reshape(2, 128).T.copy()
        s["rows0"] = np.concatenate([f(inputs["L0_attn_bv"]), f(inputs["L0_attn_bo"])])[None, :].copy()
    if "l0f" in stages:
        s["rw0"] = f(inputs["L0_ffn_router_w"])
        s["w1"] = f(inputs["L0_ffn_w1"])
        s["w2"] = f(inputs["L0_ffn_w2"])
        s["b1T"] = f(inputs["L0_ffn_b1"]).reshape(NE, 32, 128).transpose(0, 2, 1).copy()
        s["b2r"] = f(inputs["L0_ffn_b2"])
        s["gam"] = f(inputs["L0_ffn_gamma"])
        s["bet"] = f(inputs["L0_ffn_beta"])
    if "l1a" in stages:
        s["WqT1"] = f(inputs["L1_attn_Wq"]).T.copy()
        s["WkT1"] = f(inputs["L1_attn_Wk"]).T.copy()
        s["WvT1"] = f(inputs["L1_attn_Wv"]).T.copy()
        s["WoT1"] = f(inputs["L1_attn_Wo"]).T.copy()
    if "l1f" in stages:
        s["wg1"] = f(inputs["L1_ffn_w_gate"])
        s["wu1"] = f(inputs["L1_ffn_w_up"])
        s["wd1"] = f(inputs["L1_ffn_w_down"])
    c = _host_consts()
    s["ident"] = c["ident"]; s["ones"] = c["ones"]; s["o64x2"] = c["o64x2"]
    s["R2"] = c["R2"]; s["tail"] = c["tail"]; s["I2"] = c["I2"]
    if "l1a" in stages:
        s["cs2"] = c["cs2"]; s["mask"] = c["mask"]
    return s


_BUILT = {}


def _get_nc(b_loc, stages):
    key = (b_loc, tuple(stages))
    if key not in _BUILT:
        _BUILT[key] = MK(b_loc, tuple(stages)).build()
    return _BUILT[key]


def kernel(**inputs):
    n_cores = 8
    B = int(np.asarray(inputs["L0_attn_x"]).shape[0])
    bl = B // n_cores
    x = np.ascontiguousarray(np.asarray(inputs["L0_attn_x"], dtype=np.float32))
    for stages in GROUPS:
        nc = _get_nc(bl, stages)
        shared = _prep_shared(inputs, stages)
        in_maps = []
        for c in range(n_cores):
            mdl = dict(shared)
            mdl["x"] = np.ascontiguousarray(x[c * bl:(c + 1) * bl])
            in_maps.append(mdl)
        res = run_bass_kernel_spmd(nc, in_maps, core_ids=list(range(n_cores)))
        x = np.concatenate([r["y"] for r in res.results], axis=0)
    return x


# version 10 (final)
# speedup vs baseline: 1.1289x; 1.0341x over previous
import contextlib
import math
import numpy as np
import ml_dtypes

import concourse.bass as bass
import concourse.mybir as mybir
from concourse.bass_utils import run_bass_kernel_spmd
from concourse.alu_op_type import AluOpType as OP

F32 = mybir.dt.float32
BF16 = mybir.dt.bfloat16
I32 = mybir.dt.int32
AF = mybir.ActivationFunctionType
AX = mybir.AxisListType

D = 1024
S = 2048
NH = 16
DH = 64
LAT = 256
DFF = 4096
NE = 8
IFF = 2816
NT = S // 128
NC = D // 128
NW = S // 512
SCALE = 1.0 / math.sqrt(DH)
MAGIC = 8388608.0
ENGS = ("pe", "act", "dve", "pool", "sp")
_BLOCKN = {"pe": "tensor", "act": "scalar", "dve": "vector", "pool": "gpsimd", "sp": "sync"}
ALL_STAGES = ("l0a", "l0f", "l1a", "l1f")
DBG = False
PHASES = 99
SUBP = 9
MOEPH = 9

GROUPS = [ALL_STAGES]


def blk_bcast(ap2d, nb, blk):
    return bass.AP(ap2d.tensor, ap2d.offset, [list(ap2d.ap[0]), list(ap2d.ap[1]), [0, blk]])


def aug_lhsT(vcol_ap, tail_off_elems):
    base = vcol_ap.offset
    step = tail_off_elems - base
    return bass.AP(vcol_ap.tensor, base, [list(vcol_ap.ap[0]), [step, 2], [1, 64]])


class Ctx:
    def __init__(self):
        self.cmds = {e: [] for e in ENGS}
        self.counts = {}
        self.steps = {}
        self.waited = {e: {} for e in ENGS}
        self.handles = None

    def sem(self, name, step=1):
        self.counts[name] = 0
        self.steps[name] = step

    def ev(self, name):
        return (name, self.counts[name])

    def _filter(self, eng, waits):
        ws = []
        for w in waits:
            if w is None:
                continue
            s, v = w
            if v <= 0:
                continue
            if self.waited[eng].get(s, 0) < v:
                ws.append((s, v))
                self.waited[eng][s] = v
        return ws

    def op(self, eng, build, waits=(), inc=None, fence=True):
        if fence and eng in ("dve", "act", "pool") and inc is not None                 and self.counts.get(inc, 0) > 0:
            waits = list(waits) + [(inc, self.counts[inc])]
        ws = self._filter(eng, waits)
        amt = self.steps[inc] if inc is not None else 0
        if inc is not None:
            self.counts[inc] += amt

        def run(e, ws=tuple(ws), build=build, inc=inc, amt=amt, self=self):
            h = self.handles
            for (s, v) in ws:
                e.wait_ge(h[s], v)
            inst = build(e)
            if inc is not None:
                inst.then_inc(h[inc], amt)
        self.cmds[eng].append(run)
        return (inc, self.counts[inc]) if inc is not None else None

    def wait(self, eng, evs):
        ws = self._filter(eng, evs)
        if not ws:
            return

        def run(e, ws=tuple(ws), self=self):
            h = self.handles
            for (s, v) in ws:
                e.wait_ge(h[s], v)
        self.cmds[eng].append(run)


class MK:
    def __init__(self, b_loc=4, stages=ALL_STAGES):
        self.b_loc = b_loc
        self.stages = stages
        self.nc = bass.Bass("TRN2", target_bir_lowering=False)
        self.C = Ctx()
        self.es = contextlib.ExitStack()
        self.t = {}
        self.dram = {}

    def declare(self):
        nc, d, t = self.nc, self.dram, self.t
        B = self.b_loc
        st = self.stages

        def ext(n, sh, dt=F32):
            d[n] = nc.dram_tensor(n, sh, dt, kind="ExternalInput").ap()

        ext("x", [B, S, D])
        if "l0a" in st:
            ext("Wq0", [D, D]); ext("Wl0", [D, LAT]); ext("Wk0", [LAT, D])
            ext("Wv0", [LAT, D]); ext("Wo0", [D, D])
            ext("bqT0", [128, NC]); ext("blT0", [128, LAT // 128]); ext("bkT0", [128, NC])
            ext("rows0", [1, 2048])
        if "l0f" in st:
            ext("rw0", [D, NE]); ext("w1", [NE, D, DFF]); ext("w2", [NE, DFF, D])
            ext("b1T", [NE, 128, DFF // 128]); ext("b2r", [NE, D])
            ext("gam", [NE, D]); ext("bet", [NE, D])
        if "l1a" in st:
            ext("WqT1", [D, D]); ext("WkT1", [D, D]); ext("WvT1", [D, D]); ext("WoT1", [D, D])
            ext("cs2", [128, 2 * S]); ext("mask", [128, 896])
        if "l1f" in st:
            ext("wg1", [IFF, D]); ext("wu1", [IFF, D]); ext("wd1", [D, IFF])
        ext("ident", [128, 128]); ext("ones", [128, 128]); ext("o64x2", [128, 2])
        ext("R2", [128, 128]); ext("tail", [128, 64]); ext("I2", [2, 2])
        d["y"] = nc.dram_tensor("y", [B, S, D], F32, kind="ExternalOutput").ap()
        if DBG:
            d["dbg"] = nc.dram_tensor("dbg", [16, 128, 2048], F32, kind="ExternalOutput").ap()
            self._dbg_slot = 0
        if "l0f" in st:
            d["hscr"] = nc.dram_tensor("hscr", [2, DFF // 128, 128, S], BF16).ap()
            d["yscr"] = nc.dram_tensor("yscr", [NT, 128, D], F32).ap()
        if "l1f" in st:
            d["wgT"] = nc.dram_tensor("wgT", [D, IFF], BF16).ap()
            d["wuT"] = nc.dram_tensor("wuT", [D, IFF], BF16).ap()
            d["wdT"] = nc.dram_tensor("wdT", [IFF, D], BF16).ap()

        es = self.es

        def sb(n, sh, dt=F32):
            t[n] = es.enter_context(nc.sbuf_tensor(n, sh, dt))

        sb("x_sb", [128, NT * D])
        sb("ar1", [128, 16512])
        sb("ar2", [128, 12288])
        sb("ar3", [128, 5120])
        sb("ident_bf", [128, 128], BF16)
        sb("ones_bf", [128, 128], BF16)
        sb("o64x2_bf", [128, 2], BF16)
        sb("R2_bf", [128, 128], BF16)
        sb("I2_bf", [2, 2], BF16)
        sb("rows_bf", [1, 1024], BF16)
        sb("bqT", [128, NC]); sb("bkT", [128, NC]); sb("blT", [128, LAT // 128])
        sb("stat_m", [128, NT]); sb("stat_r", [128, NT])
        sb("gates", [128, NT * NE])
        sb("gatesT_bf", [8, 128], BF16)
        sb("g8", [128, NE]); sb("g8b", [128, NE], BF16)
        sb("beta_sb", [NE, D], BF16)
        sb("b2sb", [NE, D], BF16)
        sb("rw_bf", [128, NC * NE], BF16)
        sb("b1t", [128, 2 * (DFF // 128)])
        sb("csA", [128, 1])
        sb("recL", [1, 512], BF16)
        sb("Lbc", [64, 512], BF16)
        sb("nrm", [128, 16])
        sb("nrm_bf", [128, 4], BF16)
        sb("tiny", [128, 8])
        sb("epsb", [128, 2])
        sb("qcb", [128, 3])

        def ps(n, sh):
            t[n] = es.enter_context(nc.psum_tensor(n, sh, F32))

        ps("pS", [128, 1024]); ps("pT", [128, 1024])
        ps("pU2", [128, 1024]); ps("pV2", [128, 1024])
        t["pU"] = t["pU2"][:, 0:512]
        t["pV"] = t["pU2"][:, 512:1024]
        t["pW"] = t["pV2"][:, 0:512]
        t["pX"] = t["pV2"][:, 512:1024]

        C = self.C
        C.sem("sd", 16); C.sem("gd", 16)
        C.sem("pe"); C.sem("act"); C.sem("dve"); C.sem("pool")

    def v(self, name, dt=None):
        ap = self.t[name][:, :]
        return ap.bitcast(dt) if dt is not None else ap

    def ar(self, arena, dt, off_kib, cols):
        ap = self.t[arena][:, :].bitcast(dt)
        esz = 2 if dt == BF16 else 4
        off = int(off_kib * 1024) // esz
        return ap[:, off:off + cols]

    def dma(self, queue, dst, src, waits=(), **kw):
        sem = "sd" if queue == "sp" else "gd"
        return self.C.op(queue, lambda e: e.dma_start(dst, src, **kw), waits=waits, inc=sem)

    def mm(self, out, lhsT, rhs, start, stop, waits=(), inc=None, tp=None):
        return self.C.op("pe", lambda e: e.matmul(out, lhsT, rhs, start=start, stop=stop,
                                                  tile_position=tp), waits=waits, inc=inc)

    def trans(self, out, in_, waits=(), inc=None):
        idb = self.t["ident_bf"][:, :]
        return self.C.op("pe", lambda e: e.transpose(out, in_, idb), waits=waits, inc=inc)

    def act(self, out, in_, func, bias=None, scale=1.0, accum=None, waits=()):
        if bias is None:
            return self.C.op("act", lambda e: e.activation(out, in_, func, scale=scale,
                                                           accum_out=accum), waits=waits, inc="act")
        return self.C.op("act", lambda e: e.activation(out, in_, func, bias=bias, scale=scale,
                                                       accum_out=accum), waits=waits, inc="act")

    def dve(self, build, waits=()):
        return self.C.op("dve", build, waits=waits, inc="dve")

    def pool(self, build, waits=()):
        return self.C.op("pool", build, waits=waits, inc="pool")

    def dbg_dump(self, ap_bf, cols=2048):
        if not DBG:
            return
        t = self.t
        scr = self.ar("ar3", F32, 16, 1024)
        sl = self._dbg_slot
        self._dbg_slot += 1
        self.barrier()
        for h in range(0, cols, 1024):
            w = min(1024, cols - h)
            self.dve(lambda e, o=scr[:, 0:w], i=ap_bf[:, h:h + w]: e.tensor_copy(o, i))
            ev = self.C.ev("dve")
            self.dma("sp", self.dram["dbg"][sl][:, h:h + w], scr[:, 0:w], waits=[ev],
                     allow_slow_non_contiguous=True)
            self.barrier()

    def barrier(self):
        C = self.C
        evs = [C.ev(s) for s in ("pe", "act", "dve", "pool", "sd", "gd")]
        for e in ENGS:
            C.wait(e, evs)

    def build(self):
        with self.es:
            self.declare()
            self.load_consts()
            if "l1f" in self.stages:
                self.quant_weights_once()
                self.barrier()
            for b in range(self.b_loc):
                self.load_x(b)
                self.barrier()
                for stg in self.stages:
                    getattr(self, "emit_" + stg)(b)
                    self.barrier()
                self.store_x(b)
            self.C.wait("sp", [self.ev_store])
            C = self.C
            handles = {}
            for name in C.counts:
                handles[name] = self.es.enter_context(self.nc.semaphore(name))
            C.handles = handles
            with self.nc.Block() as block:
                for ename in ENGS:
                    cmds = C.cmds[ename]
                    if cmds:
                        getattr(block, _BLOCKN[ename])(
                            lambda e, cmds=tuple(cmds): [c(e) for c in cmds])
        return self.nc

    def load_consts(self):
        d = self.dram
        st = self.stages
        if "l0a" in st:
            self.dma("sp", self.v("bqT"), d["bqT0"])
            self.dma("sp", self.v("bkT"), d["bkT0"])
            self.dma("sp", self.v("blT"), d["blT0"])
        for dst, src in [(self.v("ident_bf"), d["ident"]), (self.v("ones_bf"), d["ones"]),
                         (self.v("o64x2_bf"), d["o64x2"]), (self.v("R2_bf"), d["R2"]),
                         (self.v("I2_bf"), d["I2"])]:
            self.dma("pool", dst, src)
        self.C.op("dve", lambda e: e.memset(self.t["epsb"][:, 0:1], 1e-5), inc="dve")
        self.C.op("dve", lambda e: e.memset(self.t["epsb"][:, 1:2], 1e-6), inc="dve")
        self.C.op("dve", lambda e: e.memset(self.t["qcb"][:, 0:1], 2.0), inc="dve")
        self.C.op("dve", lambda e: e.memset(self.t["qcb"][:, 1:2], -2.0), inc="dve")
        self.C.op("dve", lambda e: e.memset(self.t["qcb"][:, 2:3], -4.0), inc="dve")
        if "l0f" in st:
            self.dma("pool", self.t["beta_sb"][:, :], d["bet"])
            self.dma("pool", self.t["b2sb"][:, :], d["b2r"])
            self.dma("pool", self.v("rw_bf").rearrange("p (c e) -> p c e", e=NE), d["rw0"].rearrange("(c p) e -> p c e", p=128))
        self.barrier()

    def load_x(self, b):
        src = self.dram["x"][b].rearrange("(t p) d -> p t d", p=128)
        dst = self.t["x_sb"][:, :].rearrange("p (t d) -> p t d", d=D)
        self.dma("sp", dst, src)

    def store_x(self, b):
        self.barrier()
        dst = self.dram["y"][b].rearrange("(t p) d -> p t d", p=128)
        src = self.t["x_sb"][:, :].rearrange("p (t d) -> p t d", d=D)
        self.ev_store = self.dma("sp", dst, src)

    def ln_stats(self):
        C, t = self.C, self.t
        x, m, r = t["x_sb"], t["stat_m"], t["stat_r"]
        dump = self.ar("ar2", BF16, 44, 1024)
        for tt in range(NT):
            xt = x[:, tt * D:(tt + 1) * D]
            self.dve(lambda e, o=m[:, tt:tt + 1], i=xt: e.reduce_sum(o, i, axis=AX.X))
            self.act(dump, xt, AF.Square, accum=r[:, tt:tt + 1])
        evd, eva = C.ev("dve"), C.ev("act")
        self.dve(lambda e: e.tensor_scalar(m[:, :], m[:, :], 1.0 / D, None, OP.mult),
                 waits=[evd])
        self.dve(lambda e: e.tensor_tensor(t["tiny"][:, 0:1] if False else t["nrm"][:, 0:NT],
                                           m[:, :], m[:, :], OP.mult))
        self.dve(lambda e: e.scalar_tensor_tensor(r[:, :], r[:, :], 1.0 / D, t["nrm"][:, 0:NT],
                                                  OP.mult, OP.subtract), waits=[eva])
        ev = C.ev("dve")
        self.act(r[:, :], r[:, :], AF.Ln, bias=self.t["epsb"][:, 0:1], waits=[ev])
        self.act(r[:, :], r[:, :], AF.Exp, scale=-0.5)
        return C.ev("act")

    def xnT_build(self, ev_stat, tt_lo=0, tt_hi=NT, dst_kib=0):
        C, t = self.C, self.t
        x, m, r = t["x_sb"], t["stat_m"], t["stat_r"]
        W = (tt_hi - tt_lo) * 128
        xnT = self.ar("ar2", BF16, dst_kib, NC * W)
        scr = self.ar("ar2", BF16, 40, 4096)
        pE = self.t["pW"][:, :].bitcast(BF16)
        pF = self.t["pX"][:, :].bitcast(BF16)
        cons = {0: None, 1: None}
        evac = {0: None, 1: None}
        for i, tt in enumerate(range(tt_lo, tt_hi)):
            sl = i % 2
            xn_t = scr[:, sl * 2048:sl * 2048 + 1024]
            self.dve(lambda e, o=xn_t, x_=x[:, tt * D:(tt + 1) * D],
                     mm=m[:, tt:tt + 1], rr=r[:, tt:tt + 1]:
                     e.tensor_scalar(o, x_, mm, rr, OP.subtract, OP.mult),
                     waits=[ev_stat, cons[sl]])
            ev_xn = C.ev("dve")
            pb = pE if sl == 0 else pF
            for c in range(NC):
                self.trans(pb[:, c * 128:(c + 1) * 128], xn_t[:, c * 128:(c + 1) * 128],
                           waits=[ev_xn, evac[sl]], inc="pe" if c == NC - 1 else None)
            ev_tr = C.ev("pe")
            cons[sl] = ev_tr
            dst = xnT.rearrange("p (c q) -> p c q", q=W)[:, :, i * 128:i * 128 + 128]
            srcv = pb.rearrange("p (c q) -> p c q", q=128)
            self.act(dst, srcv, AF.Copy, waits=[ev_tr])
            evac[sl] = C.ev("act")
        return C.ev("act")

    def emit_l0a(self, b):
        self.attn(b, 0)

    def emit_l1a(self, b):
        self.attn(b, 1)

    def attn(self, b, layer):
        C, t, d = self.C, self.t, self.dram
        x = t["x_sb"]
        TW = 16 * 65
        ar1b = self.t["ar1"][:, :].bitcast(BF16)
        Vg = ar1b[:, 0:NT * TW]
        aoT = ar1b[:, NT * TW:NT * TW + NC * S]
        aoTc = lambda c: aoT[:, c * S:(c + 1) * S]
        xnT = self.ar("ar2", BF16, 0, NC * S)
        xnTc = lambda c: xnT[:, c * S:(c + 1) * S]
        qt = self.ar("ar2", BF16, 32, S)
        kt_ = self.ar("ar2", BF16, 36, S)
        scr_b = self.ar("ar2", BF16, 40, 4096)
        scr_f = self.ar("ar2", F32, 40, 2048)
        pt_slot = [scr_b[:, i * 512:(i + 1) * 512] for i in range(4)]
        pS, pT = t["pS"], t["pT"]
        pU, pV_, pW, pX = t["pU"], t["pV"], t["pW"], t["pX"]
        bank4 = lambda i, j: [pS[:, j * 512:(j + 1) * 512], pT[:, j * 512:(j + 1) * 512],
                              [pU, pV_][j][:, :], [pW, pX][j][:, :]][i]

        if layer == 0:
            latT = self.ar("ar3", BF16, 0, 2 * S)
            wl = self.ar("ar3", BF16, 8, 2048)
            wk = self.ar("ar3", BF16, 12, 2048)
            wv = self.ar("ar3", BF16, 16, 2048)
            self.dma("pool", wl.rearrange("p (c l) -> p c l", l=LAT),
                     d["Wl0"].rearrange("(c p) l -> p c l", p=128))
            self.dma("pool", wk.rearrange("p (c n) -> p c n", n=D),
                     d["Wk0"].rearrange("(c p) n -> p c n", p=128))
            self.dma("pool", wv.rearrange("p (c n) -> p c n", n=D),
                     d["Wv0"].rearrange("(c p) n -> p c n", p=128))
            self.dma("pool", t["rows_bf"][:, 0:1024], d["rows0"][:, 0:1024])
            Wq_src, Wo_src = d["Wq0"], d["Wo0"]
        else:
            cs2 = self.ar("ar3", BF16, 0, 2 * S)
            msk = self.ar("ar3", BF16, 8, 896)
            self.dma("pool", cs2, d["cs2"])
            self.dma("pool", msk, d["mask"])
            Wq_src, Wo_src = d["WqT1"], d["WoT1"]
        if layer == 0:
            wstr = self.ar("ar3", BF16, 8, 2048)
            wjits = [self.ar("ar3", BF16, 8, 1024), self.ar("ar3", BF16, 10, 1024)]
        else:
            wstr = self.ar("ar3", BF16, 10, 2048)
            wjits = [self.ar("ar3", BF16, 14, 2048), self.ar("ar3", BF16, 10, 2048)]
        ev_wq = {}
        ev_wk = {}

        def issue_jit(hp, waits=()):
            sl = hp % 2
            ev_wq[sl] = self.dma("pool", wjits[sl][:, 0:1024].rearrange("p (c m) -> p c m", m=128),
                                 Wq_src[:, hp * 128:(hp + 1) * 128].rearrange("(c p) m -> p c m", p=128),
                                 waits=list(waits))
            if layer == 1:
                ev_wk[sl] = self.dma("pool", wjits[sl][:, 1024:2048].rearrange("p (c m) -> p c m", m=128),
                                     d["WkT1"][:, hp * 128:(hp + 1) * 128].rearrange("(c p) m -> p c m", p=128),
                                     waits=list(waits))

        ev_stat = self.ln_stats()
        if DBG:
            self.dbg_dump(t["stat_m"], 16)
            self.dbg_dump(t["stat_r"], 16)
        self.xnT_build(ev_stat)
        self.barrier()
        if PHASES < 1:
            return

        if layer == 0:
            for lc in range(2):
                for w in range(NW):
                    pb = pU if (lc * NW + w) % 2 == 0 else pV_
                    for c in range(NC):
                        self.mm(pb[:, :], wl[:, c * LAT + lc * 128:c * LAT + lc * 128 + 128],
                                xnTc(c)[:, w * 512:(w + 1) * 512],
                                start=(c == 0), stop=(c == NC - 1),
                                inc="pe" if c == NC - 1 else None)
                    ev = C.ev("pe")
                    self.act(latT[:, lc * S + w * 512: lc * S + w * 512 + 512], pb[:, :],
                             AF.Identity, bias=t["blT"][:, lc:lc + 1], waits=[ev])
            self.barrier()
        if PHASES < 2:
            return

        ones_ap = bass.AP(ar1b.tensor, 64, [list(ar1b.ap[0]), [TW, NT], [65, 16], [1, 1]])
        self.dve(lambda e: e.memset(ones_ap, 1.0))
        vw_cons = {}
        for ttg in range(4):
            nk = 2 if layer == 0 else NC
            evl = {}
            for c in range(nk):
                if layer == 1:
                    sl = c % 2
                    evl[sl] = self.dma("pool", wstr[:, sl * 1024:sl * 1024 + 1024],
                                       d["WvT1"][c * 128:(c + 1) * 128, :],
                                       waits=[vw_cons.get(sl)])
                for i in range(4):
                    tt = ttg * 4 + i
                    for j in range(2):
                        if layer == 0:
                            lhsT = latT[:, c * S + tt * 128:c * S + tt * 128 + 128]
                            rhs = wv[:, c * D + j * 512:c * D + j * 512 + 512]
                            w8 = []
                        else:
                            lhsT = xnTc(c)[:, tt * 128:(tt + 1) * 128]
                            rhs = wstr[:, (c % 2) * 1024 + j * 512:(c % 2) * 1024 + j * 512 + 512]
                            w8 = [evl[c % 2]]
                        last = (c == nk - 1) and layer == 1
                        self.mm(bank4(i, j), lhsT, rhs, start=(c == 0), stop=last,
                                waits=w8, inc="pe" if (j == 1 and i == 3) else None)
                if layer == 1:
                    vw_cons[c % 2] = C.ev("pe")
            if layer == 0:
                for i in range(4):
                    tt = ttg * 4 + i
                    for j in range(2):
                        self.mm(bank4(i, j), t["ones_bf"][0:1, 0:128],
                                t["rows_bf"][0:1, j * 512:(j + 1) * 512],
                                start=False, stop=True,
                                inc="pe" if (i == 3 and j == 1) else None)
            ev = C.ev("pe")
            for i in range(4):
                tt = ttg * 4 + i
                for j in range(2):
                    base = tt * TW + j * 8 * 65
                    dst = bass.AP(ar1b.tensor, ar1b.offset + base,
                                  [list(ar1b.ap[0]), [65, 8], [1, 64]])
                    self.dve(lambda e, o=dst, p=bank4(i, j): e.tensor_copy(o, p), waits=[ev])
            self.barrier()

        if PHASES < 3:
            return
        for hp in range(min(NC, PHASES - 2)):
            sl_j = hp % 2
            if hp == 0:
                issue_jit(0)
            wjit = wjits[sl_j]
            ev_evq = {0: None, 1: None}
            ev_rot = None
            for dst, which in ((qt, "q"), (kt_, "k")):
                for w in range(NW):
                    pb = pT[:, (w % 2) * 512:(w % 2) * 512 + 512]
                    if layer == 0 and which == "k":
                        for lc in range(2):
                            self.mm(pb, wk[:, lc * D + hp * 128:lc * D + hp * 128 + 128],
                                    latT[:, lc * S + w * 512:lc * S + w * 512 + 512],
                                    start=(lc == 0), stop=(lc == 1),
                                    waits=[ev_evq[w % 2]],
                                    inc="pe" if lc == 1 else None)
                    else:
                        wj = wjit[:, 0:1024] if which == "q" else wjit[:, 1024:2048]
                        evw = ev_wq[sl_j] if which == "q" else ev_wk[sl_j]
                        for c in range(NC):
                            self.mm(pb, wj[:, c * 128:(c + 1) * 128],
                                    xnTc(c)[:, w * 512:(w + 1) * 512],
                                    start=(c == 0), stop=(c == NC - 1),
                                    waits=[evw, ev_evq[w % 2]],
                                    inc="pe" if c == NC - 1 else None)
                    ev = C.ev("pe")
                    dsl = dst[:, w * 512:(w + 1) * 512]
                    if layer == 0:
                        bias = t["bqT"][:, hp:hp + 1] if which == "q" else t["bkT"][:, hp:hp + 1]
                        self.act(dsl, pb, AF.Identity, bias=bias, waits=[ev])
                        ev_evq[w % 2] = C.ev("act")
                    else:
                        qp = scr_b[:, 0:512] if which == "q" else scr_b[:, 512:1024]
                        self.act(qp, pb, AF.Copy, waits=[ev])
                        eva = C.ev("act")
                        ev_evq[w % 2] = eva
                        rb = pV_[:, :]
                        self.mm(rb, t["R2_bf"][:, :], qp, start=True, stop=True,
                                waits=[eva, ev_rot], inc="pe")
                        evr = C.ev("pe")
                        fA = scr_f[:, 1024:1536]
                        fB = scr_f[:, 1536:2048]
                        self.dve(lambda e, o=fA, a=qp, c_=cs2[:, w * 512:(w + 1) * 512]:
                                 e.tensor_tensor(o, a, c_, OP.mult), waits=[eva])
                        self.dve(lambda e, o=fB, a=rb, c_=cs2[:, S + w * 512:S + w * 512 + 512]:
                                 e.tensor_tensor(o, a, c_, OP.mult), waits=[evr])
                        ev_rot = C.ev("dve")
                        self.dve(lambda e, o=dsl, a=fA, b_=fB:
                                 e.tensor_tensor(o, a, b_, OP.add))
            ev_qk = C.ev("act") if layer == 0 else C.ev("dve")
            if hp + 1 < NC:
                issue_jit(hp + 1, waits=[C.ev("pe")])
            if DBG and hp == 0:
                self.dbg_dump(qt)
                self.dbg_dump(kt_)
                self.dbg_dump(t["nrm"], 16)
                self.dbg_dump(t["csA"], 1)
                self.dbg_dump(xnTc(0))
                self.dbg_dump(wjit[:, 0:1024], 1024)
                self.dbg_dump(latT[:, 0:2048] if layer == 0 else qt)

            if SUBP < 1:
                continue
            nrm, nrm_bf, csA = t["nrm"], t["nrm_bf"], t["csA"]
            ev_nrd = None
            for src_i, (tens, col0) in enumerate(((qt, 0), (kt_, 4))):
                for w in range(NW):
                    sq = scr_b[:, 2048 + (w % 2) * 512: 2048 + (w % 2) * 512 + 512]
                    self.act(sq, tens[:, w * 512:(w + 1) * 512], AF.Square, waits=[ev_qk])
                    eva = C.ev("act")
                    self.mm(pU[0:2, :], t["o64x2_bf"][:, :], sq, start=True, stop=True,
                            waits=[eva, ev_nrd], inc="pe")
                    evp = C.ev("pe")
                    self.dve(lambda e, o=nrm[0:2, col0 + w:col0 + w + 1], i=pU[0:2, :]:
                             e.reduce_max(o, i, axis=AX.X), waits=[evp])
                    ev_nrd = C.ev("dve")
            self.dve(lambda e: e.reduce_max(nrm[0:2, 8:9], nrm[0:2, 0:4], axis=AX.X))
            self.dve(lambda e: e.reduce_max(nrm[0:2, 9:10], nrm[0:2, 4:8], axis=AX.X))
            self.dve(lambda e: e.tensor_tensor(nrm[0:2, 10:11], nrm[0:2, 8:9],
                                               nrm[0:2, 9:10], OP.mult))
            self.dve(lambda e: e.tensor_copy(nrm_bf[0:2, 0:1], nrm[0:2, 10:11]))
            ev = C.ev("dve")
            self.mm(pU[0:1, 0:2], nrm_bf[0:2, 0:1], t["I2_bf"][:, :], start=True, stop=True,
                    waits=[ev], inc="pe")
            ev = C.ev("pe")
            self.act(nrm[0:1, 11:13], pU[0:1, 0:2], AF.Ln, waits=[ev])
            ev = C.ev("act")
            self.dve(lambda e: e.reduce_max(nrm[0:1, 13:14], nrm[0:1, 11:13], axis=AX.X),
                     waits=[ev])
            ev = C.ev("dve")
            self.act(nrm[0:1, 14:15], nrm[0:1, 13:14], AF.Exp, scale=0.5, waits=[ev])
            ev = C.ev("act")
            self.dve(lambda e: e.tensor_copy(nrm_bf[0:1, 1:2], nrm[0:1, 14:15]), waits=[ev])
            ev = C.ev("dve")
            self.mm(pU[:, 0:1], t["ones_bf"][0:1, 0:128], nrm_bf[0:1, 1:2], start=True,
                    stop=True, waits=[ev], inc="pe")
            ev = C.ev("pe")
            self.act(csA[:, :], pU[:, 0:1], AF.Copy, scale=-SCALE, waits=[ev])
            ev_c = C.ev("act")
            self.barrier()
            if SUBP < 2:
                continue

            for qw in range(NW):
                kt_hi = NT if layer == 0 else 4 * qw + 4
                np_pairs = kt_hi // 2
                sc_bank = {0: t["pT"], 1: t["pU2"]}
                ev_exp = {}
                ev_pv = {}
                for pi in range(np_pairs):
                    for j in range(2):
                        ev_s = None
                        for ti in range(2):
                            ktile = 2 * pi + ti
                            sbank = sc_bank[j][:, ti * 512:(ti + 1) * 512]
                            self.mm(sbank, kt_[64 * j:64 * j + 64, ktile * 128:(ktile + 1) * 128],
                                    qt[64 * j:64 * j + 64, qw * 512:(qw + 1) * 512],
                                    start=True, stop=True,
                                    waits=[ev_exp.get((pi - 1, j))],
                                    inc="pe")
                            ev_s = C.ev("pe")
                            if layer == 1 and ktile >= 4 * qw:
                                jd = ktile - 4 * qw
                                o0 = 384 - 128 * jd
                                self.dve(lambda e, p=sbank, m_=msk[:, o0:o0 + 512]:
                                         e.tensor_tensor(p, p, m_, OP.add), waits=[ev_s])
                                ev_s = C.ev("dve")
                        slot = scr_b[:, ((pi % 2) * 2 + j) * 1024:((pi % 2) * 2 + j) * 1024 + 1024]
                        self.act(slot, sc_bank[j][:, :], AF.Exp, bias=csA[:, 0:1], scale=SCALE,
                                 waits=[ev_s, ev_pv.get((pi - 2, j))])
                        ev_exp[(pi, j)] = C.ev("act")
                        if SUBP < 3:
                            ev_pv[(pi, j)] = None
                            continue
                        h_glob = 2 * hp + j
                        for ti in range(2):
                            ktile = 2 * pi + ti
                            lhs = Vg[:, ktile * TW + 65 * h_glob:ktile * TW + 65 * h_glob + 65]
                            self.mm(pS[0:65, j * 512:(j + 1) * 512], lhs,
                                    slot[:, ti * 512:(ti + 1) * 512],
                                    start=(ktile == 0), stop=(ktile == kt_hi - 1),
                                    waits=[ev_exp[(pi, j)]], inc="pe" if ti == 1 else None)
                        ev_pv[(pi, j)] = C.ev("pe")
                for j in range(2 if SUBP >= 4 else 0):
                    Lrow = pS[64:65, j * 512:(j + 1) * 512]
                    lnl = scr_f[:, 1024:1536]
                    self.act(lnl[0:1, 0:512], Lrow, AF.Ln, waits=[ev_pv[(kt_hi // 2 - 1, j)]])
                    self.act(t["recL"][:, :], lnl[0:1, 0:512], AF.Exp, scale=-1.0)
                    ev = C.ev("act")
                    self.mm(t["pV2"][0:64, 0:512], t["ones_bf"][0:1, 0:64], t["recL"][:, :],
                            start=True, stop=True,
                            waits=[ev, getattr(self, "_ev_lbc", None)], inc="pe")
                    ev = C.ev("pe")
                    self.dve(lambda e: e.tensor_copy(t["Lbc"][:, :], t["pV2"][0:64, 0:512]),
                             waits=[ev])
                    self._ev_lbc = C.ev("dve")
                    self.dve(lambda e, o=aoTc(hp)[64 * j:64 * j + 64, qw * 512:(qw + 1) * 512],
                             p=pS[0:64, j * 512:(j + 1) * 512]:
                             e.tensor_tensor(o, p, t["Lbc"][:, :], OP.mult),
                             waits=[self._ev_lbc])
                self.barrier()
            if DBG and hp == 0:
                self.dbg_dump(aoTc(0))
        if PHASES < 20:
            return
        if layer == 0:
            self.dma("pool", t["rows_bf"][:, 0:1024], d["rows0"][:, 1024:2048])
            self.barrier()
        bo_row = t["rows_bf"] if layer == 0 else None
        wo_cons = {}
        for ttg in range(4):
            evl = {}
            for c in range(NC):
                sl = c % 2
                evl[sl] = self.dma("pool", wstr[:, sl * 1024:sl * 1024 + 1024],
                                   Wo_src[c * 128:(c + 1) * 128, :],
                                   waits=[wo_cons.get(sl)])
                for i in range(4):
                    tt = ttg * 4 + i
                    for j in range(2):
                        last = (c == NC - 1) and bo_row is None
                        self.mm(bank4(i, j), aoTc(c)[:, tt * 128:(tt + 1) * 128],
                                wstr[:, sl * 1024 + j * 512:sl * 1024 + j * 512 + 512],
                                start=(c == 0), stop=last, waits=[evl[sl]],
                                inc="pe" if (i == 3 and j == 1) else None)
                wo_cons[sl] = C.ev("pe")
            if bo_row is not None:
                for i in range(4):
                    for j in range(2):
                        self.mm(bank4(i, j), t["ones_bf"][0:1, 0:128],
                                bo_row[0:1, j * 512:(j + 1) * 512], start=False, stop=True,
                                inc="pe" if (i == 3 and j == 1) else None)
            ev = C.ev("pe")
            for i in range(4):
                tt = ttg * 4 + i
                for j in range(2):
                    xsl = x[:, tt * D + j * 512:tt * D + j * 512 + 512]
                    self.dve(lambda e, o=xsl, p=bank4(i, j):
                             e.tensor_tensor(o, o, p, OP.add), waits=[ev])
            self.barrier()

    def emit_l0f(self, b):
        moe(self, b)

    def emit_l1f(self, b):
        ffn1(self, b)

    def quant_weights_once(self):
        quant_weights_once(self)


def moe(K, b):
    C, t, d = K.C, K.t, K.dram
    x = t["x_sb"]
    xnT = K.ar("ar2", BF16, 0, NC * S)
    xnTc = lambda c: xnT[:, c * S:(c + 1) * S]
    gbc = K.ar("ar2", F32, 32, D)
    xj = K.ar("ar2", F32, 36, D)
    ctr = K.ar("ar2", F32, 40, D)
    w2sb = K.ar("ar1", BF16, 0, 32 * D)
    w1s = K.ar("ar3", BF16, 0, 2048)
    htile = K.ar("ar3", BF16, 4, 8192)
    pS, pT = t["pS"], t["pT"]
    pU, pV_, pW, pX = t["pU"], t["pV"], t["pW"], t["pX"]
    m, r = t["stat_m"], t["stat_r"]

    ev_stat = K.ln_stats()
    K.xnT_build(ev_stat)
    K.barrier()

    g8, g8b, gates = t["g8"], t["g8b"], t["gates"]
    nrm = t["nrm"]
    ev_lgrd = None
    ev_gtrd = None
    ev_btrd = None
    for tt in range(NT):
        for c in range(NC):
            K.mm(pU[:, 0:8], xnTc(c)[:, tt * 128:(tt + 1) * 128],
                 t["rw_bf"][:, c * 8:(c + 1) * 8], start=(c == 0), stop=(c == NC - 1),
                 waits=[ev_lgrd],
                 inc="pe" if c == NC - 1 else None)
        ev = C.ev("pe")
        lg = pU[:, 0:8]
        K.dve(lambda e: e.reduce_max(nrm[:, 0:1], lg, axis=AX.X), waits=[ev])
        K.dve(lambda e: e.tensor_scalar(g8[:, :], lg, nrm[:, 0:1], None, OP.is_equal))
        msked = gates[:, tt * 8:(tt + 1) * 8]
        K.dve(lambda e, o=msked: e.scalar_tensor_tensor(o, g8[:, :],
                                                        -1e30, lg, OP.mult, OP.add))
        K.dve(lambda e, i=msked: e.reduce_max(nrm[:, 1:2], i, axis=AX.X))
        K.dve(lambda e, o=msked: e.tensor_scalar(o, o, nrm[:, 1:2], None, OP.is_equal))
        K.dve(lambda e: e.tensor_tensor(nrm[:, 2:3], nrm[:, 1:2], nrm[:, 0:1], OP.subtract))
        ev = C.ev("dve")
        K.act(nrm[:, 3:4], nrm[:, 2:3], AF.Exp, waits=[ev])
        ev = C.ev("act")
        K.dve(lambda e: e.tensor_scalar(nrm[:, 4:5], nrm[:, 3:4], 1.0, None, OP.add),
              waits=[ev])
        K.dve(lambda e: e.reciprocal(nrm[:, 5:6], nrm[:, 4:5]))
        K.dve(lambda e: e.tensor_scalar(nrm[:, 6:7], nrm[:, 5:6], -1.0, 1.0, OP.mult, OP.add))
        K.dve(lambda e: e.tensor_scalar(g8[:, :], g8[:, :], nrm[:, 5:6], None, OP.mult))
        K.dve(lambda e, o=msked: e.scalar_tensor_tensor(o, o, nrm[:, 6:7], g8[:, :],
                                                         OP.mult, OP.add))
        ev_lgrd = C.ev("dve")
        if DBG:
            evx = C.ev("dve")
            K.dma("sp", d["dbg"][tt][:, 0:8], gates[:, tt * 8:(tt + 1) * 8], waits=[evx],
                  allow_slow_non_contiguous=True)
        K.dve(lambda e, i=msked: e.tensor_copy(g8b[:, :], i))
        ev = C.ev("dve")
        K.trans(t["pX"][:, :].bitcast(BF16)[0:8, 0:128], g8b[:, :], waits=[ev, ev_gtrd],
                inc="pe")
        ev = C.ev("pe")
        K.dve(lambda e: e.tensor_copy(t["gatesT_bf"][:, :],
                                      t["pX"][:, :].bitcast(BF16)[0:8, 0:128]), waits=[ev])
        ev_gtrd = C.ev("dve")
        ev = ev_gtrd
        for j in range(2):
            K.mm(pW[:, :], t["gatesT_bf"][:, :], t["beta_sb"][:, j * 512:(j + 1) * 512],
                 start=True, stop=True, waits=[ev, ev_btrd], inc="pe")
            evp = C.ev("pe")
            K.dve(lambda e, o=ctr[:, j * 512:(j + 1) * 512]: e.tensor_copy(o, pW[:, :]),
                  waits=[evp])
            ev_btrd = C.ev("dve")
        evd = C.ev("dve")
        K.dma("sp", d["yscr"][tt], ctr, waits=[evd])
    K.barrier()
    if MOEPH < 1:
        return

    stage = K.ar("ar2", BF16, 44, 2048)
    ev_haccum = {}
    ev_sq = {}
    ev_b1rd = {}
    ev_rowrd = {}
    ev_gbcrd = {}
    ev_w2rd = {}
    ev_stg_last = {}
    ev_w2 = {}
    ev_cld = {}
    ev_slab = {}
    ev_stg = {}
    ev_p1rd = {}
    ev_ht = {}
    ev_grp = {}

    def consts(e_i):
        sl = e_i % 2
        K.dma("sp", t["b1t"][:, sl * 32:sl * 32 + 32], d["b1T"][e_i],
              waits=[ev_b1rd.get(sl)])

    def rows_load(e_i):
        K.dma("pool", t["rows_bf"][:, :], t["b2sb"][e_i:e_i + 1, :],
              waits=[ev_rowrd.get("v")])
        K.dma("pool", gbc, d["gam"][e_i:e_i + 1, :].broadcast_to((128, D)),
              waits=[ev_gbcrd.get("v")])
        ev_cld[e_i % 2] = [C.ev("gd"), C.ev("sd")]

    def pass1(e_i):
        sl_e = e_i % 2
        hbuf = d["hscr"][sl_e]
        for fc in range(32):
            sl = fc % 2
            ev_slab[sl] = K.dma("pool", w1s[:, sl * 1024:sl * 1024 + 1024]
                                .rearrange("p (c f) -> p c f", f=128),
                                d["w1"][e_i][:, fc * 128:(fc + 1) * 128]
                                .rearrange("(c p) f -> p c f", p=128),
                                waits=[ev_slab.get((sl, "cons"))])
            for w in range(NW):
                pb = [pU, pV_, pW, pX][w]
                for c in range(NC):
                    K.mm(pb[:, :], w1s[:, sl * 1024 + c * 128:sl * 1024 + c * 128 + 128],
                         xnTc(c)[:, w * 512:(w + 1) * 512], start=(c == 0),
                         stop=(c == NC - 1), waits=[ev_slab[sl], ev_p1rd.get(w)],
                         inc="pe" if c == NC - 1 else None)
                evp = C.ev("pe")
                if w == NW - 1:
                    ev_slab[(sl, "cons")] = evp
                ssl = stage[:, ((w % 2) * 2 + (fc % 2)) * 512:((w % 2) * 2 + (fc % 2)) * 512 + 512]
                K.act(ssl, pb[:, :], AF.Gelu, bias=t["b1t"][:, sl_e * 32 + fc:sl_e * 32 + fc + 1],
                      waits=[evp, ev_stg.get((w % 2, fc % 2))])
                eva = C.ev("act")
                ev_p1rd[w] = eva
                ev_stg[(w % 2, fc % 2)] = K.dma("sp", hbuf[fc][:, w * 512:(w + 1) * 512],
                                                ssl, waits=[eva])
        ev_b1rd[sl_e] = C.ev("act")
        ev_stg_last[sl_e] = C.ev("sd")

    def pass2(e_i):
        sl_e = e_i % 2
        hbuf = d["hscr"][sl_e]
        for tt in range(NT):
            sl = tt % 2
            ev_ht[sl] = K.dma("sp", htile[:, sl * 4096:sl * 4096 + 4096]
                              .rearrange("p (c q) -> p c q", q=128),
                              hbuf[:, :, tt * 128:(tt + 1) * 128]
                              .rearrange("c p q -> p c q"),
                              waits=[ev_ht.get((sl, "cons")), ev_stg_last[sl_e]])
            grp = pS if sl == 0 else pT
            for j in range(2):
                for fc in range(32):
                    K.mm(grp[:, j * 512:(j + 1) * 512],
                         htile[:, sl * 4096 + fc * 128:sl * 4096 + fc * 128 + 128],
                         w2sb[:, fc * D + j * 512:fc * D + j * 512 + 512],
                         start=(fc == 0), stop=False,
                         waits=[ev_ht[sl], ev_w2[sl_e], ev_grp.get((sl, j))])
                K.mm(grp[:, j * 512:(j + 1) * 512], t["ones_bf"][0:1, 0:128],
                     t["rows_bf"][0:1, j * 512:(j + 1) * 512],
                     start=False, stop=True, waits=ev_cld[sl_e], inc="pe")
            ev_ht[(sl, "cons")] = C.ev("pe")
            evp = C.ev("pe")
            K.dve(lambda e, o=xj, xs=x[:, tt * D:(tt + 1) * D], mm=m[:, tt:tt + 1],
                  rr=r[:, tt:tt + 1]: e.tensor_scalar(o, xs, mm, rr, OP.subtract, OP.mult),
                  waits=[ev_sq.get("v")])
            K.dve(lambda e, g=grp[:, :]: e.tensor_tensor(g, g, xj, OP.add), waits=[evp])
            K.dve(lambda e, g=grp[:, :]: e.reduce_sum(nrm[:, 0:1], g, axis=AX.X))
            ev = C.ev("dve")
            K.act(xj, grp[:, :], AF.Square, accum=nrm[:, 1:2], waits=[ev])
            eva = C.ev("act")
            ev_sq["v"] = eva
            K.dve(lambda e: e.tensor_scalar(nrm[:, 0:1], nrm[:, 0:1], 1.0 / D, None, OP.mult))
            K.dve(lambda e: e.tensor_tensor(nrm[:, 2:3], nrm[:, 0:1], nrm[:, 0:1], OP.mult))
            K.dve(lambda e: e.scalar_tensor_tensor(nrm[:, 1:2], nrm[:, 1:2], 1.0 / D,
                                                   nrm[:, 2:3], OP.mult, OP.subtract),
                  waits=[eva])
            ev = C.ev("dve")
            K.act(nrm[:, 1:2], nrm[:, 1:2], AF.Ln, bias=t["epsb"][:, 1:2], waits=[ev])
            K.act(nrm[:, 1:2], nrm[:, 1:2], AF.Exp, scale=-0.5)
            eva = C.ev("act")
            K.dve(lambda e, g=grp[:, :]: e.tensor_scalar(ctr, g, nrm[:, 0:1], nrm[:, 1:2],
                                                         OP.subtract, OP.mult),
                  waits=[eva, ev_haccum.get(tt - 1 if tt > 0 else NT - 1)])
            ev_grp[(sl, 0)] = C.ev("dve")
            ev_grp[(sl, 1)] = ev_grp[(sl, 0)]
            gate_col = gates[:, tt * 8 + e_i:tt * 8 + e_i + 1]
            K.dve(lambda e, gc=gate_col: e.scalar_tensor_tensor(ctr, ctr, gc, gbc,
                                                                OP.mult, OP.mult))
            ev = C.ev("dve")
            ev_haccum[tt] = K.dma("pool", d["yscr"][tt], ctr,
                                  waits=[ev, ev_haccum.get(tt)], accum_op=OP.add)
        ev_w2rd["v"] = ev_rowrd["v"] = C.ev("pe")
        ev_gbcrd["v"] = C.ev("dve")

    for e_i in range(NE):
        consts(e_i)
        if e_i == 0:
            rows_load(0)
            ev_w2[0] = K.dma("pool", w2sb.rearrange("p (c dd) -> p c dd", dd=D),
                             d["w2"][0].rearrange("(c p) dd -> p c dd", p=128))
        pass1(e_i)
        if e_i >= 1:
            pass2(e_i - 1)
            ev_w2[e_i % 2] = K.dma("pool", w2sb.rearrange("p (c dd) -> p c dd", dd=D),
                                   d["w2"][e_i].rearrange("(c p) dd -> p c dd", p=128),
                                   waits=[ev_w2rd["v"]])
            rows_load(e_i)
    pass2(NE - 1)
    K.barrier()
    slots = [ctr, xj]
    ev_ld = {}
    ev_add = {}
    for tt in range(NT):
        sl = tt % 2
        ev_ld[sl] = K.dma("sp" if sl == 0 else "pool", slots[sl], d["yscr"][tt],
                          waits=[ev_add.get(sl)])
        K.dve(lambda e, o=x[:, tt * D:(tt + 1) * D], c_=slots[sl]:
              e.tensor_tensor(o, o, c_, OP.add), waits=[ev_ld[sl]])
        ev_add[sl] = C.ev("dve")


def mk_qctx(K, arena, off_kib, W):
    slot = W / 256.0
    c = {}
    for i, n in enumerate(("A", "Bu", "Cu", "Du", "Eu", "Fu", "Gu")):
        c[n] = K.ar(arena, F32, off_kib + i * slot, W)
    sa = K.ar(arena, F32, off_kib + 7 * slot, W // 8)
    c["sc"] = sa[:, 0:W // 16]
    c["rec"] = sa[:, W // 16:W // 8]
    c["size_kib"] = 7 * slot + slot / 8.0
    return c


def t3v(ap, nb):
    return bass.AP(ap.tensor, ap.offset, [list(ap.ap[0]), [16, nb], [1, 16]])


def qt_p1(K, ctx, tile_f32, W, waits=()):
    C = K.C
    Bu = ctx["Bu"][:, 0:W]
    nb = W // 16
    sc, rec = ctx["sc"][:, 0:nb], ctx["rec"][:, 0:nb]
    K.dve(lambda e: e.reduce_max(sc, t3v(tile_f32, nb), axis=AX.X,
                                 apply_absolute_value=True), waits=list(waits))
    K.dve(lambda e: e.tensor_scalar(sc, sc, 1.0 / 6.0, 1e-12, OP.mult, OP.max))
    K.dve(lambda e: e.reciprocal(rec, sc))
    K.dve(lambda e: e.tensor_tensor(t3v(Bu, nb), t3v(tile_f32, nb),
                                    blk_bcast(rec, nb, 16), OP.mult))
    return C.ev("dve")


def qt_p2a(K, ctx, W, ev_p1):
    Bu, Cu = ctx["Bu"][:, 0:W], ctx["Cu"][:, 0:W]
    K.act(Cu, Bu, AF.Abs, waits=[ev_p1])
    return K.C.ev("act")


def qt_stair(K, ctx, W, ev_abs, ev_p4_prev):
    C = K.C
    A, Cu, Du = ctx["A"][:, 0:W], ctx["Cu"][:, 0:W], ctx["Du"][:, 0:W]
    K.pool(lambda e: e.tensor_scalar(Du, Cu, 1.9999990, MAGIC, OP.mult, OP.add),
           waits=[ev_abs, ev_p4_prev])
    K.pool(lambda e: e.tensor_scalar(Du, Du, -MAGIC, 0.5, OP.add, OP.mult))
    K.pool(lambda e: e.tensor_scalar(A, Cu, 0.9999995, MAGIC, OP.mult, OP.add))
    K.pool(lambda e: e.tensor_scalar(A, A, -MAGIC, 1.0, OP.add, OP.mult))
    return C.ev("pool")


def qt_p2b(K, ctx, W, qcb, ev_stair):
    C = K.C
    A, Du, Eu, Fu, Gu = [ctx[n][:, 0:W] for n in ("A", "Du", "Eu", "Fu", "Gu")]
    K.act(Eu, Du, AF.Relu, bias=qcb[:, 0:1], scale=-1.0, waits=[ev_stair])
    K.act(Fu, Eu, AF.Identity, bias=qcb[:, 0:1], scale=-1.0)
    K.act(Gu, A, AF.Relu, bias=qcb[:, 1:2])
    K.act(Eu, A, AF.Relu, bias=qcb[:, 2:3])
    return C.ev("act")


def qt_p3b(K, ctx, W, ev_p2b):
    C = K.C
    A, Eu, Fu, Gu = [ctx[n][:, 0:W] for n in ("A", "Eu", "Fu", "Gu")]
    K.pool(lambda e: e.tensor_tensor(A, Gu, Eu, OP.subtract), waits=[ev_p2b])
    K.pool(lambda e: e.tensor_tensor(A, A, Fu, OP.add))
    return C.ev("pool")


def qt_p4(K, ctx, out_bf, W, ev_p2b, ev_p3b, out_waits=()):
    C = K.C
    A, Bu, Cu, Du = [ctx[n][:, 0:W] for n in ("A", "Bu", "Cu", "Du")]
    nb = W // 16
    sc = ctx["sc"][:, 0:nb]
    K.dve(lambda e: e.tensor_single_scalar(Bu.bitcast(I32), Bu.bitcast(I32), -2147483648,
                                           OP.bitwise_and), waits=[ev_p2b])
    K.dve(lambda e: e.tensor_single_scalar(Du, Cu, 5.0, OP.is_gt))
    K.dve(lambda e: e.scalar_tensor_tensor(Du, Du, 2.0, A, OP.mult, OP.add),
          waits=[ev_p3b])
    K.dve(lambda e: e.tensor_tensor(t3v(Bu.bitcast(I32), nb),
                                    blk_bcast(sc.bitcast(I32), nb, 16),
                                    t3v(Bu.bitcast(I32), nb), OP.bitwise_or))
    K.dve(lambda e: e.tensor_tensor(out_bf, Du, Bu, OP.mult), waits=list(out_waits))
    return C.ev("dve")


def quant_weights_once(K):
    C, t, d = K.C, K.t, K.dram
    bigs = [K.ar("ar2", F32, 11 * i, IFF) for i in range(2)]
    qc = [mk_qctx(K, "ar1", 0, 512), mk_qctx(K, "ar1", 14.5, 512)]
    qb = K.ar("ar1", BF16, 29, IFF)
    stg = K.ar("ar1", BF16, 35, 2816)
    qcb = K.t["qcb"][:, :]
    pE = t["pW"][:, :].bitcast(BF16)
    pF = t["pX"][:, :].bitcast(BF16)
    units = []
    for (srcn, dstn, nrc, wid, sub) in (("wg1", "wgT", 22, D, 512), ("wu1", "wuT", 22, D, 512),
                                        ("wd1", "wdT", 8, IFF, 256)):
        for rc in range(nrc):
            units.append((srcn, dstn, rc, wid, sub))
    ev_big = {}
    ev_pfx = {}
    ev_evac = {0: None, 1: None}
    ev_trans_prev = None
    ev_stg_dma = None
    ev_p2b = {}
    ev_st = {}
    ev_p3b = {}
    ev_p4 = {}
    tinfo = {}
    gt0 = [0]

    def u_p4(gt):
        cx = gt % 2
        qsl, sub_ = tinfo[gt]
        ev_p4[gt] = qt_p4(K, qc[cx], qsl, sub_, ev_p2b[gt], ev_p3b[gt],
                          out_waits=[ev_trans_prev])

    def u_p23(gt):
        _, sub_ = tinfo[gt]
        ev_p2b[gt] = qt_p2b(K, qc[gt % 2], sub_, qcb, ev_st[gt])
        ev_p3b[gt] = qt_p3b(K, qc[gt % 2], sub_, ev_p2b[gt])

    for g, (srcn, dstn, rc, wid, sub) in enumerate(units):
        if g == 0:
            ev_big[0] = K.dma("sp", bigs[0][:, 0:wid], d[srcn][rc * 128:(rc + 1) * 128, :])
        if g + 1 < len(units):
            ns, nd, nrc_, nwid, _ = units[g + 1]
            ev_big[(g + 1) % 2] = K.dma("sp", bigs[(g + 1) % 2][:, 0:nwid],
                                        d[ns][nrc_ * 128:(nrc_ + 1) * 128, :],
                                        waits=[ev_pfx.get((g + 1) % 2)])
        big = bigs[g % 2]
        nst = wid // sub
        base = gt0[0]
        for sti in range(nst):
            gt = base + sti
            cx = gt % 2
            tinfo[gt] = (qb[:, sti * sub:(sti + 1) * sub], sub)
            if gt >= base + 2:
                u_p4(gt - 2)
            ev_p1 = qt_p1(K, qc[cx], big[:, sti * sub:(sti + 1) * sub], sub,
                          waits=[ev_big[g % 2], ev_p2b.get(gt - 2), ev_p4.get(gt - 2)])
            ev_p2a = qt_p2a(K, qc[cx], sub, ev_p1)
            ev_st[gt] = qt_stair(K, qc[cx], sub, ev_p2a, ev_p4.get(gt - 2))
            if gt >= base + 1:
                u_p23(gt - 1)
        ev_pfx[g % 2] = C.ev("dve")
        last = base + nst - 1
        if nst >= 2:
            u_p4(last - 1)
        u_p23(last)
        u_p4(last)
        gt0[0] = last + 1
        ev_sfx = C.ev("dve")
        nch = wid // 128
        for c in range(nch):
            pb = pE if c % 2 == 0 else pF
            K.trans(pb[:, 0:128], qb[:, c * 128:(c + 1) * 128],
                    waits=[ev_sfx, ev_evac[c % 2]], inc="pe")
            ev = C.ev("pe")
            K.act(stg[:, c * 128:(c + 1) * 128], pb[:, 0:128], AF.Copy,
                  waits=[ev, ev_stg_dma])
            ev_evac[c % 2] = C.ev("act")
        ev_trans_prev = C.ev("pe")
        ev = C.ev("act")
        ev_stg_dma = K.dma("sp", d[dstn].rearrange("(c p) n -> p c n", p=128)[:, :, rc * 128:(rc + 1) * 128],
                           stg[:, 0:nch * 128].rearrange("p (c n) -> p c n", n=128), waits=[ev])


def ffn1(K, b):
    C, t, d = K.C, K.t, K.dram
    x = t["x_sb"]
    hqT = K.ar("ar1", BF16, 0, 22 * 1024)
    hqTc = lambda ic: hqT[:, ic * 1024:(ic + 1) * 1024]
    WG = [K.ar("ar1", BF16, 44, 4096), K.ar("ar3", BF16, 0, 4096)]
    WU = [K.ar("ar1", BF16, 52, 4096), K.ar("ar3", BF16, 8, 4096)]
    xqT = K.ar("ar2", BF16, 0, NC * 1024)
    xqTc = lambda c: xqT[:, c * 1024:(c + 1) * 1024]
    qc = [mk_qctx(K, "ar2", 16, 512), mk_qctx(K, "ar2", 31.5, 512)]
    qtl = [K.ar("ar2", BF16, 30.25, 512), K.ar("ar2", BF16, 45.75, 512)]
    wds = K.ar("ar3", BF16, 16, 2048)
    qcb = K.t["qcb"][:, :]
    pS, pT = t["pS"], t["pT"]
    pU, pV_, pW, pX = t["pU"], t["pV"], t["pW"], t["pX"]
    bank4 = lambda i, j: [pS[:, j * 512:(j + 1) * 512], pT[:, j * 512:(j + 1) * 512],
                          [pU, pV_][j][:, :], [pW, pX][j][:, :]][i]
    m, r = t["stat_m"], t["stat_r"]
    pE = pW[:, :].bitcast(BF16)
    pF = pX[:, :].bitcast(BF16)

    ev_stat = K.ln_stats()
    NSL = [512, 512, 512, 512, 512, 256]
    csum = [0]
    for w_ in NSL:
        csum.append(csum[-1] + w_)
    ev_wg = {}
    ev_wu = {}
    ev_slab_cons = {}

    def issue_slab(ns):
        par = ns % 2
        nn = NSL[ns]
        nc0 = csum[ns]
        ev_wg[par] = K.dma("sp", WG[par][:, 0:8 * nn].rearrange("p (c n) -> p c n", n=nn),
                           d["wgT"][:, nc0:nc0 + nn].rearrange("(c p) n -> p c n", p=128),
                           waits=[ev_slab_cons.get(par)])
        ev_wu[par] = K.dma("pool", WU[par][:, 0:8 * nn].rearrange("p (c n) -> p c n", n=nn),
                           d["wuT"][:, nc0:nc0 + nn].rearrange("(c p) n -> p c n", p=128),
                           waits=[ev_slab_cons.get(par)])

    for half in range(2):
        tt0 = half * 8
        issue_slab(0)
        issue_slab(1)
        ev_tre = {0: None, 1: None}
        ev_p1 = {}
        ev_p2a = {}
        ev_p2b = {}
        ev_st = {}
        ev_p3b = {}
        ev_p4 = {}
        ev_tr = {}

        def xq_trans(n):
            cx = n % 2
            i, sub = n // 2, n % 2
            for cc in range(4):
                c = sub * 4 + cc
                pb = pE if cc % 2 == 0 else pF
                K.trans(pb[:, 0:128], qtl[cx][:, cc * 128:(cc + 1) * 128],
                        waits=[ev_p4[n], ev_tre[cc % 2]], inc="pe")
                ev = C.ev("pe")
                K.act(xqTc(c)[:, i * 128:(i + 1) * 128], pb[:, 0:128], AF.Copy,
                      waits=[ev])
                ev_tre[cc % 2] = C.ev("act")
            ev_tr[cx] = C.ev("pe")

        def xq_p4(n):
            cx = n % 2
            ev_p4[n] = ev_p4[cx] = qt_p4(K, qc[cx], qtl[cx][:, 0:512], 512,
                                         ev_p2b[n], ev_p3b[n],
                                         out_waits=[ev_tr.get(cx)])

        for n in range(16):
            cx = n % 2
            tt, sub = tt0 + n // 2, n % 2
            if n >= 2:
                xq_p4(n - 2)
            IN = qc[cx]["A"][:, 0:512]
            K.dve(lambda e, o=IN, xs=x[:, tt * D + sub * 512:tt * D + sub * 512 + 512],
                  mm=m[:, tt:tt + 1], rr=r[:, tt:tt + 1]:
                  e.tensor_scalar(o, xs, mm, rr, OP.subtract, OP.mult),
                  waits=[ev_stat, ev_st.get(cx)])
            ev_p1[n] = qt_p1(K, qc[cx], IN, 512, waits=[ev_p2b.get(n - 2)])
            ev_p2a[n] = qt_p2a(K, qc[cx], 512, ev_p1[n])
            ev_st[cx] = ev_st[n] = qt_stair(K, qc[cx], 512, ev_p2a[n], ev_p4.get(cx))
            if n >= 1:
                ev_p2b[n - 1] = qt_p2b(K, qc[(n - 1) % 2], 512, qcb, ev_st[n - 1])
                ev_p3b[n - 1] = qt_p3b(K, qc[(n - 1) % 2], 512, ev_p2b[n - 1])
            if n >= 2:
                xq_trans(n - 2)
        xq_p4(14)
        xq_trans(14)
        ev_p2b[15] = qt_p2b(K, qc[1], 512, qcb, ev_st[15])
        ev_p3b[15] = qt_p3b(K, qc[1], 512, ev_p2b[15])
        xq_p4(15)
        xq_trans(15)
        K.barrier()
        nsw_of = lambda k: NSL[k // 8]
        ic0_of = lambda k: csum[k // 8] // 128
        ev_silu = {}
        ev_mult = {}
        ev_p2b = {}
        ev_st = {}
        ev_p3b = {}
        ev_p4 = {}
        ev_tr = {}
        ev_tre = {0: None, 1: None}

        def h_trans(k):
            cx = k % 2
            nsw = nsw_of(k)
            i = k % 8
            for cc in range(nsw // 128):
                ic = ic0_of(k) + cc
                pb = pE if cc % 2 == 0 else pF
                K.trans(pb[:, 0:128], qtl[cx][:, cc * 128:(cc + 1) * 128],
                        waits=[ev_p4[k], ev_tre[cc % 2]], inc="pe")
                ev = C.ev("pe")
                K.act(hqTc(ic)[:, i * 128:(i + 1) * 128], pb[:, 0:128], AF.Copy,
                      waits=[ev])
                ev_tre[cc % 2] = C.ev("act")
            ev_tr[cx] = C.ev("pe")

        def h_p4(k):
            cx = k % 2
            ev_p4[k] = ev_p4[(cx, "c")] = qt_p4(K, qc[cx], qtl[cx][:, 0:nsw_of(k)],
                                                nsw_of(k), ev_p2b[k], ev_p3b[k],
                                                out_waits=[ev_tr.get(cx)])

        def h_p23(k):
            ev_p2b[k] = qt_p2b(K, qc[k % 2], nsw_of(k), qcb, ev_st[k])
            ev_p3b[k] = qt_p3b(K, qc[k % 2], nsw_of(k), ev_p2b[k])

        for ns, nsw in enumerate(NSL):
            par = ns % 2
            for i in range(8):
                k = ns * 8 + i
                cx = k % 2
                gbank = pU[:, 0:nsw] if cx == 0 else pS[:, 0:nsw]
                ubank = pV_[:, 0:nsw] if cx == 0 else pS[:, 512:512 + nsw]
                for c in range(NC):
                    K.mm(gbank, xqTc(c)[:, i * 128:(i + 1) * 128],
                         WG[par][:, c * nsw:(c + 1) * nsw], start=(c == 0),
                         stop=(c == NC - 1),
                         waits=[ev_wg[par], ev_silu.get(k - 2)],
                         inc="pe" if c == NC - 1 else None)
                for c in range(NC):
                    K.mm(ubank, xqTc(c)[:, i * 128:(i + 1) * 128],
                         WU[par][:, c * nsw:(c + 1) * nsw], start=(c == 0),
                         stop=(c == NC - 1),
                         waits=[ev_wu[par], ev_mult.get(k - 2)],
                         inc="pe" if c == NC - 1 else None)
                evp = C.ev("pe")
                if i == 7:
                    ev_slab_cons[par] = evp
                if k >= 2:
                    h_p4(k - 2)
                IN = qc[cx]["A"][:, 0:nsw]
                K.act(IN, gbank, AF.Silu,
                      waits=[evp, ev_p4.get((cx, "c")), ev_mult.get(k - 2)])
                ev_silu[k] = C.ev("act")
                K.dve(lambda e, o=IN, u=ubank: e.tensor_tensor(o, o, u, OP.mult),
                      waits=[ev_silu[k]])
                ev_mult[k] = C.ev("dve")
                ev_p1 = qt_p1(K, qc[cx], IN, nsw, waits=[ev_p2b.get(k - 2)])
                ev_p2a = qt_p2a(K, qc[cx], nsw, ev_p1)
                ev_st[k] = qt_stair(K, qc[cx], nsw, ev_p2a, ev_p4.get((cx, "c")))
                if k >= 1:
                    h_p23(k - 1)
                if k >= 2:
                    h_trans(k - 2)
            if ns + 2 < len(NSL):
                issue_slab(ns + 2)
        h_p4(46)
        h_trans(46)
        h_p23(47)
        h_p4(47)
        h_trans(47)
        K.barrier()
        for ttg in range(2):
            evl = {}
            for ic in range(22):
                sl = ic % 2
                evl[sl] = K.dma("sp" if sl == 0 else "pool",
                                wds[:, sl * 1024:sl * 1024 + 1024],
                                d["wdT"][ic * 128:(ic + 1) * 128, :],
                                waits=[evl.get((sl, "cons"))])
                for i in range(4):
                    il = ttg * 4 + i
                    for j in range(2):
                        K.mm(bank4(i, j), hqTc(ic)[:, il * 128:(il + 1) * 128],
                             wds[:, sl * 1024 + j * 512:sl * 1024 + j * 512 + 512],
                             start=(ic == 0), stop=(ic == 21), waits=[evl[sl]],
                             inc="pe" if (i == 3 and j == 1) else None)
                evl[(sl, "cons")] = C.ev("pe")
            ev = C.ev("pe")
            for i in range(4):
                tt = tt0 + ttg * 4 + i
                for j in range(2):
                    xsl = x[:, tt * D + j * 512:tt * D + j * 512 + 512]
                    K.dve(lambda e, o=xsl, p=bank4(i, j): e.tensor_tensor(o, o, p, OP.add),
                          waits=[ev])
            K.barrier()


MK.emit_l0f = lambda self, b: moe(self, b)
MK.emit_l1f = lambda self, b: ffn1(self, b)
MK.quant_weights_once = lambda self: quant_weights_once(self)


def _host_consts():
    c = {}
    c["ident"] = np.eye(128, dtype=np.float32)
    c["ones"] = np.ones((128, 128), dtype=np.float32)
    o = np.zeros((128, 2), dtype=np.float32)
    o[:64, 0] = 1.0
    o[64:, 1] = 1.0
    c["o64x2"] = o
    R = np.zeros((128, 128), dtype=np.float32)
    for i in range(64):
        R[2 * i + 1, 2 * i] = -1.0
        R[2 * i, 2 * i + 1] = 1.0
    c["R2"] = R
    tl = np.zeros((128, 64), dtype=np.float32)
    tl[:, 0] = 1.0
    c["tail"] = tl
    c["I2"] = np.eye(2, dtype=np.float32)
    inv = 1.0 / (10000.0 ** (np.arange(0, DH, 2, dtype=np.float64) / DH))
    pos = np.arange(S, dtype=np.float64)
    fr = pos[:, None] * inv[None, :]
    emb = np.repeat(fr, 2, axis=1)
    cs = np.zeros((128, 2 * S), dtype=np.float32)
    cs[:64, 0:S] = np.cos(emb).T.astype(np.float32)
    cs[64:, 0:S] = cs[:64, 0:S]
    cs[:64, S:] = np.sin(emb).T.astype(np.float32)
    cs[64:, S:] = cs[:64, S:]
    c["cs2"] = cs
    kk = np.arange(128)[:, None]
    ccol = np.arange(896)[None, :]
    c["mask"] = np.where(kk > ccol - 384, np.float32(-1e30), np.float32(0.0)).astype(np.float32)
    return c


def _prep_shared(inputs, stages):
    f = lambda a: np.ascontiguousarray(np.asarray(a), dtype=np.float32)
    s = {}
    if "l0a" in stages:
        s["Wq0"] = f(inputs["L0_attn_Wq"])
        s["Wl0"] = f(inputs["L0_attn_Wl"])
        s["Wk0"] = f(inputs["L0_attn_Wk"])
        s["Wv0"] = f(inputs["L0_attn_Wv"])
        s["Wo0"] = f(inputs["L0_attn_Wo"])
        s["bqT0"] = f(inputs["L0_attn_bq"]).reshape(NC, 128).T.copy()
        s["bkT0"] = f(inputs["L0_attn_bk"]).reshape(NC, 128).T.copy()
        s["blT0"] = f(inputs["L0_attn_bl"]).reshape(2, 128).T.copy()
        s["rows0"] = np.concatenate([f(inputs["L0_attn_bv"]), f(inputs["L0_attn_bo"])])[None, :].copy()
    if "l0f" in stages:
        s["rw0"] = f(inputs["L0_ffn_router_w"])
        s["w1"] = f(inputs["L0_ffn_w1"])
        s["w2"] = f(inputs["L0_ffn_w2"])
        s["b1T"] = f(inputs["L0_ffn_b1"]).reshape(NE, 32, 128).transpose(0, 2, 1).copy()
        s["b2r"] = f(inputs["L0_ffn_b2"])
        s["gam"] = f(inputs["L0_ffn_gamma"])
        s["bet"] = f(inputs["L0_ffn_beta"])
    if "l1a" in stages:
        s["WqT1"] = f(inputs["L1_attn_Wq"]).T.copy()
        s["WkT1"] = f(inputs["L1_attn_Wk"]).T.copy()
        s["WvT1"] = f(inputs["L1_attn_Wv"]).T.copy()
        s["WoT1"] = f(inputs["L1_attn_Wo"]).T.copy()
    if "l1f" in stages:
        s["wg1"] = f(inputs["L1_ffn_w_gate"])
        s["wu1"] = f(inputs["L1_ffn_w_up"])
        s["wd1"] = f(inputs["L1_ffn_w_down"])
    c = _host_consts()
    s["ident"] = c["ident"]; s["ones"] = c["ones"]; s["o64x2"] = c["o64x2"]
    s["R2"] = c["R2"]; s["tail"] = c["tail"]; s["I2"] = c["I2"]
    if "l1a" in stages:
        s["cs2"] = c["cs2"]; s["mask"] = c["mask"]
    return s


_BUILT = {}


def _get_nc(b_loc, stages):
    key = (b_loc, tuple(stages))
    if key not in _BUILT:
        _BUILT[key] = MK(b_loc, tuple(stages)).build()
    return _BUILT[key]


def kernel(**inputs):
    n_cores = 8
    B = int(np.asarray(inputs["L0_attn_x"]).shape[0])
    bl = B // n_cores
    x = np.ascontiguousarray(np.asarray(inputs["L0_attn_x"], dtype=np.float32))
    for stages in GROUPS:
        nc = _get_nc(bl, stages)
        shared = _prep_shared(inputs, stages)
        in_maps = []
        for c in range(n_cores):
            mdl = dict(shared)
            mdl["x"] = np.ascontiguousarray(x[c * bl:(c + 1) * bl])
            in_maps.append(mdl)
        res = run_bass_kernel_spmd(nc, in_maps, core_ids=list(range(n_cores)))
        x = np.concatenate([r["y"] for r in res.results], axis=0)
    return x
